# version 20
# baseline (speedup 1.0000x reference)
import numpy as np
import ml_dtypes
import concourse.bass as bass
import concourse.mybir as mybir
from concourse.bass import IndirectOffsetOnAxis
from concourse.bass_utils import run_bass_kernel_spmd

F32 = mybir.dt.float32
BF16 = mybir.dt.bfloat16
I32 = mybir.dt.int32
AF = mybir.ActivationFunctionType
ALU = mybir.AluOpType
AXX = mybir.AxisListType.X

NCORE = 8
D = 1024
R = 256
E = 8
F = 4096
EPS = 1e-5
NITER = 45
NKD = D // 128
NKF = F // 128
ENGS = ("pe", "dve", "act", "pool", "sync")


class Ctx:

    def __init__(self, nc, es):
        self.nc = nc
        self.es = es
        self.ops = {e: [] for e in ENGS}
        self.sem = {}
        self.cnt = {}

    def new_sem(self, name):
        self.sem[name] = self.es.enter_context(self.nc.semaphore(name))
        self.cnt[name] = 0

    def on(self, eng, fn, inc=None, n=1):
        if inc is not None:
            self.cnt[inc] += n
            sem = self.sem[inc]

            def run(e, fn=fn, sem=sem, n=n):
                fn(e).then_inc(sem, n)
            self.ops[eng].append(run)
            return self.cnt[inc]
        self.ops[eng].append(lambda e, fn=fn: fn(e))
        return None

    def dma(self, eng, sem_name, fn):
        return self.on(eng, fn, inc=sem_name, n=16)

    def wait(self, eng, sem_name, val):
        if val is None or val <= 0:
            return
        sem = self.sem[sem_name]
        self.ops[eng].append(lambda e, sem=sem, val=val: e.wait_ge(sem, val))

    def dve(self, fn, waits=()):
        for s, v in waits:
            self.wait("dve", s, v)
        self.wait("dve", "dve", self.cnt["dve"])
        return self.on("dve", fn, inc="dve")

    def act(self, fn, waits=()):
        for s, v in waits:
            self.wait("act", s, v)
        self.wait("act", "act", self.cnt["act"])
        return self.on("act", fn, inc="act")

    def xbar(self, extra=()):
        tots = [(s, v) for s, v in self.cnt.items() if v > 0]
        tots += list(extra)
        for eng in ENGS:
            for s, v in tots:
                if eng == {"pe": "pe", "dve": "dve", "act": "act"}.get(s, None):
                    continue
                self.wait(eng, s, v)

    def emit(self):
        blk = self.es.enter_context(self.nc.Block())
        blk.tensor(lambda e: [f(e) for f in self.ops["pe"]])
        blk.vector(lambda e: [f(e) for f in self.ops["dve"]])
        blk.scalar(lambda e: [f(e) for f in self.ops["act"]])
        blk.gpsimd(lambda e: [f(e) for f in self.ops["pool"]])
        blk.sync(lambda e: [f(e) for f in self.ops["sync"]])


def build_nc(TCORE=2048, phases="all", use_mirror=False):
    from contextlib import ExitStack
    NT = TCORE // 128
    TGLOB = 4 * TCORE
    KSEL = TGLOB // 4
    SEGW = TGLOB // 8
    CAP = ((KSEL // 4) * 5 // 4 + 127) // 128 * 128
    CB = CAP // 128
    PW = 1040
    FC1 = 512
    NFC1 = F // FC1

    ACH = min(512, TCORE)
    NCH_A = TCORE // ACH
    CGRP = []
    b0 = 0
    while b0 < CB:
        nb = min(4, CB - b0)
        CGRP.append((b0, nb))
        b0 += nb

    nc = bass.Bass("TRN2", target_bir_lowering=False, debug=False)
    es = ExitStack()
    c = Ctx(nc, es)

    dram = {}

    def din(name, shape, dt=F32):
        dram[name] = nc.dram_tensor(name, list(shape), dt, kind="ExternalInput")

    din("xin", [TCORE, D])
    for nm in ["wq", "wk"]:
        din(nm, [D, D])
    for nm in ["vb", "vt", "wb", "wt", "xt_w", "ub", "ut"]:
        din(nm, [D, R])
    din("woT", [D, D])
    din("bias_b", [1, D]); din("bias_t", [1, D]); din("alpha", [1, 1])
    din("gate_wT", [D, E])
    din("w1", [E, NFC1, 128, NKD, FC1]); din("w2", [E, NKD, 128, NKF, 128])
    din("ident_f32", [128, 128])
    din("blockdiag8", [128, 128])
    din("ones_1x128_f32", [1, 128])
    din("ones_1x128_bf16", [1, 128], BF16)
    din("iota_em", [E, TCORE])
    din("tile_em", [E, TCORE])
    din("thitlo", [128, NT, 2], BF16)
    din("invc_row", [1, TCORE])
    din("selmat", [128, E])
    din("wvec", [128, E])

    y_out = nc.dram_tensor("y_out", [TCORE, D], F32, kind="ExternalOutput")
    w1m = nc.dram_tensor("w1m", [E, NFC1, 128, NKD, FC1], BF16)
    w2m = nc.dram_tensor("w2m", [E, NKD, 128, NKF, 128], BF16)

    x_res = nc.dram_tensor("x_res", [TCORE, D], F32)
    compA = [nc.dram_tensor(f"compA{e_i}", [CAP + NT, PW], BF16) for e_i in range(E)]
    cc_prob_in = nc.dram_tensor("cc_prob_in", [E, TCORE], F32)
    cc_prob_out = nc.dram_tensor("cc_prob_out", [NCORE * E, TCORE], F32)
    cc_car_in = nc.dram_tensor("cc_car_in", [128, 8], F32)
    cc_car_out = nc.dram_tensor("cc_car_out", [NCORE * 128, 8], F32)

    for s in ["pe", "dve", "act", "cc",
              "s_sml", "s_x0", "s_x1", "s_tp0", "s_tp1", "s_cr0", "s_cr1",
              "s_st", "s_pr", "s_bis", "s_m10", "s_m11", "s_m20", "s_m21", "s_w0", "s_w1",
              "a_tp00", "a_tp01", "a_tp10", "a_tp11",
              "g_pr", "g_gt", "g_w0", "g_w1", "g_w2", "g_sa", "g_cc",
              "g_fa0", "g_fa1", "g_xt"]:
        c.new_sem(s)
    for e_i in range(E):
        c.new_sem(f"g_sc{e_i}")

    S = {}

    def sb(name, shape, dt=F32):
        S[name] = es.enter_context(nc.sbuf_tensor("sb_" + name, list(shape), dt))
        return S[name]

    sb("ident_f32", [128, 128])
    sb("blockdiag8", [128, 128])
    sb("ones_1x128_f32", [1, 128]); sb("ones_1x128_bf16", [1, 128], BF16)
    sb("gate_wT", [128, NKD, E], BF16)
    sb("selmat", [128, E]); sb("wvec", [128, E])
    sb("thitlo", [128, NT, 2], BF16)
    sb("alpha", [1, 1])
    sb("rstd_all", [128, NT])
    sb("probs_tm", [128, NT, E])
    sb("slots_i32", [128, NT, E], I32)
    sb("th_tm", [128, E]); sb("th_em", [E, 1]); sb("th_row", [1, E])
    sb("xtile0", [128, D]); sb("xtile1", [128, D])
    sb("stat", [128, 24])
    sb("et0", [128, E]); sb("et1", [128, E])
    sb("gm_s", [128, E])
    sb("lo", [128, 1]); sb("hi", [128, 1]); sb("th", [128, 1])
    sb("cnt_s", [128, 1]); sb("ge", [128, 1]); sb("d1", [128, 1]); sb("d2c", [128, 1])
    sb("Ab", [128, NKD, R], BF16); sb("At", [128, NKD, R], BF16)
    sb("Cb", [128, NKD, R], BF16); sb("Ct", [128, NKD, R], BF16)
    sb("Xtw", [128, NKD, R], BF16)
    sb("Fb", [128, 2, D], BF16); sb("Ft", [128, 2, D], BF16)
    sb("bvfm", [128, NKD])
    sb("bvfmb", [128, NKD], BF16)
    sb("bprojT", [1, D], BF16)
    sb("g_col0", [128, CB]); sb("g_col1", [128, CB])
    sb("idx0", [128, CB], I32); sb("idx1", [128, CB], I32)
    sb("trow0", [128, CB]); sb("trow1", [128, CB])
    payload = sb("payload", [128, NT, PW], BF16)
    xnfmt = [sb("xnfmt0", [128, NKD, 128], BF16), sb("xnfmt1", [128, NKD, 128], BF16)]
    fwa = [sb("fwa0", [128, 1024], BF16), sb("fwa1", [128, 1024], BF16)]
    fvb = [sb("fvb0", [128, 256], BF16), sb("fvb1", [128, 256], BF16)]
    sb("alphab", [128, 1])
    at_car_loc = sb("at_car_loc", [128, 8])
    at_car_all = sb("at_car_all", [128, NCORE, 8])
    at_carry = sb("at_carry", [128, 8])

    def ov_plan(items):
        offs, cur = {}, 0
        for name, shape, dt in items:
            nbytes = int(np.prod(shape[1:])) * (2 if dt in (BF16, mybir.dt.float16) else 4)
            nbytes = (nbytes + 31) // 32 * 32
            offs[name] = cur
            cur += nbytes
        return cur, offs

    o1a = [("probs_em", [E, TCORE], F32), ("em_incl", [E, TCORE], F32),
           ("em_ssel", [E, TCORE], F32), ("em_d2", [E, TCORE], F32),
           ("em_mask", [E, TCORE], BF16), ("em_neg", [E, TCORE], BF16),
           ("probbis", [128, SEGW], F32), ("iota_em", [E, TCORE], F32),
           ("tile_em", [E, TCORE], F32), ("maskjunk", [128, SEGW], BF16)]
    o1b = [("h", [128, NKF, CAP], BF16)]
    o2a = [("cbuf0", [128, CB, PW], BF16),
           ("xinfm0", [128, CB, NKD, 128], BF16),
           ("w1c0", [128, NKD, FC1], BF16), ("w1c1", [128, NKD, FC1], BF16),
           ("w2c0", [128, NKF, 128], BF16), ("w2c1", [128, NKF, 128], BF16),
           ("yfm0", [128, CB, NKD, 128], BF16), ("yfm1", [128, CB, NKD, 128], BF16),
           ("ytm0", [128, NKD, 128], BF16), ("ytm1", [128, NKD, 128], BF16),
           ("yf0", [128, D], F32), ("yf1", [128, D], F32)]
    ACHW = min(512, TCORE)
    o2b = [("at_xnfmc0", [128, NKD, ACHW], BF16), ("at_xnfmc1", [128, NKD, ACHW], BF16),
           ("rm_ab", [128, 2, TCORE], BF16), ("rm_at", [128, 2, TCORE], BF16),
           ("rm_cb", [128, 2, TCORE], BF16), ("rm_ct", [128, 2, TCORE], BF16),
           ("rm_zt", [128, 2, TCORE], BF16),
           ("at_rcf", [128, TCORE], BF16), ("at_rcaf", [128, TCORE], BF16),
           ("at_xntm0", [128, D], BF16), ("at_xntm1", [128, D], BF16),
           ("invc_row", [1, TCORE], F32)]
    p3 = o1b + o2a
    sz_p2, off_p2 = ov_plan(o1a)
    sz_p3, off_p3 = ov_plan(p3)
    sz_at, off_at = ov_plan(o2b)
    REG = max(sz_p2, sz_p3, sz_at)
    top = nc.sbuf_top
    reg_base = (top - REG) // 32 * 32
    assert reg_base >= nc.sbuf_base, (
        f"SBUF overlay overflow: reg_base={reg_base} auto_base={nc.sbuf_base} "
        f"sizes p2={sz_p2} p3={sz_p3} at={sz_at}")
    for items, offs in [(o1a, off_p2), (p3, off_p3), (o2b, off_at)]:
        for name, shape, dt in items:
            S[name] = nc.alloc_sbuf_tensor_at("ov_" + name, list(shape), dt,
                                              offset=reg_base + offs[name])
    probs_em = S["probs_em"]
    probbis = S["probbis"]
    em = {"mask": S["em_mask"], "incl": S["em_incl"], "ssel": S["em_ssel"],
          "d2": S["em_d2"], "neg": S["em_neg"], "slotf": S["em_ssel"]}
    cbuf = [S["cbuf0"], S["cbuf0"]]
    xinfm = [S["xinfm0"], S["xinfm0"]]
    h = S["h"]
    w1c = [S["w1c0"], S["w1c1"]]
    w2c = [S["w2c0"], S["w2c1"]]
    yfm = [S["yfm0"], S["yfm1"]]
    ytm = [S["ytm0"], S["ytm1"]]
    yf32 = [S["yf0"], S["yf1"]]
    at_xnfmc = [S["at_xnfmc0"], S["at_xnfmc1"]]
    rm = {n: S["rm_" + n] for n in ["ab", "at", "cb", "ct", "zt"]}
    at_rcf = S["at_rcf"]
    at_rcaf = S["at_rcaf"]
    at_xntm = [S["at_xntm0"], S["at_xntm1"]]

    ps_a = es.enter_context(nc.psum_tensor("ps_a", [128, 1024], F32))
    ps_b = es.enter_context(nc.psum_tensor("ps_b", [128, 1024], F32))
    ps_c = es.enter_context(nc.psum_tensor("ps_c", [128, 1024], F32))
    ps_d = es.enter_context(nc.psum_tensor("ps_d", [128, 1024], F32))
    ps_sc = ps_c
    ps_tr = ps_d
    psbufs = [ps_a, ps_b, ps_c, ps_d]

    for sbn in ["ident_f32", "blockdiag8", "ones_1x128_f32", "ones_1x128_bf16",
                "selmat", "wvec", "thitlo", "alpha"]:
        c.dma("sync", "s_sml", lambda e, sbn=sbn: e.dma_start(
            out=S[sbn][:], in_=dram[sbn].ap()))
    SML = c.cnt["s_sml"]
    c.dma("pool", "g_gt", lambda e: e.dma_start(
        out=S["gate_wT"][:], in_=dram["gate_wT"].ap().rearrange("(k p) e -> p k e", p=128)))
    c.dma("pool", "g_pr", lambda e: e.dma_start(out=x_res.ap(), in_=dram["xin"].ap()))
    GPR = c.cnt["g_pr"]

    c.dve(lambda e: e.memset(payload[:, :, D + E + 2:PW], 0.0))

    def ln_tile(st, i, xt, sq, sc, sx, xv, out_ap_fn, extra_waits=()):
        del sq
        """Emit LN for tile i: stats + normalized bf16 tile via out_ap_fn."""
        c.dve(lambda e, xt=xt, sc=sc: e.reduce_sum(S["stat"][:, sc:sc + 1], xt[:], axis=AXX),
              waits=[(sx, xv)])
        c.dve(lambda e, sc=sc: e.tensor_scalar_mul(S["stat"][:, sc + 1:sc + 2],
                                                   S["stat"][:, sc:sc + 1], 1.0 / D))
        c.dve(lambda e, xt=xt, sc=sc: e.scalar_tensor_tensor(
            out_ap_fn(), xt[:], 1.0, xt[:], ALU.mult, ALU.mult,
            accum_out=S["stat"][:, sc + 2:sc + 3]), waits=extra_waits)
        c.dve(lambda e, sc=sc: e.tensor_scalar_mul(S["stat"][:, sc + 3:sc + 4],
                                                   S["stat"][:, sc + 2:sc + 3], 1.0 / D))
        c.dve(lambda e, sc=sc: e.tensor_scalar(
            S["stat"][:, sc + 4:sc + 5], S["stat"][:, sc + 1:sc + 2],
            S["stat"][:, sc + 1:sc + 2], None, ALU.mult))
        c.dve(lambda e, sc=sc: e.tensor_sub(S["stat"][:, sc + 5:sc + 6],
                                            S["stat"][:, sc + 3:sc + 4],
                                            S["stat"][:, sc + 4:sc + 5]))
        c.dve(lambda e, sc=sc: e.tensor_scalar_add(S["stat"][:, sc + 6:sc + 7],
                                                   S["stat"][:, sc + 5:sc + 6], EPS))
        rv = c.dve(lambda e, sc=sc: e.reciprocal(S["stat"][:, sc + 7:sc + 8],
                                                 S["stat"][:, sc + 6:sc + 7]))
        rs = c.act(lambda e, i=i, sc=sc: e.activation(
            out=S["rstd_all"][:, i:i + 1], in_=S["stat"][:, sc + 7:sc + 8], func=AF.Sqrt),
            waits=[("dve", rv)])
        nm = c.dve(lambda e, i=i, sc=sc: e.tensor_scalar_mul(
            S["stat"][:, sc + 8:sc + 9], S["stat"][:, sc + 1:sc + 2], -1.0),
            waits=[("act", rs)])
        return c.dve(lambda e, xt=xt, i=i, sc=sc: e.scalar_tensor_tensor(
            out_ap_fn(), xt[:], S["stat"][:, sc + 8:sc + 9],
            S["rstd_all"][:, i:i + 1].to_broadcast([128, D]), ALU.add, ALU.mult),
            waits=extra_waits)

    def ffn_phase1(st, base, base_ready):
        st["xn"] = {}; st["sc"] = {}; st["sm"] = {}; st["pr"] = {}; st["em"] = {}
        for i in range(NT):
            par = i % 2
            xt = S["xtile0"] if par == 0 else S["xtile1"]
            sq = None
            et = S["et0"] if par == 0 else S["et1"]
            sx = f"s_x{par}"
            sc = 12 * par
            if i >= 2:
                c.wait("sync", "dve", st["xn"][i - 2])
            for s_, v_ in base_ready:
                c.wait("sync", s_, v_)
            c.dma("sync", sx, lambda e, xt=xt, i=i: e.dma_start(
                out=xt[:], in_=base.ap()[i * 128:(i + 1) * 128, :]))
            xv = c.cnt[sx]
            if i >= 2:
                c.wait("dve", "act", st["sm"][i - 2])
            st["xn"][i] = ln_tile(st, i, xt, sq, sc, sx, xv,
                                  (lambda i=i: payload[:, i, 0:D]))
            stp = f"s_tp{par}"
            if i >= 2:
                c.wait("sync", "pe", st["sc"][i - 2])
            c.wait("sync", "dve", st["xn"][i])
            c.dma("sync", stp, lambda e, par=par, i=i: e.dma_start(
                out=xnfmt[par][:], in_=payload[:, i, 0:D], transpose=True))
            tpv = c.cnt[stp]
            c.wait("pe", stp, tpv)
            c.wait("pe", "g_gt", 16)
            if i >= 1:
                c.wait("pe", "act", st["sm"][i - 1])
            for kc in range(NKD):
                st["sc"][i] = c.on("pe", lambda e, par=par, kc=kc: e.matmul(
                    ps_sc[:, 0:E], xnfmt[par][:, kc, :], S["gate_wT"][:, kc, :],
                    start=(kc == 0), stop=(kc == NKD - 1)),
                    inc=("pe" if kc == NKD - 1 else None))
            c.dve(lambda e, sc=sc: e.reduce_max(S["stat"][:, sc + 9:sc + 10],
                                                ps_sc[:, 0:E], axis=AXX),
                  waits=[("pe", st["sc"][i])])
            nx = c.dve(lambda e, sc=sc: e.tensor_scalar_mul(
                S["stat"][:, sc + 10:sc + 11], S["stat"][:, sc + 9:sc + 10], -1.0))
            st["sm"][i] = c.act(lambda e, et=et, sc=sc: e.activation(
                out=et[:], in_=ps_sc[:, 0:E], func=AF.Exp,
                bias=S["stat"][:, sc + 10:sc + 11], scale=1.0,
                accum_out=S["stat"][:, sc + 11:sc + 12]), waits=[("dve", nx)])
            c.dve(lambda e, sc=sc: e.reciprocal(S["stat"][:, sc + 9:sc + 10],
                                                S["stat"][:, sc + 11:sc + 12]),
                  waits=[("act", st["sm"][i])])
            st["pr"][i] = c.dve(lambda e, et=et, i=i, sc=sc: e.tensor_scalar_mul(
                S["probs_tm"][:, i, :], et[:], S["stat"][:, sc + 9:sc + 10]))
            c.wait("pe", "dve", st["pr"][i])
            if i >= 1:
                c.wait("pe", "dve", st["em"][i - 1])
            tv = c.on("pe", lambda e, i=i: e.transpose(
                ps_tr[0:E, 0:128], S["probs_tm"][:, i, :], S["ident_f32"][:]), inc="pe")
            st["em"][i] = c.dve(lambda e, i=i: e.tensor_copy(
                probs_em[:, i * 128:(i + 1) * 128], ps_tr[0:E, 0:128]),
                waits=[("pe", tv)])

    def ffn_phase2(st):
        st["sl"] = {}; st["gm"] = {}
        for nm in ["iota_em", "tile_em"]:
            c.wait("sync", "s_pr", c.cnt["s_pr"])
            c.dma("sync", "s_pr", lambda e, nm=nm: e.dma_start(
                out=S[nm][:], in_=dram[nm].ap()))
        iotld = c.cnt["s_pr"]
        c.wait("dve", "s_pr", iotld)
        c.wait("sync", "dve", st["em"][NT - 1])
        c.wait("sync", "s_pr", c.cnt["s_pr"])
        c.dma("sync", "s_pr", lambda e: e.dma_start(out=cc_prob_in.ap(), in_=probs_em[:]))
        c.wait("pool", "s_pr", c.cnt["s_pr"])
        c.on("pool", lambda e: e.collective_compute(
            "AllGather", ALU.bypass, replica_groups=[list(range(NCORE))],
            ins=[cc_prob_in.ap()], outs=[cc_prob_out.ap()]), inc="cc", n=1)
        c.wait("pool", "cc", c.cnt["cc"])
        c.wait("sync", "cc", c.cnt["cc"])
        for r_ in range(NCORE):
            n_, q_ = r_ // 4, r_ % 4
            for e_i in range(E):
                p0 = 64 * n_ + 8 * e_i + 2 * q_

                def mk(e, r_=r_, e_i=e_i, p0=p0):
                    src = cc_prob_out.ap()[r_ * E + e_i:r_ * E + e_i + 1, :].rearrange(
                        "one (s w) -> (one s) w", s=2)
                    return e.dma_start(out=probbis[p0:p0 + 2, :], in_=src)
                c.dma("sync", "s_bis", mk)
        gccv = c.cnt["s_bis"]
        c.dve(lambda e: e.memset(S["lo"][:], 0.0), waits=[("s_bis", gccv)])
        c.dve(lambda e: e.memset(S["hi"][:], 1.0))
        c.dve(lambda e: e.memset(S["th"][:], 0.5))
        for _ in range(NITER):
            cv = c.dve(lambda e: e.tensor_scalar(
                S["maskjunk"][:], probbis[:], S["th"][:], None, ALU.is_ge,
                op1=ALU.add, accum_out=S["cnt_s"][:]))
            c.wait("pe", "dve", cv)
            pv = c.on("pe", lambda e: e.matmul(
                ps_tr[:, 256:257], S["blockdiag8"][:], S["cnt_s"][:],
                start=True, stop=True), inc="pe")
            c.dve(lambda e: e.tensor_scalar(
                S["ge"][:], ps_tr[:, 256:257], float(KSEL), None, ALU.is_ge),
                waits=[("pe", pv)])
            c.dve(lambda e: e.tensor_sub(S["d1"][:], S["th"][:], S["lo"][:]))
            c.dve(lambda e: e.scalar_tensor_tensor(
                S["lo"][:], S["ge"][:], S["d1"][:], S["lo"][:], ALU.mult, ALU.add))
            c.dve(lambda e: e.tensor_sub(S["d1"][:], S["hi"][:], S["th"][:]))
            c.dve(lambda e: e.scalar_tensor_tensor(
                S["d2c"][:], S["ge"][:], S["d1"][:], S["th"][:], ALU.mult, ALU.add))
            c.dve(lambda e: e.tensor_copy(S["hi"][:], S["d2c"][:]))
            c.dve(lambda e: e.tensor_add(S["d1"][:], S["lo"][:], S["hi"][:]))
            c.dve(lambda e: e.tensor_scalar_mul(S["th"][:], S["d1"][:], 0.5))
        bis = c.cnt["dve"]
        c.wait("pe", "dve", bis)
        pv = c.on("pe", lambda e: e.matmul(ps_tr[0:E, 258:259], S["selmat"][:], S["lo"][:],
                                           start=True, stop=True), inc="pe")
        v = c.dve(lambda e: e.tensor_copy(S["th_em"][:], ps_tr[0:E, 258:259]),
                  waits=[("pe", pv)])
        c.wait("pe", "dve", v)
        pv = c.on("pe", lambda e: e.transpose(ps_tr[0:1, 260:260 + E], S["th_em"][:],
                                              S["ident_f32"][0:E, 0:E]), inc="pe")
        v = c.dve(lambda e: e.tensor_copy(S["th_row"][:], ps_tr[0:1, 260:260 + E]),
                  waits=[("pe", pv)])
        c.wait("pe", "dve", v)
        pv = c.on("pe", lambda e: e.matmul(ps_tr[:, 272:272 + E], S["ones_1x128_f32"][:],
                                           S["th_row"][:], start=True, stop=True), inc="pe")
        thtm = c.dve(lambda e: e.tensor_copy(S["th_tm"][:], ps_tr[:, 272:272 + E]),
                     waits=[("pe", pv)])
        c.dve(lambda e: e.tensor_scalar(
            em["mask"][:], probs_em[:], S["th_em"][:], None, ALU.is_ge))
        c.dve(lambda e: e.tensor_tensor_scan(
            em["incl"][:], em["mask"][:], em["mask"][:], 0.0, ALU.add, ALU.bypass))
        c.dve(lambda e: e.tensor_sub(em["ssel"][:], em["incl"][:], em["mask"][:]))
        c.dve(lambda e: e.scalar_tensor_tensor(
            em["d2"][:], em["ssel"][:], float(CAP - 1), S["iota_em"][:],
            ALU.add, ALU.subtract))
        c.dve(lambda e: e.tensor_scalar(em["neg"][:], em["d2"][:], 0.0, None, ALU.is_lt))
        c.dve(lambda e: e.scalar_tensor_tensor(
            em["d2"][:], em["neg"][:], 65536.0, em["d2"][:], ALU.mult, ALU.add))
        c.dve(lambda e: e.tensor_scalar(
            em["neg"][:], em["d2"][:], em["incl"][:, TCORE - 1:TCORE], None, ALU.is_lt))
        c.dve(lambda e: e.scalar_tensor_tensor(
            em["d2"][:], em["neg"][:], 65536.0, em["d2"][:], ALU.mult, ALU.add))
        c.dve(lambda e: e.tensor_sub(em["slotf"][:], em["ssel"][:], em["d2"][:]))
        c.dve(lambda e: e.tensor_mul(em["slotf"][:], em["slotf"][:], em["mask"][:]))
        c.dve(lambda e: e.tensor_add(em["slotf"][:], em["slotf"][:], em["d2"][:]))
        c.dve(lambda e: e.tensor_scalar_min(em["slotf"][:], em["slotf"][:],
                                            float(CAP)))
        c.dve(lambda e: e.tensor_scalar(em["neg"][:], em["slotf"][:], float(CAP),
                                        None, ALU.is_equal))
        c.dve(lambda e: e.tensor_mul(em["d2"][:], em["neg"][:], S["tile_em"][:]))
        slv = c.dve(lambda e: e.tensor_add(em["slotf"][:], em["slotf"][:], em["d2"][:]))
        for i in range(NT):
            c.wait("pe", "dve", slv if i == 0 else st["sl"][i - 1])
            pv = c.on("pe", lambda e, i=i: e.transpose(
                ps_tr[:, 288:288 + E], em["slotf"][:, i * 128:(i + 1) * 128],
                S["ident_f32"][0:E, 0:E]), inc="pe")
            st["sl"][i] = c.dve(lambda e, i=i: e.tensor_copy(
                S["slots_i32"][:, i, :], ps_tr[:, 288:288 + E]), waits=[("pe", pv)])
            c.dve(lambda e, i=i: e.tensor_tensor(
                S["gm_s"][:], S["probs_tm"][:, i, :], S["th_tm"][:], ALU.is_ge),
                waits=[("dve", thtm)])
            st["gm"][i] = c.dve(lambda e, i=i: e.tensor_tensor(
                payload[:, i, D:D + E], S["gm_s"][:], S["probs_tm"][:, i, :], ALU.mult))
        st["tcol"] = c.dve(lambda e: e.tensor_copy(
            payload[:, :, D + E:D + E + 2], S["thitlo"][:]))

    def ffn_phase3(st, target, target_ready, entry_act, first_call):
        first_call = first_call or not use_mirror
        st["mm1"] = {}; st["gel"] = {}; st["oev"] = {}; st["ysc"] = {}; st["sad"] = {}
        st["idxd"] = {}; st["tpv"] = {}; st["ytv"] = {}; st["gv"] = {}
        st["mir"] = {}; st["mir2"] = {}
        nwc = NFC1 + NKD
        def emit_scatter_batch_step(e_i, i):
            sem = f"g_sc{e_i}"
            if True:
                c.wait("pool", "dve", max(st["gm"][i], st["sl"][i], st["tcol"],
                                          st["xn"][i]))
                for s_, v_ in target_ready:
                    c.wait("pool", s_, v_)
                c.wait("pool", sem, c.cnt[sem])

                def mk(e, e_i=e_i, i=i):
                    return e.indirect_dma_start(
                        out=compA[e_i].ap(),
                        out_offset=IndirectOffsetOnAxis(S["slots_i32"][:, i, e_i:e_i + 1], 0),
                        in_=payload[:, i, :], in_offset=None)
                c.dma("pool", sem, mk)

        for i in range(NT):
            for e_i in range(E):
                emit_scatter_batch_step(e_i, i)
        first_pe_wait = entry_act

        def make_y_bundles(e_i):
            par = e_i % 2
            g_colv = S["g_col0"] if par == 0 else S["g_col1"]
            idxv = S["idx0"] if par == 0 else S["idx1"]
            oevall = st["oev"][(e_i, NKD - 1)]
            gv = st["gv"][e_i]
            yfme = yfm[par]
            g0 = c.cnt["g_sa"]
            for b in range(CB):
                st["sad"][(e_i, b)] = g0 + 16 * (b + 1)

            def chain(b):
                c.wait("act", "act", oevall)
                if b >= 2:
                    c.wait("act", "dve", st["ysc"][(e_i, b - 2)])
                elif e_i > 0:
                    c.wait("act", "dve", st["ysc"][(e_i - 1, CB - 1 - b)])
                tps = f"a_tp{par}{b % 2}"
                c.dma("act", tps, lambda e, b=b, yfme=yfme: e.dma_start(
                    out=ytm[b % 2][:], in_=yfme[:, b, :, :], transpose=True))
                ytv = c.cnt[tps]
                yb = yf32[b % 2]
                ysc_waits = [(tps, ytv), ("dve", gv)]
                if b >= 2:
                    ysc_waits.append(("g_sa", st["sad"][(e_i, b - 2)]))
                elif e_i > 0:
                    ysc_waits.append(("g_sa", st["sad"][(e_i - 1, CB - 1 - b)]))
                else:
                    ysc_waits.append(("g_sa", st["sa0"]))
                st["ysc"][(e_i, b)] = c.dve(
                    lambda e, b=b, g_colv=g_colv, yb=yb: e.tensor_scalar_mul(
                        yb[:], ytm[b % 2][:].rearrange("p c t -> p (c t)"),
                        g_colv[:, b:b + 1]), waits=ysc_waits)
                if b == CB - 1:
                    st["ytv"][e_i] = (c.cnt[f"a_tp{par}0"], c.cnt[f"a_tp{par}1"])

            def sadd(b):
                yb = yf32[b % 2]
                c.wait("pool", "dve", max(st["ysc"][(e_i, b)], st["idxd"][e_i]))
                c.wait("pool", "g_sa", c.cnt["g_sa"])

                def mksa(e, b=b, idxv=idxv, yb=yb):
                    return e.indirect_dma_start(
                        out=target.ap(),
                        out_offset=IndirectOffsetOnAxis(idxv[:, b:b + 1], 0),
                        in_=yb[:], in_offset=None, compute_op=ALU.add)
                got = c.dma("pool", "g_sa", mksa)
                assert got == st["sad"][(e_i, b)], (got, st["sad"][(e_i, b)])
            return ([lambda b=b: chain(b) for b in range(CB)],
                    [lambda b=b: sadd(b) for b in range(CB)])

        for e_i in range(E):
            par = e_i % 2
            scr = f"s_cr{par}"
            g_colv = S["g_col0"] if par == 0 else S["g_col1"]
            idxv = S["idx0"] if par == 0 else S["idx1"]
            trowv = S["trow0"] if par == 0 else S["trow1"]
            c.wait("sync", f"g_sc{e_i}", c.cnt[f"g_sc{e_i}"])
            if e_i >= 1:
                c.wait("sync", "dve", st["idxd"][e_i - 1])
                c.wait("sync", f"s_tp{(e_i - 1) % 2}", st["tpv"][e_i - 1])
            c.dma("sync", scr, lambda e, par=par, e_i=e_i: e.dma_start(
                out=cbuf[par][:],
                in_=compA[e_i].ap()[0:CAP, :].rearrange("(b p) w -> p b w", p=128)))
            crv = c.cnt[scr]
            st["rb"] = (scr, crv)
            if e_i >= 1:
                c.wait("sync", "pe", st["mm1"][e_i - 1])
            for b in range(CB):
                c.wait("sync", scr, crv)
                c.dma("sync", f"s_tp{par}", lambda e, par=par, b=b: e.dma_start(
                    out=xinfm[par][:, b, :, :], in_=cbuf[par][:, b, 0:D], transpose=True))
            tpv = c.cnt[f"s_tp{par}"]
            st["tpv"][e_i] = tpv
            gw = [(scr, crv)]
            if e_i >= 2:
                gw.append(("g_sa", st["sad"][(e_i - 2, CB - 1)]))
            gv = c.dve(lambda e, par=par, g_colv=g_colv, e_i=e_i: e.tensor_scalar_mul(
                g_colv[:], cbuf[par][:, :, D + e_i], 0.5), waits=gw)
            st["gv"][e_i] = gv
            c.dve(lambda e, par=par, trowv=trowv: e.scalar_tensor_tensor(
                trowv[:], cbuf[par][:, :, D + E], 64.0, cbuf[par][:, :, D + E + 1],
                ALU.mult, ALU.add))
            st["idxd"][e_i] = c.dve(
                lambda e, idxv=idxv, trowv=trowv: e.tensor_copy(idxv[:], trowv[:]))
            if e_i >= 1:
                ych, pending_y = make_y_bundles(e_i - 1)
                for f in ych:
                    f()
            else:
                pending_y = []
            for fc in range(NFC1):
                slot = (e_i * nwc + fc) % 2
                wsem = f"g_w{slot}" if first_call else f"s_w{slot}"
                if first_call:
                    c.wait("pool", "pe", st["wfree"].get(slot, 0))
                    c.wait("pool", f"s_m1{slot}", st["mir"].get(slot, 0))
                    c.dma("pool", wsem, lambda e, e_i=e_i, fc=fc, slot=slot: e.dma_start(
                        out=w1c[slot][:], in_=dram["w1"].ap()[e_i][fc]))
                    wv = c.cnt[wsem]
                    if use_mirror:
                        c.wait("sync", wsem, wv)
                        c.dma("sync", f"s_m1{slot}", lambda e, e_i=e_i, fc=fc, slot=slot:
                              e.dma_start(out=w1m.ap()[e_i][fc], in_=w1c[slot][:]))
                        st["mir"][slot] = c.cnt[f"s_m1{slot}"]
                else:
                    c.wait("sync", "pe", st["wfree"].get(slot, 0))
                    c.dma("sync", wsem, lambda e, e_i=e_i, fc=fc, slot=slot: e.dma_start(
                        out=w1c[slot][:], in_=w1m.ap()[e_i][fc]))
                    wv = c.cnt[wsem]
                c.wait("pe", wsem, wv)
                if fc == 0:
                    c.wait("pe", f"s_tp{par}", tpv)
                    if first_pe_wait:
                        c.wait("pe", "act", first_pe_wait)
                        first_pe_wait = 0
                for ft in range(4):
                    fidx = fc * 4 + ft
                    hp = psbufs[fidx % 4]
                    if ft % 2 == 0:
                        if fidx >= 4:
                            c.wait("pe", "act", st["gel"][(e_i, fidx - 3)])
                        elif e_i >= 1:
                            c.wait("pe", "act",
                                   st["oev"][(e_i - 1, 5 if fidx == 0 else 7)])
                    for kc in range(NKD):
                        for gi, (gb, gn) in enumerate(CGRP):
                            last = (kc == NKD - 1) and (gi == len(CGRP) - 1)
                            c.on("pe", lambda e, slot=slot, kc=kc, ft=ft, hp=hp,
                                 par=par, gb=gb, gn=gn: e.matmul(
                                hp[:, gb * 128:(gb + gn) * 128],
                                w1c[slot][:, kc, ft * 128:(ft + 1) * 128],
                                xinfm[par][:, gb:gb + gn, kc, :],
                                start=(kc == 0), stop=(kc == NKD - 1)),
                                inc=("pe" if last else None))
                    if ft % 2 == 1:
                        mmv = c.cnt["pe"]
                        st["gel"][(e_i, fidx - 1)] = c.act(
                            lambda e, fidx=fidx, hq=psbufs[(fidx - 1) % 4]:
                            e.activation(out=h[:, fidx - 1, :], in_=hq[:, 0:CAP],
                                         func=AF.Gelu), waits=[("pe", mmv)])
                        st["gel"][(e_i, fidx)] = c.act(
                            lambda e, fidx=fidx, hp=hp: e.activation(
                                out=h[:, fidx, :], in_=hp[:, 0:CAP], func=AF.Gelu))
                st["wfree"][slot] = c.cnt["pe"]
                if fc >= 1 and pending_y:
                    pending_y.pop(0)()
            st["mm1"][e_i] = c.cnt["pe"]
            gelall = st["gel"][(e_i, NKF - 1)]
            for dt_ in range(NKD):
                po = psbufs[dt_ % 4]
                slot = (e_i * nwc + NFC1 + dt_) % 2
                wsem = f"g_w{slot}" if first_call else f"s_w{slot}"
                if first_call:
                    c.wait("pool", "pe", st["wfree"].get(slot, 0))
                    c.wait("pool", f"s_m2{slot}", st["mir2"].get(slot, 0))
                    c.dma("pool", wsem, lambda e, e_i=e_i, dt_=dt_, slot=slot: e.dma_start(
                        out=w2c[slot][:], in_=dram["w2"].ap()[e_i][dt_]))
                    wv = c.cnt[wsem]
                    if use_mirror:
                        c.wait("sync", wsem, wv)
                        c.dma("sync", f"s_m2{slot}", lambda e, e_i=e_i, dt_=dt_, slot=slot:
                              e.dma_start(out=w2m.ap()[e_i][dt_], in_=w2c[slot][:]))
                        st["mir2"][slot] = c.cnt[f"s_m2{slot}"]
                else:
                    c.wait("sync", "pe", st["wfree"].get(slot, 0))
                    c.dma("sync", wsem, lambda e, e_i=e_i, dt_=dt_, slot=slot: e.dma_start(
                        out=w2c[slot][:], in_=w2m.ap()[e_i][dt_]))
                    wv = c.cnt[wsem]
                c.wait("pe", wsem, wv)
                if dt_ == 0:
                    c.wait("pe", "act", gelall)
                if dt_ % 2 == 0 and dt_ >= 4:
                    c.wait("pe", "act", st["oev"][(e_i, dt_ - 3)])
                if dt_ < 2 and e_i > 1:
                    c.wait("act", f"a_tp{e_i % 2}0", st["ytv"][e_i - 2][0])
                    c.wait("act", f"a_tp{e_i % 2}1", st["ytv"][e_i - 2][1])
                for fk in range(NKF):
                    for gi, (gb, gn) in enumerate(CGRP):
                        last = (fk == NKF - 1) and (gi == len(CGRP) - 1)
                        c.on("pe", lambda e, slot=slot, fk=fk, po=po, gb=gb,
                             gn=gn: e.matmul(
                            po[:, gb * 128:(gb + gn) * 128], w2c[slot][:, fk, :],
                            h[:, fk, gb * 128:(gb + gn) * 128],
                            start=(fk == 0), stop=(fk == NKF - 1)),
                            inc=("pe" if last else None))
                mm2v = c.cnt["pe"]
                st["wfree"][slot] = mm2v
                if dt_ % 2 == 1:
                    st["oev"][(e_i, dt_ - 1)] = c.act(
                        lambda e, dt_=dt_, pq=psbufs[(dt_ - 1) % 4], par=par:
                        e.activation(out=yfm[par][:, :, dt_ - 1, :],
                                     in_=pq[:, 0:CAP], func=AF.Copy),
                        waits=[("pe", mm2v)])
                    st["oev"][(e_i, dt_)] = c.act(
                        lambda e, dt_=dt_, po=po, par=par:
                        e.activation(out=yfm[par][:, :, dt_, :], in_=po[:, 0:CAP],
                                     func=AF.Copy))
            st["oevall"] = st["oev"][(e_i, NKD - 1)]
            for f in pending_y:
                f()
        ych, ysadds = make_y_bundles(E - 1)
        for f in ych:
            f()
        for f in ysadds:
            f()

    def emit_folds():
        fold_ev = 0
        c.wait("pe", "dve", c.cnt["dve"])
        c.wait("pe", "act", c.cnt["act"])

        def LD(par, dst_fn, src_fn, consumed):
            c.wait("pool", consumed[0], consumed[1])
            c.dma("pool", f"g_fa{par}", lambda e, dst_fn=dst_fn, src_fn=src_fn:
                  e.dma_start(out=dst_fn(), in_=src_fn()))
            return c.cnt[f"g_fa{par}"]

        c.dma("pool", "g_xt", lambda e: e.dma_start(
            out=S["Xtw"][:], in_=dram["xt_w"].ap().rearrange("(k p) r -> p k r", p=128)))
        def regA(mt):
            pst = [ps_a, ps_b, ps_c, ps_d][mt // 2]
            return pst[:, (mt % 2) * 512:(mt % 2) * 512 + R]
        lastmm = {0: 0, 1: 0}
        for wn, vn, outn in [("wq", "vb", "Ab"), ("wq", "vt", "At"),
                             ("wk", "wb", "Cb"), ("wk", "wt", "Ct")]:
            for kc in range(NKD):
                par = kc % 2
                lw = LD(par, (lambda par=par: fwa[par][:]),
                        (lambda wn=wn, kc=kc: dram[wn].ap()[kc * 128:(kc + 1) * 128, :]),
                        ("pe", lastmm[par]))
                lv = LD(par, (lambda par=par: fvb[par][:]),
                        (lambda vn=vn, kc=kc: dram[vn].ap()[kc * 128:(kc + 1) * 128, :]),
                        ("pe", lastmm[par]))
                c.wait("pe", f"g_fa{par}", max(lw, lv))
                if kc == 0:
                    c.wait("pe", "act", fold_ev)
                for mt in range(NKD):
                    c.on("pe", lambda e, par=par, mt=mt, kc=kc: e.matmul(
                        regA(mt), fwa[par][:, mt * 128:(mt + 1) * 128], fvb[par][:],
                        start=(kc == 0), stop=(kc == NKD - 1)),
                        inc=("pe" if mt == NKD - 1 else None))
                lastmm[par] = c.cnt["pe"]
            mmv = c.cnt["pe"]
            for mt in range(NKD):
                fold_ev = c.act(lambda e, outn=outn, mt=mt: e.activation(
                    out=S[outn][:, mt, :], in_=regA(mt), func=AF.Copy),
                    waits=[("pe", mmv)])
        c.wait("sync", "s_pr", c.cnt["s_pr"])
        c.dma("sync", "s_pr", lambda e: e.dma_start(
            out=S["xtile0"][0:1, :], in_=dram["bias_t"].ap()))
        c.wait("sync", "s_pr", c.cnt["s_pr"])
        c.dma("sync", "s_pr", lambda e: e.dma_start(
            out=S["xtile1"][0:1, :], in_=dram["bias_b"].ap()))
        bld = c.cnt["s_pr"]
        bv = c.dve(lambda e: e.scalar_tensor_tensor(
            S["xtile0"][0:1, :], S["xtile0"][0:1, :], S["alpha"][0:1, 0:1],
            S["xtile1"][0:1, :], ALU.mult, ALU.add),
            waits=[("s_sml", SML), ("s_pr", bld)])
        ev2 = 0
        for kc in range(NKD):
            c.wait("pe", "dve", bv if kc == 0 else ev2)
            pv = c.on("pe", lambda e, kc=kc: e.transpose(
                ps_tr[:, 304:305], S["xtile0"][0:1, kc * 128:(kc + 1) * 128],
                S["ident_f32"][0:1, 0:1]), inc="pe")
            ev2 = c.dve(lambda e, kc=kc: e.tensor_copy(S["bvfm"][:, kc:kc + 1],
                                                       ps_tr[:, 304:305]),
                        waits=[("pe", pv)])
        bvb = c.dve(lambda e: e.tensor_copy(S["bvfmb"][:], S["bvfm"][:]))
        def regF(rt, nh):
            return (ps_c if rt == 0 else ps_d)[:, nh * 512:(nh + 1) * 512]
        for fi, (un, outn) in enumerate([("ub", "Fb"), ("ut", "Ft")]):
            for kc in range(NKD):
                par = kc % 2
                lu = LD(par, (lambda par=par: fvb[par][:]),
                        (lambda un=un, kc=kc: dram[un].ap()[kc * 128:(kc + 1) * 128, :]),
                        ("pe", lastmm[par]))
                lw = LD(par, (lambda par=par: fwa[par][:]),
                        (lambda kc=kc: dram["woT"].ap()[kc * 128:(kc + 1) * 128, :]),
                        ("pe", lastmm[par]))
                c.wait("pe", f"g_fa{par}", max(lu, lw))
                if kc == 0:
                    c.wait("pe", "act", fold_ev)
                    c.wait("pe", "dve", bvb)
                for rt in range(2):
                    for nh in range(2):
                        last = (fi == 1) and (rt == 1) and (nh == 1)
                        c.on("pe", lambda e, par=par, rt=rt, nh=nh, kc=kc: e.matmul(
                            regF(rt, nh), fvb[par][:, rt * 128:(rt + 1) * 128],
                            fwa[par][:, nh * 512:(nh + 1) * 512],
                            start=(kc == 0), stop=(kc == NKD - 1)),
                            inc=("pe" if last else None))
                if fi == 0:
                    for nh in range(2):
                        c.on("pe", lambda e, par=par, nh=nh, kc=kc: e.matmul(
                            ps_a[0:1, nh * 512:(nh + 1) * 512], S["bvfmb"][:, kc:kc + 1],
                            fwa[par][:, nh * 512:(nh + 1) * 512],
                            start=(kc == 0), stop=(kc == NKD - 1)),
                            inc=("pe" if nh == 1 else None))
                lastmm[par] = c.cnt["pe"]
            mmv = c.cnt["pe"]
            for rt in range(2):
                for nh in range(2):
                    fold_ev = c.act(lambda e, outn=outn, rt=rt, nh=nh: e.activation(
                        out=S[outn][:, rt, nh * 512:(nh + 1) * 512], in_=regF(rt, nh),
                        func=AF.Copy), waits=[("pe", mmv)])
            if fi == 0:
                for nh in range(2):
                    fold_ev = c.act(lambda e, nh=nh: e.activation(
                        out=S["bprojT"][:, nh * 512:(nh + 1) * 512],
                        in_=ps_a[0:1, nh * 512:(nh + 1) * 512], func=AF.Copy),
                        waits=[("pe", mmv)])
        return fold_ev

    def emit_attn(st, fold_act, fold_fw):
        st["xn"] = {}; st["xf"] = {}; st["x2"] = {}
        c.wait("sync", "s_pr", c.cnt["s_pr"])
        c.dma("sync", "s_pr", lambda e: e.dma_start(
            out=S["invc_row"][:], in_=dram["invc_row"].ap()))
        invld = c.cnt["s_pr"]
        c.wait("pe", "s_pr", invld)
        c.wait("pe", "s_sml", SML)
        pv = c.on("pe", lambda e: e.matmul(
            ps_tr[:, 300:301], S["ones_1x128_f32"][:], S["alpha"][:],
            start=True, stop=True), inc="pe")
        alv = c.dve(lambda e: e.tensor_copy(S["alphab"][:], ps_tr[:, 300:301]),
                    waits=[("pe", pv)])
        bc_ev = 0
        for ch in range(NCH_A):
            if ch == 0:
                c.wait("pe", "dve", alv)
            c.wait("pe", "act", bc_ev)
            pv = c.on("pe", lambda e, ch=ch: e.matmul(
                ps_tr[:, 0:ACH], S["ones_1x128_f32"][:],
                S["invc_row"][:, ch * ACH:(ch + 1) * ACH],
                start=True, stop=True), inc="pe")
            bc_ev = c.act(lambda e, ch=ch: e.activation(
                out=at_rcf[:, ch * ACH:(ch + 1) * ACH], in_=ps_tr[:, 0:ACH],
                func=AF.Copy), waits=[("pe", pv)])
        c.dve(lambda e: e.tensor_mul(at_rcaf[:], at_rcf[:], at_rcf[:]),
              waits=[("act", bc_ev), ("dve", alv)])
        c.dve(lambda e: e.tensor_scalar_mul(at_rcaf[:], at_rcaf[:], S["alphab"][:]))
        NCH = NCH_A
        TPC = ACH // 128
        projd = {}
        evs = {}
        plist = [("Ab", "ab"), ("At", "at"), ("Cb", "cb"), ("Ct", "ct"),
                 ("Xtw", "zt")]

        def emit_ln_chunk(ch):
            cpar = ch % 2
            if ch >= 2:
                for _, on2 in plist:
                    for rt2 in range(2):
                        c.wait("sync", "act", projd[(on2, rt2, ch - 2)])
            for ti in range(TPC):
                i = ch * TPC + ti
                par = i % 2
                xt = S["xtile0"] if par == 0 else S["xtile1"]
                sx = f"s_x{par}"
                sc = 12 * par
                if i >= 2:
                    c.wait("sync", "dve", st["xn"][i - 2])
                c.wait("sync", "g_sa", st["f1_sa"])
                c.dma("sync", sx, lambda e, xt=xt, i=i: e.dma_start(
                    out=xt[:], in_=x_res.ap()[i * 128:(i + 1) * 128, :]))
                xv = c.cnt[sx]
                ew = [(f"s_tp{par}", st["xf"][i - 2])] if i >= 2 else []
                st["xn"][i] = ln_tile(st, i, xt, None, sc, sx, xv,
                                      (lambda par=par: at_xntm[par][:]), extra_waits=ew)
                c.wait("sync", "dve", st["xn"][i])
                st["xf"][i] = c.dma("sync", f"s_tp{par}",
                                    lambda e, par=par, ti=ti, cpar=cpar: e.dma_start(
                    out=at_xnfmc[cpar][:, :, ti * 128:(ti + 1) * 128],
                    in_=at_xntm[par][:], transpose=True))

        def emit_proj_chunk(ch):
            cpar = ch % 2
            xnfc = at_xnfmc[cpar]
            for k in range(10):
                wn, on = plist[k // 2]
                rt = k % 2
                kg = ch * 10 + k
                pp = psbufs[kg % 4]
                if k % 2 == 0:
                    if kg == 0:
                        c.wait("pe", "g_xt", fold_fw)
                        c.wait("pe", "act", fold_act)
                        c.wait("pe", "act", bc_ev)
                    if k == 0:
                        for ti in range(TPC):
                            i = ch * TPC + ti
                            c.wait("pe", f"s_tp{i % 2}", st["xf"][i])
                    if kg >= 4:
                        c.wait("pe", "act", evs[kg - 3])
                for kc in range(NKD):
                    c.on("pe", lambda e, wn=wn, rt=rt, kc=kc, pp=pp,
                         xnfc=xnfc: e.matmul(
                        pp[:, 0:ACH], S[wn][:, kc, rt * 128:(rt + 1) * 128],
                        xnfc[:, kc, :],
                        start=(kc == 0), stop=(kc == NKD - 1)),
                        inc=("pe" if kc == NKD - 1 else None))
                if k % 2 == 1:
                    mv = c.cnt["pe"]
                    ev0 = c.act(lambda e, on=on, ch=ch, q=psbufs[(kg - 1) % 4]:
                                e.activation(
                        out=rm[on][:, 0, ch * ACH:(ch + 1) * ACH], in_=q[:, 0:ACH],
                        func=AF.Copy), waits=[("pe", mv)])
                    ev1 = c.act(lambda e, on=on, ch=ch, pp=pp: e.activation(
                        out=rm[on][:, 1, ch * ACH:(ch + 1) * ACH], in_=pp[:, 0:ACH],
                        func=AF.Copy))
                    evs[kg - 1] = ev0
                    evs[kg] = ev1
                    projd[(on, 0, ch)] = ev0
                    projd[(on, 1, ch)] = ev1

        emit_ln_chunk(0)
        for ch in range(NCH):
            if ch + 1 < NCH:
                emit_ln_chunk(ch + 1)
            emit_proj_chunk(ch)

        def wait_proj(eng, names):
            for on in names:
                for rt in range(2):
                    for ch in range(NCH):
                        c.wait(eng, "act", projd[(on, rt, ch)])
        wait_proj("dve", ["cb", "ct", "zt"])
        c.dve(lambda e: e.memset(at_car_loc[:], 0.0))
        for qi, on in enumerate(["cb", "ct", "zt"]):
            for rt in range(2):
                c.dve(lambda e, on=on, rt=rt, qi=qi: e.reduce_sum(
                    at_car_loc[:, qi * 2 + rt:qi * 2 + rt + 1], rm[on][:, rt, :], axis=AXX))
        tot = c.cnt["dve"]
        c.wait("sync", "dve", tot)
        c.wait("sync", "s_pr", c.cnt["s_pr"])
        c.dma("sync", "s_pr", lambda e: e.dma_start(out=cc_car_in.ap(), in_=at_car_loc[:]))
        c.wait("pool", "s_pr", c.cnt["s_pr"])
        c.on("pool", lambda e: e.collective_compute(
            "AllGather", ALU.bypass, replica_groups=[list(range(NCORE))],
            ins=[cc_car_in.ap()], outs=[cc_car_out.ap()]), inc="cc", n=1)
        c.wait("pool", "cc", c.cnt["cc"])
        c.dma("pool", "g_cc", lambda e: e.dma_start(
            out=at_car_all[:], in_=cc_car_out.ap().rearrange("(r p) s -> p r s", p=128)))
        carv = c.cnt["g_cc"]
        c.dve(lambda e: e.memset(at_carry[:], 0.0), waits=[("g_cc", carv)])
        for r_ in range(NCORE):
            c.dve(lambda e, r_=r_: e.scalar_tensor_tensor(
                at_carry[:], at_car_all[:, r_, :], S["wvec"][:, r_:r_ + 1], at_carry[:],
                ALU.mult, ALU.add))
        card = c.cnt["dve"]
        for qi, on in enumerate(["cb", "ct", "zt"]):
            for rt in range(2):
                c.dve(lambda e, on=on, rt=rt, qi=qi: e.tensor_tensor_scan(
                    rm[on][:, rt, :], rm[on][:, rt, :], rm[on][:, rt, :],
                    at_carry[:, qi * 2 + rt:qi * 2 + rt + 1], ALU.add, ALU.bypass),
                    waits=[("dve", card)])
        wait_proj("dve", ["ab", "at"])
        for rt in range(2):
            c.dve(lambda e, rt=rt: e.tensor_mul(rm["ab"][:, rt, :], rm["ab"][:, rt, :],
                                                at_rcf[:]), waits=[("act", bc_ev)])
            c.dve(lambda e, rt=rt: e.tensor_mul(rm["ab"][:, rt, :], rm["ab"][:, rt, :],
                                                rm["cb"][:, rt, :]))
            c.dve(lambda e, rt=rt: e.tensor_mul(rm["at"][:, rt, :], rm["at"][:, rt, :],
                                                at_rcaf[:]))
            c.dve(lambda e, rt=rt: e.tensor_mul(rm["at"][:, rt, :], rm["at"][:, rt, :],
                                                rm["zt"][:, rt, :]))
            c.dve(lambda e, rt=rt: e.tensor_mul(rm["at"][:, rt, :], rm["at"][:, rt, :],
                                                rm["ct"][:, rt, :]))
        intd = c.cnt["dve"]
        for i in range(NT):
            par = i % 2
            po = ps_c if par == 0 else ps_d
            xt = S["xtile0"] if par == 0 else S["xtile1"]
            c.wait("sync", "dve", st["x2"].get(i - 2, 0))
            c.dma("sync", f"s_x{par}", lambda e, xt=xt, i=i: e.dma_start(
                out=xt[:], in_=x_res.ap()[i * 128:(i + 1) * 128, :]))
            xv = c.cnt[f"s_x{par}"]
            c.wait("pe", "dve", intd)
            c.wait("pe", "dve", st["x2"].get(i - 2, 0))
            first = True
            for srcn, fold in [("ab", "Fb"), ("at", "Ft")]:
                for rt in range(2):
                    for nh in range(2):
                        c.on("pe", lambda e, srcn=srcn, fold=fold, rt=rt, nh=nh, i=i,
                             po=po, first=first: e.matmul(
                            po[:, nh * 512:(nh + 1) * 512],
                            rm[srcn][:, rt, i * 128:(i + 1) * 128],
                            S[fold][:, rt, nh * 512:(nh + 1) * 512],
                            start=first, stop=False))
                    first = False
            for nh in range(2):
                c.on("pe", lambda e, nh=nh, po=po: e.matmul(
                    po[:, nh * 512:(nh + 1) * 512], S["ones_1x128_bf16"][:],
                    S["bprojT"][:, nh * 512:(nh + 1) * 512], start=False, stop=True),
                    inc=("pe" if nh == 1 else None))
            mv = c.cnt["pe"]
            for nh in range(2):
                st["x2"][i] = c.dve(lambda e, xt=xt, nh=nh, po=po: e.tensor_add(
                    xt[:, nh * 512:(nh + 1) * 512], xt[:, nh * 512:(nh + 1) * 512],
                    po[:, nh * 512:(nh + 1) * 512]),
                    waits=([("pe", mv), (f"s_x{par}", xv)] if nh == 0 else ()))
            c.wait("sync", "dve", st["x2"][i])
            c.dma("sync", "s_st", lambda e, xt=xt, i=i: e.dma_start(
                out=y_out.ap()[i * 128:(i + 1) * 128, :], in_=xt[:]))
        return c.cnt["s_st"]

    st1 = {"wfree": {}, "sa0": 0}
    ffn_phase1(st1, x_res, [("g_pr", GPR), ("s_sml", SML)])
    ffn_phase2(st1)
    fold_act = emit_folds()
    fold_fw = c.cnt["g_xt"]
    if phases == "f1":
        c.wait("sync", "dve", c.cnt["dve"])
        c.dma("sync", "s_st", lambda e: e.dma_start(out=y_out.ap(), in_=x_res.ap()))
        c.wait("sync", "s_st", c.cnt["s_st"])
        c.emit(); es.close(); return nc
    ffn_phase3(st1, x_res, [("g_pr", GPR)], entry_act=fold_act, first_call=True)
    st1["f1_sa"] = c.cnt["g_sa"]
    if phases == "f1x":
        c.wait("sync", "g_sa", c.cnt["g_sa"])
        c.dma("sync", "s_st", lambda e: e.dma_start(out=y_out.ap(), in_=x_res.ap()))
        c.wait("sync", "s_st", c.cnt["s_st"])
        c.emit(); es.close(); return nc

    c.xbar(extra=[("g_sa", c.cnt["g_sa"]), ("s_cr0", c.cnt["s_cr0"]),
                  ("s_cr1", c.cnt["s_cr1"])])
    sta = {"f1_sa": st1["f1_sa"]}
    wb_done = emit_attn(sta, fold_act, fold_fw)
    if phases == "f1a":
        c.wait("sync", "s_st", c.cnt["s_st"])
        c.wait("sync", "g_sa", c.cnt["g_sa"])
        c.emit(); es.close(); return nc

    clear2 = [(f"g_sc{e_i}", c.cnt[f"g_sc{e_i}"]) for e_i in range(E)]
    c.xbar(extra=clear2 + [("g_sa", c.cnt["g_sa"]), ("s_st", wb_done),
                           ("s_cr0", c.cnt["s_cr0"]), ("s_cr1", c.cnt["s_cr1"])])
    st2 = {"wfree": dict(st1["wfree"]), "sa0": c.cnt["g_sa"]}
    ffn_phase1(st2, y_out, [("s_st", wb_done)])
    ffn_phase2(st2)
    ffn_phase3(st2, y_out, [("s_st", wb_done)], entry_act=0, first_call=False)

    c.wait("sync", "g_sa", c.cnt["g_sa"])
    c.wait("sync", "s_st", c.cnt["s_st"])
    c.emit()
    es.close()
    return nc


def _make_consts(TCORE, core):
    NT = TCORE // 128
    n, q = core // 4, core % 4
    cons = {}
    cons["ident_f32"] = np.eye(128, dtype=np.float32)
    bd = np.zeros((128, 128), np.float32)
    for g in range(16):
        bd[g * 8:(g + 1) * 8, g * 8:(g + 1) * 8] = 1.0
    cons["blockdiag8"] = bd
    cons["ones_1x128_f32"] = np.ones((1, 128), np.float32)
    cons["ones_1x128_bf16"] = np.ones((1, 128), np.float32).astype(ml_dtypes.bfloat16)
    cons["iota_em"] = np.broadcast_to(
        np.arange(TCORE, dtype=np.float32)[None, :], (E, TCORE)).copy()
    cons["tile_em"] = np.broadcast_to(
        (np.arange(TCORE) // 128).astype(np.float32)[None, :], (E, TCORE)).copy()
    t = np.arange(TCORE)
    thitlo = np.stack([t // 64, t % 64], -1).reshape(NT, 128, 2).transpose(1, 0, 2)
    cons["thitlo"] = np.ascontiguousarray(thitlo.astype(np.float32)).astype(ml_dtypes.bfloat16)
    gidx = q * TCORE + t
    cons["invc_row"] = (1.0 / (gidx + 1).astype(np.float64)).astype(np.float32)[None, :]
    sel = np.zeros((128, E), np.float32)
    for e_i in range(E):
        sel[64 * n + 8 * e_i, e_i] = 1.0
    cons["selmat"] = sel
    wv = np.zeros((128, E), np.float32)
    for r_ in range(4 * n, 4 * n + q):
        wv[:, r_] = 1.0
    cons["wvec"] = wv
    return cons


def _shared_weights(Wq, Wk, Wo, U_b, V_b, W_b, bias_b, U_t, V_t, W_t, X_t, bias_t,
                    alpha, gate_w, w1, w2):
    return {
        "wq": np.ascontiguousarray(Wq, np.float32),
        "wk": np.ascontiguousarray(Wk, np.float32),
        "vb": np.ascontiguousarray(V_b, np.float32),
        "vt": np.ascontiguousarray(V_t, np.float32),
        "wb": np.ascontiguousarray(W_b, np.float32),
        "wt": np.ascontiguousarray(W_t, np.float32),
        "xt_w": np.ascontiguousarray(X_t, np.float32),
        "ub": np.ascontiguousarray(U_b, np.float32),
        "ut": np.ascontiguousarray(U_t, np.float32),
        "woT": np.ascontiguousarray(np.asarray(Wo, np.float32).T),
        "bias_b": np.asarray(bias_b, np.float32).reshape(1, D),
        "bias_t": np.asarray(bias_t, np.float32).reshape(1, D),
        "alpha": np.asarray(alpha, np.float32).reshape(1, 1),
        "gate_wT": np.ascontiguousarray(np.asarray(gate_w, np.float32).T),
        "w1": np.ascontiguousarray(
            np.asarray(w1, np.float32).reshape(8, 8, 128, 8, 512)
            .transpose(0, 3, 2, 1, 4)),
        "w2": np.ascontiguousarray(
            np.asarray(w2, np.float32).reshape(8, 32, 128, 8, 128)
            .transpose(0, 3, 2, 1, 4)),
    }


def run_cores(x, shared, TCORE, phases="all", sim=False, use_mirror=False):
    nc = build_nc(TCORE=TCORE, phases=phases, use_mirror=use_mirror)
    xf = np.ascontiguousarray(x.reshape(NCORE, TCORE, D))
    in_maps = []
    for core in range(NCORE):
        m = dict(shared)
        m["xin"] = np.ascontiguousarray(xf[core])
        m.update(_make_consts(TCORE, core))
        in_maps.append(m)
    if sim:
        from concourse import bass_interp
        s = bass_interp.MultiCoreSim(nc, NCORE)
        for core in range(NCORE):
            for k, v in in_maps[core].items():
                s.cores[core].tensor(k)[:] = v
        s.simulate()
        outs = [np.array(s.cores[core].mem_tensor("y_out")) for core in range(NCORE)]
    else:
        res = run_bass_kernel_spmd(nc, in_maps, core_ids=list(range(NCORE)))
        outs = [res.results[core]["y_out"] for core in range(NCORE)]
    return np.stack(outs, 0)


def kernel(L0_attn_x, L0_attn_Wq, L0_attn_Wk, L0_attn_Wo, L0_attn_U_b, L0_attn_V_b,
           L0_attn_W_b, L0_attn_bias_b, L0_attn_U_t, L0_attn_V_t, L0_attn_W_t,
           L0_attn_X_t, L0_attn_bias_t, L0_attn_alpha, L0_ffn_gate_w, L0_ffn_w1,
           L0_ffn_w2):
    x = np.ascontiguousarray(np.asarray(L0_attn_x, dtype=np.float32))
    B, T, _ = x.shape
    TCORE = (B * T) // NCORE
    shared = _shared_weights(L0_attn_Wq, L0_attn_Wk, L0_attn_Wo, L0_attn_U_b,
                             L0_attn_V_b, L0_attn_W_b, L0_attn_bias_b, L0_attn_U_t,
                             L0_attn_V_t, L0_attn_W_t, L0_attn_X_t, L0_attn_bias_t,
                             L0_attn_alpha, L0_ffn_gate_w, L0_ffn_w1, L0_ffn_w2)
    out = run_cores(x, shared, TCORE, phases="all", sim=False)
    return out.reshape(B, T, D)


# version 21
# speedup vs baseline: 1.0088x; 1.0088x over previous
import numpy as np
import ml_dtypes
import concourse.bass as bass
import concourse.mybir as mybir
from concourse.bass import IndirectOffsetOnAxis
from concourse.bass_utils import run_bass_kernel_spmd

F32 = mybir.dt.float32
BF16 = mybir.dt.bfloat16
I32 = mybir.dt.int32
AF = mybir.ActivationFunctionType
ALU = mybir.AluOpType
AXX = mybir.AxisListType.X

NCORE = 8
D = 1024
R = 256
E = 8
F = 4096
EPS = 1e-5
NITER = 45
NKD = D // 128
NKF = F // 128
ENGS = ("pe", "dve", "act", "pool", "sync")


class Ctx:

    def __init__(self, nc, es):
        self.nc = nc
        self.es = es
        self.ops = {e: [] for e in ENGS}
        self.sem = {}
        self.cnt = {}

    def new_sem(self, name):
        self.sem[name] = self.es.enter_context(self.nc.semaphore(name))
        self.cnt[name] = 0

    def on(self, eng, fn, inc=None, n=1):
        if inc is not None:
            self.cnt[inc] += n
            sem = self.sem[inc]

            def run(e, fn=fn, sem=sem, n=n):
                fn(e).then_inc(sem, n)
            self.ops[eng].append(run)
            return self.cnt[inc]
        self.ops[eng].append(lambda e, fn=fn: fn(e))
        return None

    def dma(self, eng, sem_name, fn):
        return self.on(eng, fn, inc=sem_name, n=16)

    def wait(self, eng, sem_name, val):
        if val is None or val <= 0:
            return
        sem = self.sem[sem_name]
        self.ops[eng].append(lambda e, sem=sem, val=val: e.wait_ge(sem, val))

    def dve(self, fn, waits=()):
        for s, v in waits:
            self.wait("dve", s, v)
        self.wait("dve", "dve", self.cnt["dve"])
        return self.on("dve", fn, inc="dve")

    def act(self, fn, waits=()):
        for s, v in waits:
            self.wait("act", s, v)
        self.wait("act", "act", self.cnt["act"])
        return self.on("act", fn, inc="act")

    def xbar(self, extra=()):
        tots = [(s, v) for s, v in self.cnt.items() if v > 0]
        tots += list(extra)
        for eng in ENGS:
            for s, v in tots:
                if eng == {"pe": "pe", "dve": "dve", "act": "act"}.get(s, None):
                    continue
                self.wait(eng, s, v)

    def emit(self):
        blk = self.es.enter_context(self.nc.Block())
        blk.tensor(lambda e: [f(e) for f in self.ops["pe"]])
        blk.vector(lambda e: [f(e) for f in self.ops["dve"]])
        blk.scalar(lambda e: [f(e) for f in self.ops["act"]])
        blk.gpsimd(lambda e: [f(e) for f in self.ops["pool"]])
        blk.sync(lambda e: [f(e) for f in self.ops["sync"]])


def build_nc(TCORE=2048, phases="all", use_mirror=False):
    from contextlib import ExitStack
    NT = TCORE // 128
    TGLOB = 4 * TCORE
    KSEL = TGLOB // 4
    SEGW = TGLOB // 8
    CAP = ((KSEL // 4) * 5 // 4 + 127) // 128 * 128
    CB = CAP // 128
    PW = 1040
    FC1 = 512
    NFC1 = F // FC1

    ACH = min(512, TCORE)
    NCH_A = TCORE // ACH
    CGRP = []
    b0 = 0
    while b0 < CB:
        nb = min(4, CB - b0)
        CGRP.append((b0, nb))
        b0 += nb

    nc = bass.Bass("TRN2", target_bir_lowering=False, debug=False)
    es = ExitStack()
    c = Ctx(nc, es)

    dram = {}

    def din(name, shape, dt=F32):
        dram[name] = nc.dram_tensor(name, list(shape), dt, kind="ExternalInput")

    din("xin", [TCORE, D])
    for nm in ["wq", "wk"]:
        din(nm, [D, D])
    for nm in ["vb", "vt", "wb", "wt", "xt_w", "ub", "ut"]:
        din(nm, [D, R])
    din("woT", [D, D])
    din("bias_b", [1, D]); din("bias_t", [1, D]); din("alpha", [1, 1])
    din("gate_wT", [D, E])
    din("w1", [E, NFC1, 128, NKD, FC1]); din("w2", [E, NKD, 128, NKF, 128])
    din("ident_f32", [128, 128])
    din("blockdiag8", [128, 128])
    din("ones_1x128_f32", [1, 128])
    din("ones_1x128_bf16", [1, 128], BF16)
    din("iota_em", [E, TCORE])
    din("tile_em", [E, TCORE])
    din("thitlo", [128, NT, 2], BF16)
    din("invc_row", [1, TCORE])
    din("selmat", [128, E])
    din("wvec", [128, E])

    y_out = nc.dram_tensor("y_out", [TCORE, D], F32, kind="ExternalOutput")
    w1m = nc.dram_tensor("w1m", [E, NFC1, 128, NKD, FC1], BF16)
    w2m = nc.dram_tensor("w2m", [E, NKD, 128, NKF, 128], BF16)

    x_res = nc.dram_tensor("x_res", [TCORE, D], F32)
    compA = [nc.dram_tensor(f"compA{e_i}", [CAP + NT, PW], BF16) for e_i in range(E)]
    cc_prob_in = nc.dram_tensor("cc_prob_in", [E, TCORE], F32)
    cc_prob_out = nc.dram_tensor("cc_prob_out", [NCORE * E, TCORE], F32)
    cc_car_in = nc.dram_tensor("cc_car_in", [128, 8], F32)
    cc_car_out = nc.dram_tensor("cc_car_out", [NCORE * 128, 8], F32)

    for s in ["pe", "dve", "act", "cc",
              "s_sml", "s_x0", "s_x1", "s_tp0", "s_tp1", "s_cr0", "s_cr1",
              "s_st", "s_pr", "s_bis", "s_m10", "s_m11", "s_m20", "s_m21", "s_w0", "s_w1",
              "a_tp00", "a_tp01", "a_tp10", "a_tp11",
              "g_pr", "g_gt", "g_w0", "g_w1", "g_w2", "g_sa", "g_cc",
              "g_fa0", "g_fa1", "g_xt"]:
        c.new_sem(s)
    for e_i in range(E):
        c.new_sem(f"g_sc{e_i}")

    S = {}

    def sb(name, shape, dt=F32):
        S[name] = es.enter_context(nc.sbuf_tensor("sb_" + name, list(shape), dt))
        return S[name]

    sb("ident_f32", [128, 128])
    sb("blockdiag8", [128, 128])
    sb("ones_1x128_f32", [1, 128]); sb("ones_1x128_bf16", [1, 128], BF16)
    sb("gate_wT", [128, NKD, E], BF16)
    sb("selmat", [128, E]); sb("wvec", [128, E])
    sb("thitlo", [128, NT, 2], BF16)
    sb("alpha", [1, 1])
    sb("rstd_all", [128, NT])
    sb("probs_tm", [128, NT, E])
    sb("slots_i32", [128, NT, E], I32)
    sb("th_tm", [128, E]); sb("th_em", [E, 1]); sb("th_row", [1, E])
    sb("xtile0", [128, D]); sb("xtile1", [128, D])
    sb("stat", [128, 24])
    sb("et0", [128, E]); sb("et1", [128, E])
    sb("gm_s", [128, E])
    sb("lo", [128, 1]); sb("hi", [128, 1]); sb("th", [128, 1])
    sb("cnt_s", [128, 1]); sb("ge", [128, 1]); sb("d1", [128, 1]); sb("d2c", [128, 1])
    sb("Ab", [128, NKD, R], BF16); sb("At", [128, NKD, R], BF16)
    sb("Cb", [128, NKD, R], BF16); sb("Ct", [128, NKD, R], BF16)
    sb("Xtw", [128, NKD, R], BF16)
    sb("Fb", [128, 2, D], BF16); sb("Ft", [128, 2, D], BF16)
    sb("bvfm", [128, NKD])
    sb("bvfmb", [128, NKD], BF16)
    sb("bprojT", [1, D], BF16)
    sb("g_col0", [128, CB]); sb("g_col1", [128, CB])
    sb("idx0", [128, CB], I32); sb("idx1", [128, CB], I32)
    sb("trow0", [128, CB]); sb("trow1", [128, CB])
    payload = sb("payload", [128, NT, PW], BF16)
    xnfmt = [sb("xnfmt0", [128, NKD, 128], BF16), sb("xnfmt1", [128, NKD, 128], BF16)]
    fwa = [sb("fwa0", [128, 1024], BF16), sb("fwa1", [128, 1024], BF16)]
    fvb = [sb("fvb0", [128, 256], BF16), sb("fvb1", [128, 256], BF16)]
    sb("alphab", [128, 1])
    at_car_loc = sb("at_car_loc", [128, 8])
    at_car_all = sb("at_car_all", [128, NCORE, 8])
    at_carry = sb("at_carry", [128, 8])

    def ov_plan(items):
        offs, cur = {}, 0
        for name, shape, dt in items:
            nbytes = int(np.prod(shape[1:])) * (2 if dt in (BF16, mybir.dt.float16) else 4)
            nbytes = (nbytes + 31) // 32 * 32
            offs[name] = cur
            cur += nbytes
        return cur, offs

    o1a = [("probs_em", [E, TCORE], F32), ("em_incl", [E, TCORE], F32),
           ("em_ssel", [E, TCORE], F32), ("em_d2", [E, TCORE], F32),
           ("em_mask", [E, TCORE], BF16), ("em_neg", [E, TCORE], BF16),
           ("probbis", [128, SEGW], F32), ("iota_em", [E, TCORE], F32),
           ("tile_em", [E, TCORE], F32), ("maskjunk", [128, SEGW], BF16)]
    o1b = [("h", [128, NKF, CAP], BF16)]
    o2a = [("cbuf0", [128, CB, PW], BF16),
           ("xinfm0", [128, CB, NKD, 128], BF16),
           ("w1c0", [128, NKD, FC1], BF16), ("w1c1", [128, NKD, FC1], BF16),
           ("w2c0", [128, NKF, 128], BF16), ("w2c1", [128, NKF, 128], BF16),
           ("yfm0", [128, CB, NKD, 128], BF16), ("yfm1", [128, CB, NKD, 128], BF16),
           ("ytm0", [128, NKD, 128], BF16), ("ytm1", [128, NKD, 128], BF16),
           ("yf0", [128, D], F32), ("yf1", [128, D], F32)]
    ACHW = min(512, TCORE)
    o2b = [("at_xnfmc0", [128, NKD, ACHW], BF16), ("at_xnfmc1", [128, NKD, ACHW], BF16),
           ("rm_ab", [128, 2, TCORE], BF16), ("rm_at", [128, 2, TCORE], BF16),
           ("rm_cb", [128, 2, TCORE], BF16), ("rm_ct", [128, 2, TCORE], BF16),
           ("rm_zt", [128, 2, TCORE], BF16),
           ("at_rcf", [128, TCORE], BF16), ("at_rcaf", [128, TCORE], BF16),
           ("at_xntm0", [128, D], BF16), ("at_xntm1", [128, D], BF16),
           ("invc_row", [1, TCORE], F32)]
    p3 = o1b + o2a
    sz_p2, off_p2 = ov_plan(o1a)
    sz_p3, off_p3 = ov_plan(p3)
    sz_at, off_at = ov_plan(o2b)
    REG = max(sz_p2, sz_p3, sz_at)
    top = nc.sbuf_top
    reg_base = (top - REG) // 32 * 32
    assert reg_base >= nc.sbuf_base, (
        f"SBUF overlay overflow: reg_base={reg_base} auto_base={nc.sbuf_base} "
        f"sizes p2={sz_p2} p3={sz_p3} at={sz_at}")
    for items, offs in [(o1a, off_p2), (p3, off_p3), (o2b, off_at)]:
        for name, shape, dt in items:
            S[name] = nc.alloc_sbuf_tensor_at("ov_" + name, list(shape), dt,
                                              offset=reg_base + offs[name])
    probs_em = S["probs_em"]
    probbis = S["probbis"]
    em = {"mask": S["em_mask"], "incl": S["em_incl"], "ssel": S["em_ssel"],
          "d2": S["em_d2"], "neg": S["em_neg"], "slotf": S["em_ssel"]}
    cbuf = [S["cbuf0"], S["cbuf0"]]
    xinfm = [S["xinfm0"], S["xinfm0"]]
    h = S["h"]
    w1c = [S["w1c0"], S["w1c1"]]
    w2c = [S["w2c0"], S["w2c1"]]
    yfm = [S["yfm0"], S["yfm1"]]
    ytm = [S["ytm0"], S["ytm1"]]
    yf32 = [S["yf0"], S["yf1"]]
    at_xnfmc = [S["at_xnfmc0"], S["at_xnfmc1"]]
    rm = {n: S["rm_" + n] for n in ["ab", "at", "cb", "ct", "zt"]}
    at_rcf = S["at_rcf"]
    at_rcaf = S["at_rcaf"]
    at_xntm = [S["at_xntm0"], S["at_xntm1"]]

    ps_a = es.enter_context(nc.psum_tensor("ps_a", [128, 1024], F32))
    ps_b = es.enter_context(nc.psum_tensor("ps_b", [128, 1024], F32))
    ps_c = es.enter_context(nc.psum_tensor("ps_c", [128, 1024], F32))
    ps_d = es.enter_context(nc.psum_tensor("ps_d", [128, 1024], F32))
    ps_sc = ps_c
    ps_tr = ps_d
    psbufs = [ps_a, ps_b, ps_c, ps_d]

    for sbn in ["ident_f32", "blockdiag8", "ones_1x128_f32", "ones_1x128_bf16",
                "selmat", "wvec", "thitlo", "alpha"]:
        c.dma("sync", "s_sml", lambda e, sbn=sbn: e.dma_start(
            out=S[sbn][:], in_=dram[sbn].ap()))
    SML = c.cnt["s_sml"]
    c.dma("pool", "g_gt", lambda e: e.dma_start(
        out=S["gate_wT"][:], in_=dram["gate_wT"].ap().rearrange("(k p) e -> p k e", p=128)))
    c.dma("pool", "g_pr", lambda e: e.dma_start(out=x_res.ap(), in_=dram["xin"].ap()))
    GPR = c.cnt["g_pr"]

    c.dve(lambda e: e.memset(payload[:, :, D + E + 2:PW], 0.0))

    def ln_tile(st, i, xt, sq, sc, sx, xv, out_ap_fn, extra_waits=()):
        del sq
        """Emit LN for tile i: stats + normalized bf16 tile via out_ap_fn."""
        c.dve(lambda e, xt=xt, sc=sc: e.reduce_sum(S["stat"][:, sc:sc + 1], xt[:], axis=AXX),
              waits=[(sx, xv)])
        c.dve(lambda e, sc=sc: e.tensor_scalar_mul(S["stat"][:, sc + 1:sc + 2],
                                                   S["stat"][:, sc:sc + 1], 1.0 / D))
        c.dve(lambda e, xt=xt, sc=sc: e.scalar_tensor_tensor(
            out_ap_fn(), xt[:], 1.0, xt[:], ALU.mult, ALU.mult,
            accum_out=S["stat"][:, sc + 2:sc + 3]), waits=extra_waits)
        c.dve(lambda e, sc=sc: e.tensor_scalar_mul(S["stat"][:, sc + 3:sc + 4],
                                                   S["stat"][:, sc + 2:sc + 3], 1.0 / D))
        c.dve(lambda e, sc=sc: e.tensor_scalar(
            S["stat"][:, sc + 4:sc + 5], S["stat"][:, sc + 1:sc + 2],
            S["stat"][:, sc + 1:sc + 2], None, ALU.mult))
        c.dve(lambda e, sc=sc: e.tensor_sub(S["stat"][:, sc + 5:sc + 6],
                                            S["stat"][:, sc + 3:sc + 4],
                                            S["stat"][:, sc + 4:sc + 5]))
        c.dve(lambda e, sc=sc: e.tensor_scalar_add(S["stat"][:, sc + 6:sc + 7],
                                                   S["stat"][:, sc + 5:sc + 6], EPS))
        rv = c.dve(lambda e, sc=sc: e.reciprocal(S["stat"][:, sc + 7:sc + 8],
                                                 S["stat"][:, sc + 6:sc + 7]))
        rs = c.act(lambda e, i=i, sc=sc: e.activation(
            out=S["rstd_all"][:, i:i + 1], in_=S["stat"][:, sc + 7:sc + 8], func=AF.Sqrt),
            waits=[("dve", rv)])
        nm = c.dve(lambda e, i=i, sc=sc: e.tensor_scalar_mul(
            S["stat"][:, sc + 8:sc + 9], S["stat"][:, sc + 1:sc + 2], -1.0),
            waits=[("act", rs)])
        return c.dve(lambda e, xt=xt, i=i, sc=sc: e.scalar_tensor_tensor(
            out_ap_fn(), xt[:], S["stat"][:, sc + 8:sc + 9],
            S["rstd_all"][:, i:i + 1].to_broadcast([128, D]), ALU.add, ALU.mult),
            waits=extra_waits)

    def ffn_phase1(st, base, base_ready):
        st["xn"] = {}; st["sc"] = {}; st["sm"] = {}; st["pr"] = {}; st["em"] = {}
        for i in range(NT):
            par = i % 2
            xt = S["xtile0"] if par == 0 else S["xtile1"]
            sq = None
            et = S["et0"] if par == 0 else S["et1"]
            sx = f"s_x{par}"
            sc = 12 * par
            if i >= 2:
                c.wait("sync", "dve", st["xn"][i - 2])
            for s_, v_ in base_ready:
                c.wait("sync", s_, v_)
            c.dma("sync", sx, lambda e, xt=xt, i=i: e.dma_start(
                out=xt[:], in_=base.ap()[i * 128:(i + 1) * 128, :]))
            xv = c.cnt[sx]
            if i >= 2:
                c.wait("dve", "act", st["sm"][i - 2])
            st["xn"][i] = ln_tile(st, i, xt, sq, sc, sx, xv,
                                  (lambda i=i: payload[:, i, 0:D]))
            stp = f"s_tp{par}"
            if i >= 2:
                c.wait("sync", "pe", st["sc"][i - 2])
            c.wait("sync", "dve", st["xn"][i])
            c.dma("sync", stp, lambda e, par=par, i=i: e.dma_start(
                out=xnfmt[par][:], in_=payload[:, i, 0:D], transpose=True))
            tpv = c.cnt[stp]
            c.wait("pe", stp, tpv)
            c.wait("pe", "g_gt", 16)
            if i >= 1:
                c.wait("pe", "act", st["sm"][i - 1])
            for kc in range(NKD):
                st["sc"][i] = c.on("pe", lambda e, par=par, kc=kc: e.matmul(
                    ps_sc[:, 0:E], xnfmt[par][:, kc, :], S["gate_wT"][:, kc, :],
                    start=(kc == 0), stop=(kc == NKD - 1)),
                    inc=("pe" if kc == NKD - 1 else None))
            c.dve(lambda e, sc=sc: e.reduce_max(S["stat"][:, sc + 9:sc + 10],
                                                ps_sc[:, 0:E], axis=AXX),
                  waits=[("pe", st["sc"][i])])
            nx = c.dve(lambda e, sc=sc: e.tensor_scalar_mul(
                S["stat"][:, sc + 10:sc + 11], S["stat"][:, sc + 9:sc + 10], -1.0))
            st["sm"][i] = c.act(lambda e, et=et, sc=sc: e.activation(
                out=et[:], in_=ps_sc[:, 0:E], func=AF.Exp,
                bias=S["stat"][:, sc + 10:sc + 11], scale=1.0,
                accum_out=S["stat"][:, sc + 11:sc + 12]), waits=[("dve", nx)])
            c.dve(lambda e, sc=sc: e.reciprocal(S["stat"][:, sc + 9:sc + 10],
                                                S["stat"][:, sc + 11:sc + 12]),
                  waits=[("act", st["sm"][i])])
            st["pr"][i] = c.dve(lambda e, et=et, i=i, sc=sc: e.tensor_scalar_mul(
                S["probs_tm"][:, i, :], et[:], S["stat"][:, sc + 9:sc + 10]))
            c.wait("pe", "dve", st["pr"][i])
            if i >= 1:
                c.wait("pe", "dve", st["em"][i - 1])
            tv = c.on("pe", lambda e, i=i: e.transpose(
                ps_tr[0:E, 0:128], S["probs_tm"][:, i, :], S["ident_f32"][:]), inc="pe")
            st["em"][i] = c.dve(lambda e, i=i: e.tensor_copy(
                probs_em[:, i * 128:(i + 1) * 128], ps_tr[0:E, 0:128]),
                waits=[("pe", tv)])

    def ffn_phase2(st):
        st["sl"] = {}; st["gm"] = {}
        for nm in ["iota_em", "tile_em"]:
            c.wait("sync", "s_pr", c.cnt["s_pr"])
            c.dma("sync", "s_pr", lambda e, nm=nm: e.dma_start(
                out=S[nm][:], in_=dram[nm].ap()))
        iotld = c.cnt["s_pr"]
        c.wait("dve", "s_pr", iotld)
        c.wait("sync", "dve", st["em"][NT - 1])
        c.wait("sync", "s_pr", c.cnt["s_pr"])
        c.dma("sync", "s_pr", lambda e: e.dma_start(out=cc_prob_in.ap(), in_=probs_em[:]))
        c.wait("pool", "s_pr", c.cnt["s_pr"])
        c.on("pool", lambda e: e.collective_compute(
            "AllGather", ALU.bypass, replica_groups=[list(range(NCORE))],
            ins=[cc_prob_in.ap()], outs=[cc_prob_out.ap()]), inc="cc", n=1)
        c.wait("pool", "cc", c.cnt["cc"])
        c.wait("sync", "cc", c.cnt["cc"])
        for r_ in range(NCORE):
            n_, q_ = r_ // 4, r_ % 4
            for e_i in range(E):
                p0 = 64 * n_ + 8 * e_i + 2 * q_

                def mk(e, r_=r_, e_i=e_i, p0=p0):
                    src = cc_prob_out.ap()[r_ * E + e_i:r_ * E + e_i + 1, :].rearrange(
                        "one (s w) -> (one s) w", s=2)
                    return e.dma_start(out=probbis[p0:p0 + 2, :], in_=src)
                c.dma("sync", "s_bis", mk)
        gccv = c.cnt["s_bis"]
        c.dve(lambda e: e.memset(S["lo"][:], 0.0), waits=[("s_bis", gccv)])
        c.dve(lambda e: e.memset(S["hi"][:], 1.0))
        c.dve(lambda e: e.memset(S["th"][:], 0.5))
        for _ in range(NITER):
            cv = c.dve(lambda e: e.tensor_scalar(
                S["maskjunk"][:], probbis[:], S["th"][:], None, ALU.is_ge,
                op1=ALU.add, accum_out=S["cnt_s"][:]))
            c.wait("pe", "dve", cv)
            pv = c.on("pe", lambda e: e.matmul(
                ps_tr[:, 256:257], S["blockdiag8"][:], S["cnt_s"][:],
                start=True, stop=True), inc="pe")
            c.dve(lambda e: e.tensor_scalar(
                S["ge"][:], ps_tr[:, 256:257], float(KSEL), None, ALU.is_ge),
                waits=[("pe", pv)])
            c.dve(lambda e: e.tensor_sub(S["d1"][:], S["th"][:], S["lo"][:]))
            c.dve(lambda e: e.scalar_tensor_tensor(
                S["lo"][:], S["ge"][:], S["d1"][:], S["lo"][:], ALU.mult, ALU.add))
            c.dve(lambda e: e.tensor_sub(S["d1"][:], S["hi"][:], S["th"][:]))
            c.dve(lambda e: e.scalar_tensor_tensor(
                S["d2c"][:], S["ge"][:], S["d1"][:], S["th"][:], ALU.mult, ALU.add))
            c.dve(lambda e: e.tensor_copy(S["hi"][:], S["d2c"][:]))
            c.dve(lambda e: e.tensor_add(S["d1"][:], S["lo"][:], S["hi"][:]))
            c.dve(lambda e: e.tensor_scalar_mul(S["th"][:], S["d1"][:], 0.5))
        bis = c.cnt["dve"]
        c.wait("pe", "dve", bis)
        pv = c.on("pe", lambda e: e.matmul(ps_tr[0:E, 258:259], S["selmat"][:], S["lo"][:],
                                           start=True, stop=True), inc="pe")
        v = c.dve(lambda e: e.tensor_copy(S["th_em"][:], ps_tr[0:E, 258:259]),
                  waits=[("pe", pv)])
        c.wait("pe", "dve", v)
        pv = c.on("pe", lambda e: e.transpose(ps_tr[0:1, 260:260 + E], S["th_em"][:],
                                              S["ident_f32"][0:E, 0:E]), inc="pe")
        v = c.dve(lambda e: e.tensor_copy(S["th_row"][:], ps_tr[0:1, 260:260 + E]),
                  waits=[("pe", pv)])
        c.wait("pe", "dve", v)
        pv = c.on("pe", lambda e: e.matmul(ps_tr[:, 272:272 + E], S["ones_1x128_f32"][:],
                                           S["th_row"][:], start=True, stop=True), inc="pe")
        thtm = c.dve(lambda e: e.tensor_copy(S["th_tm"][:], ps_tr[:, 272:272 + E]),
                     waits=[("pe", pv)])
        c.dve(lambda e: e.tensor_scalar(
            em["mask"][:], probs_em[:], S["th_em"][:], None, ALU.is_ge))
        c.dve(lambda e: e.tensor_tensor_scan(
            em["incl"][:], em["mask"][:], em["mask"][:], 0.0, ALU.add, ALU.bypass))
        c.dve(lambda e: e.tensor_sub(em["ssel"][:], em["incl"][:], em["mask"][:]))
        c.dve(lambda e: e.scalar_tensor_tensor(
            em["d2"][:], em["ssel"][:], float(CAP - 1), S["iota_em"][:],
            ALU.add, ALU.subtract))
        c.dve(lambda e: e.tensor_scalar(em["neg"][:], em["d2"][:], 0.0, None, ALU.is_lt))
        c.dve(lambda e: e.scalar_tensor_tensor(
            em["d2"][:], em["neg"][:], 65536.0, em["d2"][:], ALU.mult, ALU.add))
        c.dve(lambda e: e.tensor_scalar(
            em["neg"][:], em["d2"][:], em["incl"][:, TCORE - 1:TCORE], None, ALU.is_lt))
        c.dve(lambda e: e.scalar_tensor_tensor(
            em["d2"][:], em["neg"][:], 65536.0, em["d2"][:], ALU.mult, ALU.add))
        c.dve(lambda e: e.tensor_sub(em["slotf"][:], em["ssel"][:], em["d2"][:]))
        c.dve(lambda e: e.tensor_mul(em["slotf"][:], em["slotf"][:], em["mask"][:]))
        c.dve(lambda e: e.tensor_add(em["slotf"][:], em["slotf"][:], em["d2"][:]))
        c.dve(lambda e: e.tensor_scalar_min(em["slotf"][:], em["slotf"][:],
                                            float(CAP)))
        c.dve(lambda e: e.tensor_scalar(em["neg"][:], em["slotf"][:], float(CAP),
                                        None, ALU.is_equal))
        c.dve(lambda e: e.tensor_mul(em["d2"][:], em["neg"][:], S["tile_em"][:]))
        slv = c.dve(lambda e: e.tensor_add(em["slotf"][:], em["slotf"][:], em["d2"][:]))
        for i in range(NT):
            c.wait("pe", "dve", slv if i == 0 else st["sl"][i - 1])
            pv = c.on("pe", lambda e, i=i: e.transpose(
                ps_tr[:, 288:288 + E], em["slotf"][:, i * 128:(i + 1) * 128],
                S["ident_f32"][0:E, 0:E]), inc="pe")
            st["sl"][i] = c.dve(lambda e, i=i: e.tensor_copy(
                S["slots_i32"][:, i, :], ps_tr[:, 288:288 + E]), waits=[("pe", pv)])
            c.dve(lambda e, i=i: e.tensor_tensor(
                S["gm_s"][:], S["probs_tm"][:, i, :], S["th_tm"][:], ALU.is_ge),
                waits=[("dve", thtm)])
            st["gm"][i] = c.dve(lambda e, i=i: e.tensor_tensor(
                payload[:, i, D:D + E], S["gm_s"][:], S["probs_tm"][:, i, :], ALU.mult))
        st["tcol"] = c.dve(lambda e: e.tensor_copy(
            payload[:, :, D + E:D + E + 2], S["thitlo"][:]))

    def ffn_phase3(st, target, target_ready, entry_act, first_call):
        first_call = first_call or not use_mirror
        st["mm1"] = {}; st["gel"] = {}; st["oev"] = {}; st["ysc"] = {}; st["sad"] = {}
        st["idxd"] = {}; st["tpv"] = {}; st["ytv"] = {}; st["gv"] = {}
        st["mir"] = {}; st["mir2"] = {}
        nwc = NFC1 + NKD
        def emit_scatter_batch_step(e_i, i):
            sem = f"g_sc{e_i}"
            if True:
                c.wait("pool", "dve", max(st["gm"][i], st["sl"][i], st["tcol"],
                                          st["xn"][i]))
                for s_, v_ in target_ready:
                    c.wait("pool", s_, v_)
                c.wait("pool", sem, c.cnt[sem])

                def mk(e, e_i=e_i, i=i):
                    return e.indirect_dma_start(
                        out=compA[e_i].ap(),
                        out_offset=IndirectOffsetOnAxis(S["slots_i32"][:, i, e_i:e_i + 1], 0),
                        in_=payload[:, i, :], in_offset=None)
                c.dma("pool", sem, mk)

        for i in range(NT):
            for e_i in range(E):
                emit_scatter_batch_step(e_i, i)
        first_pe_wait = entry_act

        def make_y_bundles(e_i):
            par = e_i % 2
            g_colv = S["g_col0"] if par == 0 else S["g_col1"]
            idxv = S["idx0"] if par == 0 else S["idx1"]
            oevall = st["oev"][(e_i, NKD - 1)]
            gv = st["gv"][e_i]
            yfme = yfm[par]
            g0 = c.cnt["g_sa"]
            for b in range(CB):
                st["sad"][(e_i, b)] = g0 + 16 * (b + 1)

            def chain(b):
                c.wait("act", "act", oevall)
                if b >= 2:
                    c.wait("act", "dve", st["ysc"][(e_i, b - 2)])
                elif e_i > 0:
                    c.wait("act", "dve", st["ysc"][(e_i - 1, CB - 1 - b)])
                tps = f"a_tp{par}{b % 2}"
                c.dma("act", tps, lambda e, b=b, yfme=yfme: e.dma_start(
                    out=ytm[b % 2][:], in_=yfme[:, b, :, :], transpose=True))
                ytv = c.cnt[tps]
                yb = yf32[b % 2]
                ysc_waits = [(tps, ytv), ("dve", gv)]
                if b >= 2:
                    ysc_waits.append(("g_sa", st["sad"][(e_i, b - 2)]))
                elif e_i > 0:
                    ysc_waits.append(("g_sa", st["sad"][(e_i - 1, CB - 1 - b)]))
                else:
                    ysc_waits.append(("g_sa", st["sa0"]))
                st["ysc"][(e_i, b)] = c.dve(
                    lambda e, b=b, g_colv=g_colv, yb=yb: e.tensor_scalar_mul(
                        yb[:], ytm[b % 2][:].rearrange("p c t -> p (c t)"),
                        g_colv[:, b:b + 1]), waits=ysc_waits)
                if b == CB - 1:
                    st["ytv"][e_i] = (c.cnt[f"a_tp{par}0"], c.cnt[f"a_tp{par}1"])

            def sadd(b):
                yb = yf32[b % 2]
                c.wait("pool", "dve", max(st["ysc"][(e_i, b)], st["idxd"][e_i]))
                c.wait("pool", "g_sa", c.cnt["g_sa"])

                def mksa(e, b=b, idxv=idxv, yb=yb):
                    return e.indirect_dma_start(
                        out=target.ap(),
                        out_offset=IndirectOffsetOnAxis(idxv[:, b:b + 1], 0),
                        in_=yb[:], in_offset=None, compute_op=ALU.add)
                got = c.dma("pool", "g_sa", mksa)
                assert got == st["sad"][(e_i, b)], (got, st["sad"][(e_i, b)])
            return ([lambda b=b: chain(b) for b in range(CB)],
                    [lambda b=b: sadd(b) for b in range(CB)])

        for e_i in range(E):
            par = e_i % 2
            scr = f"s_cr{par}"
            g_colv = S["g_col0"] if par == 0 else S["g_col1"]
            idxv = S["idx0"] if par == 0 else S["idx1"]
            trowv = S["trow0"] if par == 0 else S["trow1"]
            c.wait("sync", f"g_sc{e_i}", c.cnt[f"g_sc{e_i}"])
            if e_i >= 1:
                c.wait("sync", "dve", st["idxd"][e_i - 1])
                c.wait("sync", f"s_tp{(e_i - 1) % 2}", st["tpv"][e_i - 1])
            c.dma("sync", scr, lambda e, par=par, e_i=e_i: e.dma_start(
                out=cbuf[par][:],
                in_=compA[e_i].ap()[0:CAP, :].rearrange("(b p) w -> p b w", p=128)))
            crv = c.cnt[scr]
            st["rb"] = (scr, crv)
            if e_i >= 1:
                c.wait("sync", "pe", st["mm1"][e_i - 1])
            for b in range(CB):
                c.wait("sync", scr, crv)
                c.dma("sync", f"s_tp{par}", lambda e, par=par, b=b: e.dma_start(
                    out=xinfm[par][:, b, :, :], in_=cbuf[par][:, b, 0:D], transpose=True))
            tpv = c.cnt[f"s_tp{par}"]
            st["tpv"][e_i] = tpv
            gw = [(scr, crv)]
            if e_i >= 2:
                gw.append(("g_sa", st["sad"][(e_i - 2, CB - 1)]))
            gv = c.dve(lambda e, par=par, g_colv=g_colv, e_i=e_i: e.tensor_scalar_mul(
                g_colv[:], cbuf[par][:, :, D + e_i], 0.5), waits=gw)
            st["gv"][e_i] = gv
            c.dve(lambda e, par=par, trowv=trowv: e.scalar_tensor_tensor(
                trowv[:], cbuf[par][:, :, D + E], 64.0, cbuf[par][:, :, D + E + 1],
                ALU.mult, ALU.add))
            st["idxd"][e_i] = c.dve(
                lambda e, idxv=idxv, trowv=trowv: e.tensor_copy(idxv[:], trowv[:]))
            if e_i >= 1:
                ych, pending_y = make_y_bundles(e_i - 1)
                for f in ych:
                    f()
            else:
                pending_y = []
            for fc in range(NFC1):
                slot = (e_i * nwc + fc) % 2
                wsem = f"g_w{slot}" if first_call else f"s_w{slot}"
                if first_call:
                    c.wait("pool", "pe", st["wfree"].get(slot, 0))
                    c.wait("pool", f"s_m1{slot}", st["mir"].get(slot, 0))
                    c.dma("pool", wsem, lambda e, e_i=e_i, fc=fc, slot=slot: e.dma_start(
                        out=w1c[slot][:], in_=dram["w1"].ap()[e_i][fc]))
                    wv = c.cnt[wsem]
                    if use_mirror:
                        c.wait("sync", wsem, wv)
                        c.dma("sync", f"s_m1{slot}", lambda e, e_i=e_i, fc=fc, slot=slot:
                              e.dma_start(out=w1m.ap()[e_i][fc], in_=w1c[slot][:]))
                        st["mir"][slot] = c.cnt[f"s_m1{slot}"]
                else:
                    c.wait("sync", "pe", st["wfree"].get(slot, 0))
                    c.dma("sync", wsem, lambda e, e_i=e_i, fc=fc, slot=slot: e.dma_start(
                        out=w1c[slot][:], in_=w1m.ap()[e_i][fc]))
                    wv = c.cnt[wsem]
                c.wait("pe", wsem, wv)
                if fc == 0:
                    c.wait("pe", f"s_tp{par}", tpv)
                    if first_pe_wait:
                        c.wait("pe", "act", first_pe_wait)
                        first_pe_wait = 0
                for ft in range(4):
                    fidx = fc * 4 + ft
                    hp = psbufs[fidx % 4]
                    if ft % 2 == 0:
                        if fidx >= 4:
                            c.wait("pe", "act", st["gel"][(e_i, fidx - 3)])
                        elif e_i >= 1:
                            c.wait("pe", "act",
                                   st["oev"][(e_i - 1, 5 if fidx == 0 else 7)])
                    for kc in range(NKD):
                        for gi, (gb, gn) in enumerate(CGRP):
                            last = (kc == NKD - 1) and (gi == len(CGRP) - 1)
                            c.on("pe", lambda e, slot=slot, kc=kc, ft=ft, hp=hp,
                                 par=par, gb=gb, gn=gn: e.matmul(
                                hp[:, gb * 128:(gb + gn) * 128],
                                w1c[slot][:, kc, ft * 128:(ft + 1) * 128],
                                xinfm[par][:, gb:gb + gn, kc, :],
                                start=(kc == 0), stop=(kc == NKD - 1)),
                                inc=("pe" if last else None))
                    if ft % 2 == 1:
                        mmv = c.cnt["pe"]
                        st["gel"][(e_i, fidx - 1)] = c.act(
                            lambda e, fidx=fidx, hq=psbufs[(fidx - 1) % 4]:
                            e.activation(out=h[:, fidx - 1, :], in_=hq[:, 0:CAP],
                                         func=AF.Gelu), waits=[("pe", mmv)])
                        st["gel"][(e_i, fidx)] = c.act(
                            lambda e, fidx=fidx, hp=hp: e.activation(
                                out=h[:, fidx, :], in_=hp[:, 0:CAP], func=AF.Gelu))
                st["wfree"][slot] = c.cnt["pe"]
                if fc >= 1 and pending_y:
                    pending_y.pop(0)()
            st["mm1"][e_i] = c.cnt["pe"]
            gelall = st["gel"][(e_i, NKF - 1)]
            for dt_ in range(NKD):
                po = psbufs[dt_ % 4]
                slot = (e_i * nwc + NFC1 + dt_) % 2
                wsem = f"g_w{slot}" if first_call else f"s_w{slot}"
                if first_call:
                    c.wait("pool", "pe", st["wfree"].get(slot, 0))
                    c.wait("pool", f"s_m2{slot}", st["mir2"].get(slot, 0))
                    c.dma("pool", wsem, lambda e, e_i=e_i, dt_=dt_, slot=slot: e.dma_start(
                        out=w2c[slot][:], in_=dram["w2"].ap()[e_i][dt_]))
                    wv = c.cnt[wsem]
                    if use_mirror:
                        c.wait("sync", wsem, wv)
                        c.dma("sync", f"s_m2{slot}", lambda e, e_i=e_i, dt_=dt_, slot=slot:
                              e.dma_start(out=w2m.ap()[e_i][dt_], in_=w2c[slot][:]))
                        st["mir2"][slot] = c.cnt[f"s_m2{slot}"]
                else:
                    c.wait("sync", "pe", st["wfree"].get(slot, 0))
                    c.dma("sync", wsem, lambda e, e_i=e_i, dt_=dt_, slot=slot: e.dma_start(
                        out=w2c[slot][:], in_=w2m.ap()[e_i][dt_]))
                    wv = c.cnt[wsem]
                c.wait("pe", wsem, wv)
                if dt_ == 0:
                    c.wait("pe", "act", gelall)
                if dt_ % 2 == 0 and dt_ >= 4:
                    c.wait("pe", "act", st["oev"][(e_i, dt_ - 3)])
                if dt_ < 2 and e_i > 1:
                    c.wait("act", f"a_tp{e_i % 2}0", st["ytv"][e_i - 2][0])
                    c.wait("act", f"a_tp{e_i % 2}1", st["ytv"][e_i - 2][1])
                for fk in range(NKF):
                    for gi, (gb, gn) in enumerate(CGRP):
                        last = (fk == NKF - 1) and (gi == len(CGRP) - 1)
                        c.on("pe", lambda e, slot=slot, fk=fk, po=po, gb=gb,
                             gn=gn: e.matmul(
                            po[:, gb * 128:(gb + gn) * 128], w2c[slot][:, fk, :],
                            h[:, fk, gb * 128:(gb + gn) * 128],
                            start=(fk == 0), stop=(fk == NKF - 1)),
                            inc=("pe" if last else None))
                mm2v = c.cnt["pe"]
                st["wfree"][slot] = mm2v
                if dt_ % 2 == 1:
                    st["oev"][(e_i, dt_ - 1)] = c.act(
                        lambda e, dt_=dt_, pq=psbufs[(dt_ - 1) % 4], par=par:
                        e.activation(out=yfm[par][:, :, dt_ - 1, :],
                                     in_=pq[:, 0:CAP], func=AF.Copy),
                        waits=[("pe", mm2v)])
                    st["oev"][(e_i, dt_)] = c.act(
                        lambda e, dt_=dt_, po=po, par=par:
                        e.activation(out=yfm[par][:, :, dt_, :], in_=po[:, 0:CAP],
                                     func=AF.Copy))
            st["oevall"] = st["oev"][(e_i, NKD - 1)]
            for f in pending_y:
                f()
        ych, ysadds = make_y_bundles(E - 1)
        for f in ych:
            f()
        for f in ysadds:
            f()

    def emit_folds():
        fold_ev = 0
        c.wait("pe", "dve", c.cnt["dve"])
        c.wait("pe", "act", c.cnt["act"])

        def LD(par, dst_fn, src_fn, consumed):
            c.wait("pool", consumed[0], consumed[1])
            c.dma("pool", f"g_fa{par}", lambda e, dst_fn=dst_fn, src_fn=src_fn:
                  e.dma_start(out=dst_fn(), in_=src_fn()))
            return c.cnt[f"g_fa{par}"]

        c.dma("pool", "g_xt", lambda e: e.dma_start(
            out=S["Xtw"][:], in_=dram["xt_w"].ap().rearrange("(k p) r -> p k r", p=128)))
        def regA(mt):
            pst = [ps_a, ps_b, ps_c, ps_d][mt // 2]
            return pst[:, (mt % 2) * 512:(mt % 2) * 512 + R]
        lastmm = {0: 0, 1: 0}
        for wn, vn, outn in [("wq", "vb", "Ab"), ("wq", "vt", "At"),
                             ("wk", "wb", "Cb"), ("wk", "wt", "Ct")]:
            for kc in range(NKD):
                par = kc % 2
                lw = LD(par, (lambda par=par: fwa[par][:]),
                        (lambda wn=wn, kc=kc: dram[wn].ap()[kc * 128:(kc + 1) * 128, :]),
                        ("pe", lastmm[par]))
                lv = LD(par, (lambda par=par: fvb[par][:]),
                        (lambda vn=vn, kc=kc: dram[vn].ap()[kc * 128:(kc + 1) * 128, :]),
                        ("pe", lastmm[par]))
                c.wait("pe", f"g_fa{par}", max(lw, lv))
                if kc == 0:
                    c.wait("pe", "act", fold_ev)
                for mt in range(NKD):
                    c.on("pe", lambda e, par=par, mt=mt, kc=kc: e.matmul(
                        regA(mt), fwa[par][:, mt * 128:(mt + 1) * 128], fvb[par][:],
                        start=(kc == 0), stop=(kc == NKD - 1)),
                        inc=("pe" if mt == NKD - 1 else None))
                lastmm[par] = c.cnt["pe"]
            mmv = c.cnt["pe"]
            for mt in range(NKD):
                fold_ev = c.act(lambda e, outn=outn, mt=mt: e.activation(
                    out=S[outn][:, mt, :], in_=regA(mt), func=AF.Copy),
                    waits=[("pe", mmv)])
        c.wait("sync", "s_pr", c.cnt["s_pr"])
        c.dma("sync", "s_pr", lambda e: e.dma_start(
            out=S["xtile0"][0:1, :], in_=dram["bias_t"].ap()))
        c.wait("sync", "s_pr", c.cnt["s_pr"])
        c.dma("sync", "s_pr", lambda e: e.dma_start(
            out=S["xtile1"][0:1, :], in_=dram["bias_b"].ap()))
        bld = c.cnt["s_pr"]
        bv = c.dve(lambda e: e.scalar_tensor_tensor(
            S["xtile0"][0:1, :], S["xtile0"][0:1, :], S["alpha"][0:1, 0:1],
            S["xtile1"][0:1, :], ALU.mult, ALU.add),
            waits=[("s_sml", SML), ("s_pr", bld)])
        ev2 = 0
        for kc in range(NKD):
            c.wait("pe", "dve", bv if kc == 0 else ev2)
            pv = c.on("pe", lambda e, kc=kc: e.transpose(
                ps_tr[:, 304:305], S["xtile0"][0:1, kc * 128:(kc + 1) * 128],
                S["ident_f32"][0:1, 0:1]), inc="pe")
            ev2 = c.dve(lambda e, kc=kc: e.tensor_copy(S["bvfm"][:, kc:kc + 1],
                                                       ps_tr[:, 304:305]),
                        waits=[("pe", pv)])
        bvb = c.dve(lambda e: e.tensor_copy(S["bvfmb"][:], S["bvfm"][:]))
        def regF(rt, nh):
            return (ps_c if rt == 0 else ps_d)[:, nh * 512:(nh + 1) * 512]
        for fi, (un, outn) in enumerate([("ub", "Fb"), ("ut", "Ft")]):
            for kc in range(NKD):
                par = kc % 2
                lu = LD(par, (lambda par=par: fvb[par][:]),
                        (lambda un=un, kc=kc: dram[un].ap()[kc * 128:(kc + 1) * 128, :]),
                        ("pe", lastmm[par]))
                lw = LD(par, (lambda par=par: fwa[par][:]),
                        (lambda kc=kc: dram["woT"].ap()[kc * 128:(kc + 1) * 128, :]),
                        ("pe", lastmm[par]))
                c.wait("pe", f"g_fa{par}", max(lu, lw))
                if kc == 0:
                    c.wait("pe", "act", fold_ev)
                    c.wait("pe", "dve", bvb)
                for rt in range(2):
                    for nh in range(2):
                        last = (fi == 1) and (rt == 1) and (nh == 1)
                        c.on("pe", lambda e, par=par, rt=rt, nh=nh, kc=kc: e.matmul(
                            regF(rt, nh), fvb[par][:, rt * 128:(rt + 1) * 128],
                            fwa[par][:, nh * 512:(nh + 1) * 512],
                            start=(kc == 0), stop=(kc == NKD - 1)),
                            inc=("pe" if last else None))
                if fi == 0:
                    for nh in range(2):
                        c.on("pe", lambda e, par=par, nh=nh, kc=kc: e.matmul(
                            ps_a[0:1, nh * 512:(nh + 1) * 512], S["bvfmb"][:, kc:kc + 1],
                            fwa[par][:, nh * 512:(nh + 1) * 512],
                            start=(kc == 0), stop=(kc == NKD - 1)),
                            inc=("pe" if nh == 1 else None))
                lastmm[par] = c.cnt["pe"]
            mmv = c.cnt["pe"]
            for rt in range(2):
                for nh in range(2):
                    fold_ev = c.act(lambda e, outn=outn, rt=rt, nh=nh: e.activation(
                        out=S[outn][:, rt, nh * 512:(nh + 1) * 512], in_=regF(rt, nh),
                        func=AF.Copy), waits=[("pe", mmv)])
            if fi == 0:
                for nh in range(2):
                    fold_ev = c.act(lambda e, nh=nh: e.activation(
                        out=S["bprojT"][:, nh * 512:(nh + 1) * 512],
                        in_=ps_a[0:1, nh * 512:(nh + 1) * 512], func=AF.Copy),
                        waits=[("pe", mmv)])
        return fold_ev

    def emit_attn(st, fold_act, fold_fw):
        st["xn"] = {}; st["xf"] = {}; st["x2"] = {}
        c.wait("sync", "s_pr", c.cnt["s_pr"])
        c.dma("sync", "s_pr", lambda e: e.dma_start(
            out=S["invc_row"][:], in_=dram["invc_row"].ap()))
        invld = c.cnt["s_pr"]
        c.wait("pe", "s_pr", invld)
        c.wait("pe", "s_sml", SML)
        pv = c.on("pe", lambda e: e.matmul(
            ps_tr[:, 300:301], S["ones_1x128_f32"][:], S["alpha"][:],
            start=True, stop=True), inc="pe")
        alv = c.dve(lambda e: e.tensor_copy(S["alphab"][:], ps_tr[:, 300:301]),
                    waits=[("pe", pv)])
        bc_ev = 0
        for ch in range(NCH_A):
            if ch == 0:
                c.wait("pe", "dve", alv)
            c.wait("pe", "act", bc_ev)
            pv = c.on("pe", lambda e, ch=ch: e.matmul(
                ps_tr[:, 0:ACH], S["ones_1x128_f32"][:],
                S["invc_row"][:, ch * ACH:(ch + 1) * ACH],
                start=True, stop=True), inc="pe")
            bc_ev = c.act(lambda e, ch=ch: e.activation(
                out=at_rcf[:, ch * ACH:(ch + 1) * ACH], in_=ps_tr[:, 0:ACH],
                func=AF.Copy), waits=[("pe", pv)])
        c.dve(lambda e: e.tensor_mul(at_rcaf[:], at_rcf[:], at_rcf[:]),
              waits=[("act", bc_ev), ("dve", alv)])
        c.dve(lambda e: e.tensor_scalar_mul(at_rcaf[:], at_rcaf[:], S["alphab"][:]))
        NCH = NCH_A
        TPC = ACH // 128
        projd = {}
        evs = {}
        plist = [("Ab", "ab"), ("At", "at"), ("Cb", "cb"), ("Ct", "ct"),
                 ("Xtw", "zt")]

        def emit_ln_chunk(ch):
            cpar = ch % 2
            if ch >= 2:
                for _, on2 in plist:
                    for rt2 in range(2):
                        c.wait("sync", "act", projd[(on2, rt2, ch - 2)])
            for ti in range(TPC):
                i = ch * TPC + ti
                par = i % 2
                xt = S["xtile0"] if par == 0 else S["xtile1"]
                sx = f"s_x{par}"
                sc = 12 * par
                if i >= 2:
                    c.wait("sync", "dve", st["xn"][i - 2])
                c.wait("sync", "g_sa", st["f1_sa"])
                c.dma("sync", sx, lambda e, xt=xt, i=i: e.dma_start(
                    out=xt[:], in_=x_res.ap()[i * 128:(i + 1) * 128, :]))
                xv = c.cnt[sx]
                ew = [(f"s_tp{par}", st["xf"][i - 2])] if i >= 2 else []
                st["xn"][i] = ln_tile(st, i, xt, None, sc, sx, xv,
                                      (lambda par=par: at_xntm[par][:]), extra_waits=ew)
                c.wait("sync", "dve", st["xn"][i])
                st["xf"][i] = c.dma("sync", f"s_tp{par}",
                                    lambda e, par=par, ti=ti, cpar=cpar: e.dma_start(
                    out=at_xnfmc[cpar][:, :, ti * 128:(ti + 1) * 128],
                    in_=at_xntm[par][:], transpose=True))

        def emit_proj_chunk(ch):
            cpar = ch % 2
            xnfc = at_xnfmc[cpar]
            for k in range(10):
                wn, on = plist[k // 2]
                rt = k % 2
                kg = ch * 10 + k
                pp = psbufs[kg % 4]
                if k % 2 == 0:
                    if kg == 0:
                        c.wait("pe", "g_xt", fold_fw)
                        c.wait("pe", "act", fold_act)
                        c.wait("pe", "act", bc_ev)
                    if k == 0:
                        for ti in range(TPC):
                            i = ch * TPC + ti
                            c.wait("pe", f"s_tp{i % 2}", st["xf"][i])
                    if kg >= 4:
                        c.wait("pe", "act", evs[kg - 3])
                for kc in range(NKD):
                    c.on("pe", lambda e, wn=wn, rt=rt, kc=kc, pp=pp,
                         xnfc=xnfc: e.matmul(
                        pp[:, 0:ACH], S[wn][:, kc, rt * 128:(rt + 1) * 128],
                        xnfc[:, kc, :],
                        start=(kc == 0), stop=(kc == NKD - 1)),
                        inc=("pe" if kc == NKD - 1 else None))
                if k % 2 == 1:
                    mv = c.cnt["pe"]
                    ev0 = c.act(lambda e, on=on, ch=ch, q=psbufs[(kg - 1) % 4]:
                                e.activation(
                        out=rm[on][:, 0, ch * ACH:(ch + 1) * ACH], in_=q[:, 0:ACH],
                        func=AF.Copy), waits=[("pe", mv)])
                    ev1 = c.act(lambda e, on=on, ch=ch, pp=pp: e.activation(
                        out=rm[on][:, 1, ch * ACH:(ch + 1) * ACH], in_=pp[:, 0:ACH],
                        func=AF.Copy))
                    evs[kg - 1] = ev0
                    evs[kg] = ev1
                    projd[(on, 0, ch)] = ev0
                    projd[(on, 1, ch)] = ev1

        zscan = c.dve(lambda e: e.memset(at_carry[:], 0.0))

        def emit_scan_chunk(ch):
            for qi, on in enumerate(["cb", "ct", "zt"]):
                for rt in range(2):
                    init_ap = (
                        (lambda e, on=on, rt=rt, ch=ch:
                         rm[on][:, rt, ch * ACH - 1:ch * ACH]) if ch > 0
                        else (lambda e: at_carry[:, 0:1]))
                    c.dve(lambda e, on=on, rt=rt, ch=ch, init_ap=init_ap:
                          e.tensor_tensor_scan(
                        rm[on][:, rt, ch * ACH:(ch + 1) * ACH],
                        rm[on][:, rt, ch * ACH:(ch + 1) * ACH],
                        rm[on][:, rt, ch * ACH:(ch + 1) * ACH],
                        init_ap(e), ALU.add, ALU.bypass),
                        waits=[("act", projd[(on, rt, ch)]), ("dve", zscan)])

        emit_ln_chunk(0)
        for ch in range(NCH):
            if ch + 1 < NCH:
                emit_ln_chunk(ch + 1)
            emit_proj_chunk(ch)
            emit_scan_chunk(ch)

        def wait_proj(eng, names):
            for on in names:
                for rt in range(2):
                    for ch in range(NCH):
                        c.wait(eng, "act", projd[(on, rt, ch)])
        c.dve(lambda e: e.memset(at_car_loc[:], 0.0))
        for qi, on in enumerate(["cb", "ct", "zt"]):
            for rt in range(2):
                c.dve(lambda e, on=on, rt=rt, qi=qi: e.tensor_copy(
                    at_car_loc[:, qi * 2 + rt:qi * 2 + rt + 1],
                    rm[on][:, rt, TCORE - 1:TCORE]))
        tot = c.cnt["dve"]
        c.wait("sync", "dve", tot)
        c.wait("sync", "s_pr", c.cnt["s_pr"])
        c.dma("sync", "s_pr", lambda e: e.dma_start(out=cc_car_in.ap(), in_=at_car_loc[:]))
        c.wait("pool", "s_pr", c.cnt["s_pr"])
        c.on("pool", lambda e: e.collective_compute(
            "AllGather", ALU.bypass, replica_groups=[list(range(NCORE))],
            ins=[cc_car_in.ap()], outs=[cc_car_out.ap()]), inc="cc", n=1)
        c.wait("pool", "cc", c.cnt["cc"])
        c.dma("pool", "g_cc", lambda e: e.dma_start(
            out=at_car_all[:], in_=cc_car_out.ap().rearrange("(r p) s -> p r s", p=128)))
        carv = c.cnt["g_cc"]
        c.dve(lambda e: e.memset(at_carry[:], 0.0), waits=[("g_cc", carv)])
        for r_ in range(NCORE):
            c.dve(lambda e, r_=r_: e.scalar_tensor_tensor(
                at_carry[:], at_car_all[:, r_, :], S["wvec"][:, r_:r_ + 1], at_carry[:],
                ALU.mult, ALU.add))
        card = c.cnt["dve"]
        for qi, on in enumerate(["cb", "ct", "zt"]):
            for rt in range(2):
                c.dve(lambda e, on=on, rt=rt, qi=qi: e.tensor_scalar_add(
                    rm[on][:, rt, :], rm[on][:, rt, :],
                    at_carry[:, qi * 2 + rt:qi * 2 + rt + 1]),
                    waits=[("dve", card)])
        wait_proj("dve", ["ab", "at"])
        for rt in range(2):
            c.dve(lambda e, rt=rt: e.tensor_mul(rm["ab"][:, rt, :], rm["ab"][:, rt, :],
                                                at_rcf[:]), waits=[("act", bc_ev)])
            c.dve(lambda e, rt=rt: e.tensor_mul(rm["ab"][:, rt, :], rm["ab"][:, rt, :],
                                                rm["cb"][:, rt, :]))
            c.dve(lambda e, rt=rt: e.tensor_mul(rm["at"][:, rt, :], rm["at"][:, rt, :],
                                                at_rcaf[:]))
            c.dve(lambda e, rt=rt: e.tensor_mul(rm["at"][:, rt, :], rm["at"][:, rt, :],
                                                rm["zt"][:, rt, :]))
            c.dve(lambda e, rt=rt: e.tensor_mul(rm["at"][:, rt, :], rm["at"][:, rt, :],
                                                rm["ct"][:, rt, :]))
        intd = c.cnt["dve"]
        for i in range(NT):
            par = i % 2
            po = ps_c if par == 0 else ps_d
            xt = S["xtile0"] if par == 0 else S["xtile1"]
            c.wait("sync", "dve", st["x2"].get(i - 2, 0))
            c.dma("sync", f"s_x{par}", lambda e, xt=xt, i=i: e.dma_start(
                out=xt[:], in_=x_res.ap()[i * 128:(i + 1) * 128, :]))
            xv = c.cnt[f"s_x{par}"]
            c.wait("pe", "dve", intd)
            c.wait("pe", "dve", st["x2"].get(i - 2, 0))
            first = True
            for srcn, fold in [("ab", "Fb"), ("at", "Ft")]:
                for rt in range(2):
                    for nh in range(2):
                        c.on("pe", lambda e, srcn=srcn, fold=fold, rt=rt, nh=nh, i=i,
                             po=po, first=first: e.matmul(
                            po[:, nh * 512:(nh + 1) * 512],
                            rm[srcn][:, rt, i * 128:(i + 1) * 128],
                            S[fold][:, rt, nh * 512:(nh + 1) * 512],
                            start=first, stop=False))
                    first = False
            for nh in range(2):
                c.on("pe", lambda e, nh=nh, po=po: e.matmul(
                    po[:, nh * 512:(nh + 1) * 512], S["ones_1x128_bf16"][:],
                    S["bprojT"][:, nh * 512:(nh + 1) * 512], start=False, stop=True),
                    inc=("pe" if nh == 1 else None))
            mv = c.cnt["pe"]
            for nh in range(2):
                st["x2"][i] = c.dve(lambda e, xt=xt, nh=nh, po=po: e.tensor_add(
                    xt[:, nh * 512:(nh + 1) * 512], xt[:, nh * 512:(nh + 1) * 512],
                    po[:, nh * 512:(nh + 1) * 512]),
                    waits=([("pe", mv), (f"s_x{par}", xv)] if nh == 0 else ()))
            c.wait("sync", "dve", st["x2"][i])
            c.dma("sync", "s_st", lambda e, xt=xt, i=i: e.dma_start(
                out=y_out.ap()[i * 128:(i + 1) * 128, :], in_=xt[:]))
        return c.cnt["s_st"]

    st1 = {"wfree": {}, "sa0": 0}
    ffn_phase1(st1, x_res, [("g_pr", GPR), ("s_sml", SML)])
    ffn_phase2(st1)
    fold_act = emit_folds()
    fold_fw = c.cnt["g_xt"]
    if phases == "f1":
        c.wait("sync", "dve", c.cnt["dve"])
        c.dma("sync", "s_st", lambda e: e.dma_start(out=y_out.ap(), in_=x_res.ap()))
        c.wait("sync", "s_st", c.cnt["s_st"])
        c.emit(); es.close(); return nc
    ffn_phase3(st1, x_res, [("g_pr", GPR)], entry_act=fold_act, first_call=True)
    st1["f1_sa"] = c.cnt["g_sa"]
    if phases == "f1x":
        c.wait("sync", "g_sa", c.cnt["g_sa"])
        c.dma("sync", "s_st", lambda e: e.dma_start(out=y_out.ap(), in_=x_res.ap()))
        c.wait("sync", "s_st", c.cnt["s_st"])
        c.emit(); es.close(); return nc

    c.xbar(extra=[("g_sa", c.cnt["g_sa"]), ("s_cr0", c.cnt["s_cr0"]),
                  ("s_cr1", c.cnt["s_cr1"])])
    sta = {"f1_sa": st1["f1_sa"]}
    wb_done = emit_attn(sta, fold_act, fold_fw)
    if phases == "f1a":
        c.wait("sync", "s_st", c.cnt["s_st"])
        c.wait("sync", "g_sa", c.cnt["g_sa"])
        c.emit(); es.close(); return nc

    clear2 = [(f"g_sc{e_i}", c.cnt[f"g_sc{e_i}"]) for e_i in range(E)]
    c.xbar(extra=clear2 + [("g_sa", c.cnt["g_sa"]), ("s_st", wb_done),
                           ("s_cr0", c.cnt["s_cr0"]), ("s_cr1", c.cnt["s_cr1"])])
    st2 = {"wfree": dict(st1["wfree"]), "sa0": c.cnt["g_sa"]}
    ffn_phase1(st2, y_out, [("s_st", wb_done)])
    ffn_phase2(st2)
    ffn_phase3(st2, y_out, [("s_st", wb_done)], entry_act=0, first_call=False)

    c.wait("sync", "g_sa", c.cnt["g_sa"])
    c.wait("sync", "s_st", c.cnt["s_st"])
    c.emit()
    es.close()
    return nc


def _make_consts(TCORE, core):
    NT = TCORE // 128
    n, q = core // 4, core % 4
    cons = {}
    cons["ident_f32"] = np.eye(128, dtype=np.float32)
    bd = np.zeros((128, 128), np.float32)
    for g in range(16):
        bd[g * 8:(g + 1) * 8, g * 8:(g + 1) * 8] = 1.0
    cons["blockdiag8"] = bd
    cons["ones_1x128_f32"] = np.ones((1, 128), np.float32)
    cons["ones_1x128_bf16"] = np.ones((1, 128), np.float32).astype(ml_dtypes.bfloat16)
    cons["iota_em"] = np.broadcast_to(
        np.arange(TCORE, dtype=np.float32)[None, :], (E, TCORE)).copy()
    cons["tile_em"] = np.broadcast_to(
        (np.arange(TCORE) // 128).astype(np.float32)[None, :], (E, TCORE)).copy()
    t = np.arange(TCORE)
    thitlo = np.stack([t // 64, t % 64], -1).reshape(NT, 128, 2).transpose(1, 0, 2)
    cons["thitlo"] = np.ascontiguousarray(thitlo.astype(np.float32)).astype(ml_dtypes.bfloat16)
    gidx = q * TCORE + t
    cons["invc_row"] = (1.0 / (gidx + 1).astype(np.float64)).astype(np.float32)[None, :]
    sel = np.zeros((128, E), np.float32)
    for e_i in range(E):
        sel[64 * n + 8 * e_i, e_i] = 1.0
    cons["selmat"] = sel
    wv = np.zeros((128, E), np.float32)
    for r_ in range(4 * n, 4 * n + q):
        wv[:, r_] = 1.0
    cons["wvec"] = wv
    return cons


def _shared_weights(Wq, Wk, Wo, U_b, V_b, W_b, bias_b, U_t, V_t, W_t, X_t, bias_t,
                    alpha, gate_w, w1, w2):
    return {
        "wq": np.ascontiguousarray(Wq, np.float32),
        "wk": np.ascontiguousarray(Wk, np.float32),
        "vb": np.ascontiguousarray(V_b, np.float32),
        "vt": np.ascontiguousarray(V_t, np.float32),
        "wb": np.ascontiguousarray(W_b, np.float32),
        "wt": np.ascontiguousarray(W_t, np.float32),
        "xt_w": np.ascontiguousarray(X_t, np.float32),
        "ub": np.ascontiguousarray(U_b, np.float32),
        "ut": np.ascontiguousarray(U_t, np.float32),
        "woT": np.ascontiguousarray(np.asarray(Wo, np.float32).T),
        "bias_b": np.asarray(bias_b, np.float32).reshape(1, D),
        "bias_t": np.asarray(bias_t, np.float32).reshape(1, D),
        "alpha": np.asarray(alpha, np.float32).reshape(1, 1),
        "gate_wT": np.ascontiguousarray(np.asarray(gate_w, np.float32).T),
        "w1": np.ascontiguousarray(
            np.asarray(w1, np.float32).reshape(8, 8, 128, 8, 512)
            .transpose(0, 3, 2, 1, 4)),
        "w2": np.ascontiguousarray(
            np.asarray(w2, np.float32).reshape(8, 32, 128, 8, 128)
            .transpose(0, 3, 2, 1, 4)),
    }


def run_cores(x, shared, TCORE, phases="all", sim=False, use_mirror=False):
    nc = build_nc(TCORE=TCORE, phases=phases, use_mirror=use_mirror)
    xf = np.ascontiguousarray(x.reshape(NCORE, TCORE, D))
    in_maps = []
    for core in range(NCORE):
        m = dict(shared)
        m["xin"] = np.ascontiguousarray(xf[core])
        m.update(_make_consts(TCORE, core))
        in_maps.append(m)
    if sim:
        from concourse import bass_interp
        s = bass_interp.MultiCoreSim(nc, NCORE)
        for core in range(NCORE):
            for k, v in in_maps[core].items():
                s.cores[core].tensor(k)[:] = v
        s.simulate()
        outs = [np.array(s.cores[core].mem_tensor("y_out")) for core in range(NCORE)]
    else:
        res = run_bass_kernel_spmd(nc, in_maps, core_ids=list(range(NCORE)))
        outs = [res.results[core]["y_out"] for core in range(NCORE)]
    return np.stack(outs, 0)


def kernel(L0_attn_x, L0_attn_Wq, L0_attn_Wk, L0_attn_Wo, L0_attn_U_b, L0_attn_V_b,
           L0_attn_W_b, L0_attn_bias_b, L0_attn_U_t, L0_attn_V_t, L0_attn_W_t,
           L0_attn_X_t, L0_attn_bias_t, L0_attn_alpha, L0_ffn_gate_w, L0_ffn_w1,
           L0_ffn_w2):
    x = np.ascontiguousarray(np.asarray(L0_attn_x, dtype=np.float32))
    B, T, _ = x.shape
    TCORE = (B * T) // NCORE
    shared = _shared_weights(L0_attn_Wq, L0_attn_Wk, L0_attn_Wo, L0_attn_U_b,
                             L0_attn_V_b, L0_attn_W_b, L0_attn_bias_b, L0_attn_U_t,
                             L0_attn_V_t, L0_attn_W_t, L0_attn_X_t, L0_attn_bias_t,
                             L0_attn_alpha, L0_ffn_gate_w, L0_ffn_w1, L0_ffn_w2)
    out = run_cores(x, shared, TCORE, phases="all", sim=False)
    return out.reshape(B, T, D)


# version 22
# speedup vs baseline: 1.0258x; 1.0169x over previous
import numpy as np
import ml_dtypes
import concourse.bass as bass
import concourse.mybir as mybir
from concourse.bass import IndirectOffsetOnAxis
from concourse.bass_utils import run_bass_kernel_spmd

F32 = mybir.dt.float32
BF16 = mybir.dt.bfloat16
I32 = mybir.dt.int32
AF = mybir.ActivationFunctionType
ALU = mybir.AluOpType
AXX = mybir.AxisListType.X

NCORE = 8
D = 1024
R = 256
E = 8
F = 4096
EPS = 1e-5
NITER = 36
NKD = D // 128
NKF = F // 128
ENGS = ("pe", "dve", "act", "pool", "sync")


class Ctx:

    def __init__(self, nc, es):
        self.nc = nc
        self.es = es
        self.ops = {e: [] for e in ENGS}
        self.sem = {}
        self.cnt = {}

    def new_sem(self, name):
        self.sem[name] = self.es.enter_context(self.nc.semaphore(name))
        self.cnt[name] = 0

    def on(self, eng, fn, inc=None, n=1):
        if inc is not None:
            self.cnt[inc] += n
            sem = self.sem[inc]

            def run(e, fn=fn, sem=sem, n=n):
                fn(e).then_inc(sem, n)
            self.ops[eng].append(run)
            return self.cnt[inc]
        self.ops[eng].append(lambda e, fn=fn: fn(e))
        return None

    def dma(self, eng, sem_name, fn):
        return self.on(eng, fn, inc=sem_name, n=16)

    def wait(self, eng, sem_name, val):
        if val is None or val <= 0:
            return
        sem = self.sem[sem_name]
        self.ops[eng].append(lambda e, sem=sem, val=val: e.wait_ge(sem, val))

    def dve(self, fn, waits=()):
        for s, v in waits:
            self.wait("dve", s, v)
        self.wait("dve", "dve", self.cnt["dve"])
        return self.on("dve", fn, inc="dve")

    def act(self, fn, waits=()):
        for s, v in waits:
            self.wait("act", s, v)
        self.wait("act", "act", self.cnt["act"])
        return self.on("act", fn, inc="act")

    def xbar(self, extra=()):
        tots = [(s, v) for s, v in self.cnt.items() if v > 0]
        tots += list(extra)
        for eng in ENGS:
            for s, v in tots:
                if eng == {"pe": "pe", "dve": "dve", "act": "act"}.get(s, None):
                    continue
                self.wait(eng, s, v)

    def emit(self):
        blk = self.es.enter_context(self.nc.Block())
        blk.tensor(lambda e: [f(e) for f in self.ops["pe"]])
        blk.vector(lambda e: [f(e) for f in self.ops["dve"]])
        blk.scalar(lambda e: [f(e) for f in self.ops["act"]])
        blk.gpsimd(lambda e: [f(e) for f in self.ops["pool"]])
        blk.sync(lambda e: [f(e) for f in self.ops["sync"]])


def build_nc(TCORE=2048, phases="all", use_mirror=False):
    from contextlib import ExitStack
    NT = TCORE // 128
    TGLOB = 4 * TCORE
    KSEL = TGLOB // 4
    SEGW = TGLOB // 8
    CAP = ((KSEL // 4) * 5 // 4 + 127) // 128 * 128
    CB = CAP // 128
    PW = 1040
    FC1 = 512
    NFC1 = F // FC1

    ACH = min(512, TCORE)
    NCH_A = TCORE // ACH
    CGRP = []
    b0 = 0
    while b0 < CB:
        nb = min(4, CB - b0)
        CGRP.append((b0, nb))
        b0 += nb

    nc = bass.Bass("TRN2", target_bir_lowering=False, debug=False)
    es = ExitStack()
    c = Ctx(nc, es)

    dram = {}

    def din(name, shape, dt=F32):
        dram[name] = nc.dram_tensor(name, list(shape), dt, kind="ExternalInput")

    din("xin", [TCORE, D])
    for nm in ["wq", "wk"]:
        din(nm, [D, D])
    for nm in ["vb", "vt", "wb", "wt", "xt_w", "ub", "ut"]:
        din(nm, [D, R])
    din("woT", [D, D])
    din("bias_b", [1, D]); din("bias_t", [1, D]); din("alpha", [1, 1])
    din("gate_wT", [D, E])
    din("w1", [E, NFC1, 128, NKD, FC1]); din("w2", [E, NKD, 128, NKF, 128])
    din("ident_f32", [128, 128])
    din("blockdiag8", [128, 128])
    din("ones_1x128_f32", [1, 128])
    din("ones_1x128_bf16", [1, 128], BF16)
    din("iota_em", [E, TCORE])
    din("tile_em", [E, TCORE])
    din("thitlo", [128, NT, 2], BF16)
    din("invc_row", [1, TCORE])
    din("selmat", [128, E])
    din("wvec", [128, E])

    y_out = nc.dram_tensor("y_out", [TCORE, D], F32, kind="ExternalOutput")
    w1m = nc.dram_tensor("w1m", [E, NFC1, 128, NKD, FC1], BF16)
    w2m = nc.dram_tensor("w2m", [E, NKD, 128, NKF, 128], BF16)

    x_res = nc.dram_tensor("x_res", [TCORE, D], F32)
    compA = [nc.dram_tensor(f"compA{e_i}", [CAP + NT, PW], BF16) for e_i in range(E)]
    cc_prob_in = nc.dram_tensor("cc_prob_in", [E, TCORE], F32)
    cc_prob_out = nc.dram_tensor("cc_prob_out", [NCORE * E, TCORE], F32)
    cc_car_in = nc.dram_tensor("cc_car_in", [128, 8], F32)
    cc_car_out = nc.dram_tensor("cc_car_out", [NCORE * 128, 8], F32)

    for s in ["pe", "dve", "act", "cc",
              "s_sml", "s_x0", "s_x1", "s_tp0", "s_tp1", "s_cr0", "s_cr1",
              "s_st", "s_pr", "s_bis", "s_m10", "s_m11", "s_m20", "s_m21", "s_w0", "s_w1",
              "a_tp00", "a_tp01", "a_tp10", "a_tp11",
              "g_pr", "g_gt", "g_w0", "g_w1", "g_w2", "g_sa", "g_cc",
              "g_fa0", "g_fa1", "g_xt"]:
        c.new_sem(s)
    for e_i in range(E):
        c.new_sem(f"g_sc{e_i}")

    S = {}

    def sb(name, shape, dt=F32):
        S[name] = es.enter_context(nc.sbuf_tensor("sb_" + name, list(shape), dt))
        return S[name]

    sb("ident_f32", [128, 128])
    sb("blockdiag8", [128, 128])
    sb("ones_1x128_f32", [1, 128]); sb("ones_1x128_bf16", [1, 128], BF16)
    sb("gate_wT", [128, NKD, E], BF16)
    sb("selmat", [128, E]); sb("wvec", [128, E])
    sb("thitlo", [128, NT, 2], BF16)
    sb("alpha", [1, 1])
    sb("rstd_all", [128, NT])
    sb("probs_tm", [128, NT, E])
    sb("slots_i32", [128, NT, E], I32)
    sb("th_tm", [128, E]); sb("th_em", [E, 1]); sb("th_row", [1, E])
    sb("xtile0", [128, D]); sb("xtile1", [128, D])
    sb("stat", [128, 24])
    sb("et0", [128, E]); sb("et1", [128, E])
    sb("gm_s", [128, E])
    sb("lo", [128, 1]); sb("hi", [128, 1]); sb("th", [128, 1])
    sb("cnt_s", [128, 1]); sb("ge", [128, 1]); sb("d1", [128, 1]); sb("d2c", [128, 1])
    sb("Ab", [128, NKD, R], BF16); sb("At", [128, NKD, R], BF16)
    sb("Cb", [128, NKD, R], BF16); sb("Ct", [128, NKD, R], BF16)
    sb("Xtw", [128, NKD, R], BF16)
    sb("Fb", [128, 2, D], BF16); sb("Ft", [128, 2, D], BF16)
    sb("bvfm", [128, NKD])
    sb("bvfmb", [128, NKD], BF16)
    sb("bprojT", [1, D], BF16)
    sb("g_col0", [128, CB]); sb("g_col1", [128, CB])
    sb("idx0", [128, CB], I32); sb("idx1", [128, CB], I32)
    sb("trow0", [128, CB]); sb("trow1", [128, CB])
    payload = sb("payload", [128, NT, PW], BF16)
    xnfmt = [sb("xnfmt0", [128, NKD, 128], BF16), sb("xnfmt1", [128, NKD, 128], BF16)]
    fwa = [sb("fwa0", [128, 1024], BF16), sb("fwa1", [128, 1024], BF16)]
    fvb = [sb("fvb0", [128, 256], BF16), sb("fvb1", [128, 256], BF16)]
    sb("alphab", [128, 1])
    at_car_loc = sb("at_car_loc", [128, 8])
    at_car_all = sb("at_car_all", [128, NCORE, 8])
    at_carry = sb("at_carry", [128, 8])

    def ov_plan(items):
        offs, cur = {}, 0
        for name, shape, dt in items:
            nbytes = int(np.prod(shape[1:])) * (2 if dt in (BF16, mybir.dt.float16) else 4)
            nbytes = (nbytes + 31) // 32 * 32
            offs[name] = cur
            cur += nbytes
        return cur, offs

    o1a = [("probs_em", [E, TCORE], F32), ("em_incl", [E, TCORE], F32),
           ("em_ssel", [E, TCORE], F32), ("em_d2", [E, TCORE], F32),
           ("em_mask", [E, TCORE], BF16), ("em_neg", [E, TCORE], BF16),
           ("probbis", [128, SEGW], F32), ("iota_em", [E, TCORE], F32),
           ("tile_em", [E, TCORE], F32), ("maskjunk", [128, SEGW], BF16)]
    o1b = [("h", [128, NKF, CAP], BF16)]
    o2a = [("cbuf0", [128, CB, PW], BF16),
           ("xinfm0", [128, CB, NKD, 128], BF16),
           ("w1c0", [128, NKD, FC1], BF16), ("w1c1", [128, NKD, FC1], BF16),
           ("w2c0", [128, NKF, 128], BF16), ("w2c1", [128, NKF, 128], BF16),
           ("yfm0", [128, CB, NKD, 128], BF16), ("yfm1", [128, CB, NKD, 128], BF16),
           ("ytm0", [128, NKD, 128], BF16), ("ytm1", [128, NKD, 128], BF16),
           ("yf0", [128, D], F32), ("yf1", [128, D], F32)]
    ACHW = min(512, TCORE)
    o2b = [("at_xnfmc0", [128, NKD, ACHW], BF16), ("at_xnfmc1", [128, NKD, ACHW], BF16),
           ("rm_ab", [128, 2, TCORE], BF16), ("rm_at", [128, 2, TCORE], BF16),
           ("rm_cb", [128, 2, TCORE], BF16), ("rm_ct", [128, 2, TCORE], BF16),
           ("rm_zt", [128, 2, TCORE], BF16),
           ("at_rcf", [128, TCORE], BF16), ("at_rcaf", [128, TCORE], BF16),
           ("at_xntm0", [128, D], BF16), ("at_xntm1", [128, D], BF16),
           ("invc_row", [1, TCORE], F32)]
    p3 = o1b + o2a
    sz_p2, off_p2 = ov_plan(o1a)
    sz_p3, off_p3 = ov_plan(p3)
    sz_at, off_at = ov_plan(o2b)
    REG = max(sz_p2, sz_p3, sz_at)
    top = nc.sbuf_top
    reg_base = (top - REG) // 32 * 32
    assert reg_base >= nc.sbuf_base, (
        f"SBUF overlay overflow: reg_base={reg_base} auto_base={nc.sbuf_base} "
        f"sizes p2={sz_p2} p3={sz_p3} at={sz_at}")
    for items, offs in [(o1a, off_p2), (p3, off_p3), (o2b, off_at)]:
        for name, shape, dt in items:
            S[name] = nc.alloc_sbuf_tensor_at("ov_" + name, list(shape), dt,
                                              offset=reg_base + offs[name])
    probs_em = S["probs_em"]
    probbis = S["probbis"]
    em = {"mask": S["em_mask"], "incl": S["em_incl"], "ssel": S["em_ssel"],
          "d2": S["em_d2"], "neg": S["em_neg"], "slotf": S["em_ssel"]}
    cbuf = [S["cbuf0"], S["cbuf0"]]
    xinfm = [S["xinfm0"], S["xinfm0"]]
    h = S["h"]
    w1c = [S["w1c0"], S["w1c1"]]
    w2c = [S["w2c0"], S["w2c1"]]
    yfm = [S["yfm0"], S["yfm1"]]
    ytm = [S["ytm0"], S["ytm1"]]
    yf32 = [S["yf0"], S["yf1"]]
    at_xnfmc = [S["at_xnfmc0"], S["at_xnfmc1"]]
    rm = {n: S["rm_" + n] for n in ["ab", "at", "cb", "ct", "zt"]}
    at_rcf = S["at_rcf"]
    at_rcaf = S["at_rcaf"]
    at_xntm = [S["at_xntm0"], S["at_xntm1"]]

    ps_a = es.enter_context(nc.psum_tensor("ps_a", [128, 1024], F32))
    ps_b = es.enter_context(nc.psum_tensor("ps_b", [128, 1024], F32))
    ps_c = es.enter_context(nc.psum_tensor("ps_c", [128, 1024], F32))
    ps_d = es.enter_context(nc.psum_tensor("ps_d", [128, 1024], F32))
    ps_sc = ps_c
    ps_tr = ps_d
    psbufs = [ps_a, ps_b, ps_c, ps_d]

    for sbn in ["ident_f32", "blockdiag8", "ones_1x128_f32", "ones_1x128_bf16",
                "selmat", "wvec", "thitlo", "alpha"]:
        c.dma("sync", "s_sml", lambda e, sbn=sbn: e.dma_start(
            out=S[sbn][:], in_=dram[sbn].ap()))
    SML = c.cnt["s_sml"]
    c.dma("pool", "g_gt", lambda e: e.dma_start(
        out=S["gate_wT"][:], in_=dram["gate_wT"].ap().rearrange("(k p) e -> p k e", p=128)))
    c.dma("pool", "g_pr", lambda e: e.dma_start(out=x_res.ap(), in_=dram["xin"].ap()))
    GPR = c.cnt["g_pr"]

    c.dve(lambda e: e.memset(payload[:, :, D + E + 2:PW], 0.0))

    def ln_tile(st, i, xt, sq, sc, sx, xv, out_ap_fn, extra_waits=()):
        del sq
        """Emit LN for tile i: stats + normalized bf16 tile via out_ap_fn."""
        c.dve(lambda e, xt=xt, sc=sc: e.reduce_sum(S["stat"][:, sc:sc + 1], xt[:], axis=AXX),
              waits=[(sx, xv)])
        c.dve(lambda e, sc=sc: e.tensor_scalar_mul(S["stat"][:, sc + 1:sc + 2],
                                                   S["stat"][:, sc:sc + 1], 1.0 / D))
        c.dve(lambda e, xt=xt, sc=sc: e.scalar_tensor_tensor(
            out_ap_fn(), xt[:], 1.0, xt[:], ALU.mult, ALU.mult,
            accum_out=S["stat"][:, sc + 2:sc + 3]), waits=extra_waits)
        c.dve(lambda e, sc=sc: e.tensor_scalar_mul(S["stat"][:, sc + 3:sc + 4],
                                                   S["stat"][:, sc + 2:sc + 3], 1.0 / D))
        c.dve(lambda e, sc=sc: e.tensor_scalar(
            S["stat"][:, sc + 4:sc + 5], S["stat"][:, sc + 1:sc + 2],
            S["stat"][:, sc + 1:sc + 2], None, ALU.mult))
        c.dve(lambda e, sc=sc: e.tensor_sub(S["stat"][:, sc + 5:sc + 6],
                                            S["stat"][:, sc + 3:sc + 4],
                                            S["stat"][:, sc + 4:sc + 5]))
        c.dve(lambda e, sc=sc: e.tensor_scalar_add(S["stat"][:, sc + 6:sc + 7],
                                                   S["stat"][:, sc + 5:sc + 6], EPS))
        rv = c.dve(lambda e, sc=sc: e.reciprocal(S["stat"][:, sc + 7:sc + 8],
                                                 S["stat"][:, sc + 6:sc + 7]))
        rs = c.act(lambda e, i=i, sc=sc: e.activation(
            out=S["rstd_all"][:, i:i + 1], in_=S["stat"][:, sc + 7:sc + 8], func=AF.Sqrt),
            waits=[("dve", rv)])
        nm = c.dve(lambda e, i=i, sc=sc: e.tensor_scalar_mul(
            S["stat"][:, sc + 8:sc + 9], S["stat"][:, sc + 1:sc + 2], -1.0),
            waits=[("act", rs)])
        return c.dve(lambda e, xt=xt, i=i, sc=sc: e.scalar_tensor_tensor(
            out_ap_fn(), xt[:], S["stat"][:, sc + 8:sc + 9],
            S["rstd_all"][:, i:i + 1].to_broadcast([128, D]), ALU.add, ALU.mult),
            waits=extra_waits)

    def ffn_phase1(st, base, base_ready):
        st["xn"] = {}; st["sc"] = {}; st["sm"] = {}; st["pr"] = {}; st["em"] = {}
        for i in range(NT):
            par = i % 2
            xt = S["xtile0"] if par == 0 else S["xtile1"]
            sq = None
            et = S["et0"] if par == 0 else S["et1"]
            sx = f"s_x{par}"
            sc = 12 * par
            if i >= 2:
                c.wait("sync", "dve", st["xn"][i - 2])
            for s_, v_ in base_ready:
                c.wait("sync", s_, v_)
            c.dma("sync", sx, lambda e, xt=xt, i=i: e.dma_start(
                out=xt[:], in_=base.ap()[i * 128:(i + 1) * 128, :]))
            xv = c.cnt[sx]
            if i >= 2:
                c.wait("dve", "act", st["sm"][i - 2])
            st["xn"][i] = ln_tile(st, i, xt, sq, sc, sx, xv,
                                  (lambda i=i: payload[:, i, 0:D]))
            stp = f"s_tp{par}"
            if i >= 2:
                c.wait("sync", "pe", st["sc"][i - 2])
            c.wait("sync", "dve", st["xn"][i])
            c.dma("sync", stp, lambda e, par=par, i=i: e.dma_start(
                out=xnfmt[par][:], in_=payload[:, i, 0:D], transpose=True))
            tpv = c.cnt[stp]
            c.wait("pe", stp, tpv)
            c.wait("pe", "g_gt", 16)
            if i >= 1:
                c.wait("pe", "act", st["sm"][i - 1])
            for kc in range(NKD):
                st["sc"][i] = c.on("pe", lambda e, par=par, kc=kc: e.matmul(
                    ps_sc[:, 0:E], xnfmt[par][:, kc, :], S["gate_wT"][:, kc, :],
                    start=(kc == 0), stop=(kc == NKD - 1)),
                    inc=("pe" if kc == NKD - 1 else None))
            c.dve(lambda e, sc=sc: e.reduce_max(S["stat"][:, sc + 9:sc + 10],
                                                ps_sc[:, 0:E], axis=AXX),
                  waits=[("pe", st["sc"][i])])
            nx = c.dve(lambda e, sc=sc: e.tensor_scalar_mul(
                S["stat"][:, sc + 10:sc + 11], S["stat"][:, sc + 9:sc + 10], -1.0))
            st["sm"][i] = c.act(lambda e, et=et, sc=sc: e.activation(
                out=et[:], in_=ps_sc[:, 0:E], func=AF.Exp,
                bias=S["stat"][:, sc + 10:sc + 11], scale=1.0,
                accum_out=S["stat"][:, sc + 11:sc + 12]), waits=[("dve", nx)])
            c.dve(lambda e, sc=sc: e.reciprocal(S["stat"][:, sc + 9:sc + 10],
                                                S["stat"][:, sc + 11:sc + 12]),
                  waits=[("act", st["sm"][i])])
            st["pr"][i] = c.dve(lambda e, et=et, i=i, sc=sc: e.tensor_scalar_mul(
                S["probs_tm"][:, i, :], et[:], S["stat"][:, sc + 9:sc + 10]))
            c.wait("pe", "dve", st["pr"][i])
            if i >= 1:
                c.wait("pe", "dve", st["em"][i - 1])
            tv = c.on("pe", lambda e, i=i: e.transpose(
                ps_tr[0:E, 0:128], S["probs_tm"][:, i, :], S["ident_f32"][:]), inc="pe")
            st["em"][i] = c.dve(lambda e, i=i: e.tensor_copy(
                probs_em[:, i * 128:(i + 1) * 128], ps_tr[0:E, 0:128]),
                waits=[("pe", tv)])

    def ffn_phase2(st):
        st["sl"] = {}; st["gm"] = {}
        for nm in ["iota_em", "tile_em"]:
            c.wait("sync", "s_pr", c.cnt["s_pr"])
            c.dma("sync", "s_pr", lambda e, nm=nm: e.dma_start(
                out=S[nm][:], in_=dram[nm].ap()))
        iotld = c.cnt["s_pr"]
        c.wait("dve", "s_pr", iotld)
        c.wait("sync", "dve", st["em"][NT - 1])
        c.wait("sync", "s_pr", c.cnt["s_pr"])
        c.dma("sync", "s_pr", lambda e: e.dma_start(out=cc_prob_in.ap(), in_=probs_em[:]))
        c.wait("pool", "s_pr", c.cnt["s_pr"])
        c.on("pool", lambda e: e.collective_compute(
            "AllGather", ALU.bypass, replica_groups=[list(range(NCORE))],
            ins=[cc_prob_in.ap()], outs=[cc_prob_out.ap()]), inc="cc", n=1)
        c.wait("pool", "cc", c.cnt["cc"])
        c.wait("sync", "cc", c.cnt["cc"])
        for r_ in range(NCORE):
            n_, q_ = r_ // 4, r_ % 4
            for e_i in range(E):
                p0 = 64 * n_ + 8 * e_i + 2 * q_

                def mk(e, r_=r_, e_i=e_i, p0=p0):
                    src = cc_prob_out.ap()[r_ * E + e_i:r_ * E + e_i + 1, :].rearrange(
                        "one (s w) -> (one s) w", s=2)
                    return e.dma_start(out=probbis[p0:p0 + 2, :], in_=src)
                c.dma("sync", "s_bis", mk)
        gccv = c.cnt["s_bis"]
        c.dve(lambda e: e.memset(S["lo"][:], 0.0), waits=[("s_bis", gccv)])
        c.dve(lambda e: e.memset(S["hi"][:], 1.0))
        c.dve(lambda e: e.memset(S["th"][:], 0.5))
        for _ in range(NITER):
            cv = c.dve(lambda e: e.tensor_scalar(
                S["maskjunk"][:], probbis[:], S["th"][:], None, ALU.is_ge,
                op1=ALU.add, accum_out=S["cnt_s"][:]))
            c.wait("pe", "dve", cv)
            pv = c.on("pe", lambda e: e.matmul(
                ps_tr[:, 256:257], S["blockdiag8"][:], S["cnt_s"][:],
                start=True, stop=True), inc="pe")
            c.dve(lambda e: e.tensor_scalar(
                S["ge"][:], ps_tr[:, 256:257], float(KSEL), None, ALU.is_ge),
                waits=[("pe", pv)])
            c.dve(lambda e: e.tensor_sub(S["d1"][:], S["th"][:], S["lo"][:]))
            c.dve(lambda e: e.scalar_tensor_tensor(
                S["lo"][:], S["ge"][:], S["d1"][:], S["lo"][:], ALU.mult, ALU.add))
            c.dve(lambda e: e.tensor_sub(S["d1"][:], S["hi"][:], S["th"][:]))
            c.dve(lambda e: e.scalar_tensor_tensor(
                S["d2c"][:], S["ge"][:], S["d1"][:], S["th"][:], ALU.mult, ALU.add))
            c.dve(lambda e: e.tensor_copy(S["hi"][:], S["d2c"][:]))
            c.dve(lambda e: e.tensor_add(S["d1"][:], S["lo"][:], S["hi"][:]))
            c.dve(lambda e: e.tensor_scalar_mul(S["th"][:], S["d1"][:], 0.5))
        bis = c.cnt["dve"]
        c.wait("pe", "dve", bis)
        pv = c.on("pe", lambda e: e.matmul(ps_tr[0:E, 258:259], S["selmat"][:], S["lo"][:],
                                           start=True, stop=True), inc="pe")
        v = c.dve(lambda e: e.tensor_copy(S["th_em"][:], ps_tr[0:E, 258:259]),
                  waits=[("pe", pv)])
        c.wait("pe", "dve", v)
        pv = c.on("pe", lambda e: e.transpose(ps_tr[0:1, 260:260 + E], S["th_em"][:],
                                              S["ident_f32"][0:E, 0:E]), inc="pe")
        v = c.dve(lambda e: e.tensor_copy(S["th_row"][:], ps_tr[0:1, 260:260 + E]),
                  waits=[("pe", pv)])
        c.wait("pe", "dve", v)
        pv = c.on("pe", lambda e: e.matmul(ps_tr[:, 272:272 + E], S["ones_1x128_f32"][:],
                                           S["th_row"][:], start=True, stop=True), inc="pe")
        thtm = c.dve(lambda e: e.tensor_copy(S["th_tm"][:], ps_tr[:, 272:272 + E]),
                     waits=[("pe", pv)])
        c.dve(lambda e: e.tensor_scalar(
            em["mask"][:], probs_em[:], S["th_em"][:], None, ALU.is_ge))
        c.dve(lambda e: e.tensor_tensor_scan(
            em["incl"][:], em["mask"][:], em["mask"][:], 0.0, ALU.add, ALU.bypass))
        c.dve(lambda e: e.tensor_sub(em["ssel"][:], em["incl"][:], em["mask"][:]))
        c.dve(lambda e: e.scalar_tensor_tensor(
            em["d2"][:], em["ssel"][:], float(CAP - 1), S["iota_em"][:],
            ALU.add, ALU.subtract))
        c.dve(lambda e: e.tensor_scalar(em["neg"][:], em["d2"][:], 0.0, None, ALU.is_lt))
        c.dve(lambda e: e.scalar_tensor_tensor(
            em["d2"][:], em["neg"][:], 65536.0, em["d2"][:], ALU.mult, ALU.add))
        c.dve(lambda e: e.tensor_scalar(
            em["neg"][:], em["d2"][:], em["incl"][:, TCORE - 1:TCORE], None, ALU.is_lt))
        c.dve(lambda e: e.scalar_tensor_tensor(
            em["d2"][:], em["neg"][:], 65536.0, em["d2"][:], ALU.mult, ALU.add))
        c.dve(lambda e: e.tensor_sub(em["slotf"][:], em["ssel"][:], em["d2"][:]))
        c.dve(lambda e: e.tensor_mul(em["slotf"][:], em["slotf"][:], em["mask"][:]))
        c.dve(lambda e: e.tensor_add(em["slotf"][:], em["slotf"][:], em["d2"][:]))
        c.dve(lambda e: e.tensor_scalar_min(em["slotf"][:], em["slotf"][:],
                                            float(CAP)))
        c.dve(lambda e: e.tensor_scalar(em["neg"][:], em["slotf"][:], float(CAP),
                                        None, ALU.is_equal))
        c.dve(lambda e: e.tensor_mul(em["d2"][:], em["neg"][:], S["tile_em"][:]))
        slv = c.dve(lambda e: e.tensor_add(em["slotf"][:], em["slotf"][:], em["d2"][:]))
        for i in range(NT):
            c.wait("pe", "dve", slv if i == 0 else st["sl"][i - 1])
            pv = c.on("pe", lambda e, i=i: e.transpose(
                ps_tr[:, 288:288 + E], em["slotf"][:, i * 128:(i + 1) * 128],
                S["ident_f32"][0:E, 0:E]), inc="pe")
            st["sl"][i] = c.dve(lambda e, i=i: e.tensor_copy(
                S["slots_i32"][:, i, :], ps_tr[:, 288:288 + E]), waits=[("pe", pv)])
            c.dve(lambda e, i=i: e.tensor_tensor(
                S["gm_s"][:], S["probs_tm"][:, i, :], S["th_tm"][:], ALU.is_ge),
                waits=[("dve", thtm)])
            st["gm"][i] = c.dve(lambda e, i=i: e.tensor_tensor(
                payload[:, i, D:D + E], S["gm_s"][:], S["probs_tm"][:, i, :], ALU.mult))
        st["tcol"] = c.dve(lambda e: e.tensor_copy(
            payload[:, :, D + E:D + E + 2], S["thitlo"][:]))

    def ffn_phase3(st, target, target_ready, entry_act, first_call):
        first_call = first_call or not use_mirror
        st["mm1"] = {}; st["gel"] = {}; st["oev"] = {}; st["ysc"] = {}; st["sad"] = {}
        st["idxd"] = {}; st["tpv"] = {}; st["ytv"] = {}; st["gv"] = {}
        st["mir"] = {}; st["mir2"] = {}
        nwc = NFC1 + NKD
        def emit_scatter_batch_step(e_i, i):
            sem = f"g_sc{e_i}"
            if True:
                c.wait("pool", "dve", max(st["gm"][i], st["sl"][i], st["tcol"],
                                          st["xn"][i]))
                for s_, v_ in target_ready:
                    c.wait("pool", s_, v_)
                c.wait("pool", sem, c.cnt[sem])

                def mk(e, e_i=e_i, i=i):
                    return e.indirect_dma_start(
                        out=compA[e_i].ap(),
                        out_offset=IndirectOffsetOnAxis(S["slots_i32"][:, i, e_i:e_i + 1], 0),
                        in_=payload[:, i, :], in_offset=None)
                c.dma("pool", sem, mk)

        for i in range(NT):
            for e_i in range(E):
                emit_scatter_batch_step(e_i, i)
        first_pe_wait = entry_act

        def make_y_bundles(e_i):
            par = e_i % 2
            g_colv = S["g_col0"] if par == 0 else S["g_col1"]
            idxv = S["idx0"] if par == 0 else S["idx1"]
            oevall = st["oev"][(e_i, NKD - 1)]
            gv = st["gv"][e_i]
            yfme = yfm[par]
            g0 = c.cnt["g_sa"]
            for b in range(CB):
                st["sad"][(e_i, b)] = g0 + 16 * (b + 1)

            def chain(b):
                c.wait("act", "act", oevall)
                if b >= 2:
                    c.wait("act", "dve", st["ysc"][(e_i, b - 2)])
                elif e_i > 0:
                    c.wait("act", "dve", st["ysc"][(e_i - 1, CB - 1 - b)])
                tps = f"a_tp{par}{b % 2}"
                c.dma("act", tps, lambda e, b=b, yfme=yfme: e.dma_start(
                    out=ytm[b % 2][:], in_=yfme[:, b, :, :], transpose=True))
                ytv = c.cnt[tps]
                yb = yf32[b % 2]
                ysc_waits = [(tps, ytv), ("dve", gv)]
                if b >= 2:
                    ysc_waits.append(("g_sa", st["sad"][(e_i, b - 2)]))
                elif e_i > 0:
                    ysc_waits.append(("g_sa", st["sad"][(e_i - 1, CB - 1 - b)]))
                else:
                    ysc_waits.append(("g_sa", st["sa0"]))
                st["ysc"][(e_i, b)] = c.dve(
                    lambda e, b=b, g_colv=g_colv, yb=yb: e.tensor_scalar_mul(
                        yb[:], ytm[b % 2][:].rearrange("p c t -> p (c t)"),
                        g_colv[:, b:b + 1]), waits=ysc_waits)
                if b == CB - 1:
                    st["ytv"][e_i] = (c.cnt[f"a_tp{par}0"], c.cnt[f"a_tp{par}1"])

            def sadd(b):
                yb = yf32[b % 2]
                c.wait("pool", "dve", max(st["ysc"][(e_i, b)], st["idxd"][e_i]))
                c.wait("pool", "g_sa", c.cnt["g_sa"])

                def mksa(e, b=b, idxv=idxv, yb=yb):
                    return e.indirect_dma_start(
                        out=target.ap(),
                        out_offset=IndirectOffsetOnAxis(idxv[:, b:b + 1], 0),
                        in_=yb[:], in_offset=None, compute_op=ALU.add)
                got = c.dma("pool", "g_sa", mksa)
                assert got == st["sad"][(e_i, b)], (got, st["sad"][(e_i, b)])
            return ([lambda b=b: chain(b) for b in range(CB)],
                    [lambda b=b: sadd(b) for b in range(CB)])

        for e_i in range(E):
            par = e_i % 2
            scr = f"s_cr{par}"
            g_colv = S["g_col0"] if par == 0 else S["g_col1"]
            idxv = S["idx0"] if par == 0 else S["idx1"]
            trowv = S["trow0"] if par == 0 else S["trow1"]
            c.wait("sync", f"g_sc{e_i}", c.cnt[f"g_sc{e_i}"])
            if e_i >= 1:
                c.wait("sync", "dve", st["idxd"][e_i - 1])
                c.wait("sync", f"s_tp{(e_i - 1) % 2}", st["tpv"][e_i - 1])
            c.dma("sync", scr, lambda e, par=par, e_i=e_i: e.dma_start(
                out=cbuf[par][:],
                in_=compA[e_i].ap()[0:CAP, :].rearrange("(b p) w -> p b w", p=128)))
            crv = c.cnt[scr]
            st["rb"] = (scr, crv)
            if e_i >= 1:
                c.wait("sync", "pe", st["mm1"][e_i - 1])
            for b in range(CB):
                c.wait("sync", scr, crv)
                c.dma("sync", f"s_tp{par}", lambda e, par=par, b=b: e.dma_start(
                    out=xinfm[par][:, b, :, :], in_=cbuf[par][:, b, 0:D], transpose=True))
            tpv = c.cnt[f"s_tp{par}"]
            st["tpv"][e_i] = tpv
            gw = [(scr, crv)]
            if e_i >= 2:
                gw.append(("g_sa", st["sad"][(e_i - 2, CB - 1)]))
            gv = c.dve(lambda e, par=par, g_colv=g_colv, e_i=e_i: e.tensor_scalar_mul(
                g_colv[:], cbuf[par][:, :, D + e_i], 0.5), waits=gw)
            st["gv"][e_i] = gv
            c.dve(lambda e, par=par, trowv=trowv: e.scalar_tensor_tensor(
                trowv[:], cbuf[par][:, :, D + E], 64.0, cbuf[par][:, :, D + E + 1],
                ALU.mult, ALU.add))
            st["idxd"][e_i] = c.dve(
                lambda e, idxv=idxv, trowv=trowv: e.tensor_copy(idxv[:], trowv[:]))
            if e_i >= 1:
                ych, pending_y = make_y_bundles(e_i - 1)
                for f in ych:
                    f()
            else:
                pending_y = []
            for fc in range(NFC1):
                slot = (e_i * nwc + fc) % 2
                wsem = f"g_w{slot}" if first_call else f"s_w{slot}"
                if first_call:
                    c.wait("pool", "pe", st["wfree"].get(slot, 0))
                    c.wait("pool", f"s_m1{slot}", st["mir"].get(slot, 0))
                    c.dma("pool", wsem, lambda e, e_i=e_i, fc=fc, slot=slot: e.dma_start(
                        out=w1c[slot][:], in_=dram["w1"].ap()[e_i][fc]))
                    wv = c.cnt[wsem]
                    if use_mirror:
                        c.wait("sync", wsem, wv)
                        c.dma("sync", f"s_m1{slot}", lambda e, e_i=e_i, fc=fc, slot=slot:
                              e.dma_start(out=w1m.ap()[e_i][fc], in_=w1c[slot][:]))
                        st["mir"][slot] = c.cnt[f"s_m1{slot}"]
                else:
                    c.wait("sync", "pe", st["wfree"].get(slot, 0))
                    c.dma("sync", wsem, lambda e, e_i=e_i, fc=fc, slot=slot: e.dma_start(
                        out=w1c[slot][:], in_=w1m.ap()[e_i][fc]))
                    wv = c.cnt[wsem]
                c.wait("pe", wsem, wv)
                if fc == 0:
                    c.wait("pe", f"s_tp{par}", tpv)
                    if first_pe_wait:
                        c.wait("pe", "act", first_pe_wait)
                        first_pe_wait = 0
                for ft in range(4):
                    fidx = fc * 4 + ft
                    hp = psbufs[fidx % 4]
                    if ft % 2 == 0:
                        if fidx >= 4:
                            c.wait("pe", "act", st["gel"][(e_i, fidx - 3)])
                        elif e_i >= 1:
                            c.wait("pe", "act",
                                   st["oev"][(e_i - 1, 5 if fidx == 0 else 7)])
                    for kc in range(NKD):
                        for gi, (gb, gn) in enumerate(CGRP):
                            last = (kc == NKD - 1) and (gi == len(CGRP) - 1)
                            c.on("pe", lambda e, slot=slot, kc=kc, ft=ft, hp=hp,
                                 par=par, gb=gb, gn=gn: e.matmul(
                                hp[:, gb * 128:(gb + gn) * 128],
                                w1c[slot][:, kc, ft * 128:(ft + 1) * 128],
                                xinfm[par][:, gb:gb + gn, kc, :],
                                start=(kc == 0), stop=(kc == NKD - 1)),
                                inc=("pe" if last else None))
                    if ft % 2 == 1:
                        mmv = c.cnt["pe"]
                        st["gel"][(e_i, fidx - 1)] = c.act(
                            lambda e, fidx=fidx, hq=psbufs[(fidx - 1) % 4]:
                            e.activation(out=h[:, fidx - 1, :], in_=hq[:, 0:CAP],
                                         func=AF.Gelu), waits=[("pe", mmv)])
                        st["gel"][(e_i, fidx)] = c.act(
                            lambda e, fidx=fidx, hp=hp: e.activation(
                                out=h[:, fidx, :], in_=hp[:, 0:CAP], func=AF.Gelu))
                st["wfree"][slot] = c.cnt["pe"]
                if fc >= 1 and pending_y:
                    pending_y.pop(0)()
            st["mm1"][e_i] = c.cnt["pe"]
            gelall = st["gel"][(e_i, NKF - 1)]
            for dt_ in range(NKD):
                po = psbufs[dt_ % 4]
                slot = (e_i * nwc + NFC1 + dt_) % 2
                wsem = f"g_w{slot}" if first_call else f"s_w{slot}"
                if first_call:
                    c.wait("pool", "pe", st["wfree"].get(slot, 0))
                    c.wait("pool", f"s_m2{slot}", st["mir2"].get(slot, 0))
                    c.dma("pool", wsem, lambda e, e_i=e_i, dt_=dt_, slot=slot: e.dma_start(
                        out=w2c[slot][:], in_=dram["w2"].ap()[e_i][dt_]))
                    wv = c.cnt[wsem]
                    if use_mirror:
                        c.wait("sync", wsem, wv)
                        c.dma("sync", f"s_m2{slot}", lambda e, e_i=e_i, dt_=dt_, slot=slot:
                              e.dma_start(out=w2m.ap()[e_i][dt_], in_=w2c[slot][:]))
                        st["mir2"][slot] = c.cnt[f"s_m2{slot}"]
                else:
                    c.wait("sync", "pe", st["wfree"].get(slot, 0))
                    c.dma("sync", wsem, lambda e, e_i=e_i, dt_=dt_, slot=slot: e.dma_start(
                        out=w2c[slot][:], in_=w2m.ap()[e_i][dt_]))
                    wv = c.cnt[wsem]
                c.wait("pe", wsem, wv)
                if dt_ == 0:
                    c.wait("pe", "act", gelall)
                if dt_ % 2 == 0 and dt_ >= 4:
                    c.wait("pe", "act", st["oev"][(e_i, dt_ - 3)])
                if dt_ < 2 and e_i > 1:
                    c.wait("act", f"a_tp{e_i % 2}0", st["ytv"][e_i - 2][0])
                    c.wait("act", f"a_tp{e_i % 2}1", st["ytv"][e_i - 2][1])
                for fk in range(NKF):
                    for gi, (gb, gn) in enumerate(CGRP):
                        last = (fk == NKF - 1) and (gi == len(CGRP) - 1)
                        c.on("pe", lambda e, slot=slot, fk=fk, po=po, gb=gb,
                             gn=gn: e.matmul(
                            po[:, gb * 128:(gb + gn) * 128], w2c[slot][:, fk, :],
                            h[:, fk, gb * 128:(gb + gn) * 128],
                            start=(fk == 0), stop=(fk == NKF - 1)),
                            inc=("pe" if last else None))
                mm2v = c.cnt["pe"]
                st["wfree"][slot] = mm2v
                if dt_ % 2 == 1:
                    st["oev"][(e_i, dt_ - 1)] = c.act(
                        lambda e, dt_=dt_, pq=psbufs[(dt_ - 1) % 4], par=par:
                        e.activation(out=yfm[par][:, :, dt_ - 1, :],
                                     in_=pq[:, 0:CAP], func=AF.Copy),
                        waits=[("pe", mm2v)])
                    st["oev"][(e_i, dt_)] = c.act(
                        lambda e, dt_=dt_, po=po, par=par:
                        e.activation(out=yfm[par][:, :, dt_, :], in_=po[:, 0:CAP],
                                     func=AF.Copy))
            st["oevall"] = st["oev"][(e_i, NKD - 1)]
            for f in pending_y:
                f()
        ych, ysadds = make_y_bundles(E - 1)
        for f in ych:
            f()
        for f in ysadds:
            f()

    def emit_folds():
        fold_ev = 0
        c.wait("pe", "dve", c.cnt["dve"])
        c.wait("pe", "act", c.cnt["act"])

        def LD(par, dst_fn, src_fn, consumed):
            c.wait("pool", consumed[0], consumed[1])
            c.dma("pool", f"g_fa{par}", lambda e, dst_fn=dst_fn, src_fn=src_fn:
                  e.dma_start(out=dst_fn(), in_=src_fn()))
            return c.cnt[f"g_fa{par}"]

        c.dma("pool", "g_xt", lambda e: e.dma_start(
            out=S["Xtw"][:], in_=dram["xt_w"].ap().rearrange("(k p) r -> p k r", p=128)))
        def regA(mt):
            pst = [ps_a, ps_b, ps_c, ps_d][mt // 2]
            return pst[:, (mt % 2) * 512:(mt % 2) * 512 + R]
        lastmm = {0: 0, 1: 0}
        for wn, vn, outn in [("wq", "vb", "Ab"), ("wq", "vt", "At"),
                             ("wk", "wb", "Cb"), ("wk", "wt", "Ct")]:
            for kc in range(NKD):
                par = kc % 2
                lw = LD(par, (lambda par=par: fwa[par][:]),
                        (lambda wn=wn, kc=kc: dram[wn].ap()[kc * 128:(kc + 1) * 128, :]),
                        ("pe", lastmm[par]))
                lv = LD(par, (lambda par=par: fvb[par][:]),
                        (lambda vn=vn, kc=kc: dram[vn].ap()[kc * 128:(kc + 1) * 128, :]),
                        ("pe", lastmm[par]))
                c.wait("pe", f"g_fa{par}", max(lw, lv))
                if kc == 0:
                    c.wait("pe", "act", fold_ev)
                for mt in range(NKD):
                    c.on("pe", lambda e, par=par, mt=mt, kc=kc: e.matmul(
                        regA(mt), fwa[par][:, mt * 128:(mt + 1) * 128], fvb[par][:],
                        start=(kc == 0), stop=(kc == NKD - 1)),
                        inc=("pe" if mt == NKD - 1 else None))
                lastmm[par] = c.cnt["pe"]
            mmv = c.cnt["pe"]
            for mt in range(NKD):
                fold_ev = c.act(lambda e, outn=outn, mt=mt: e.activation(
                    out=S[outn][:, mt, :], in_=regA(mt), func=AF.Copy),
                    waits=[("pe", mmv)])
        c.wait("sync", "s_pr", c.cnt["s_pr"])
        c.dma("sync", "s_pr", lambda e: e.dma_start(
            out=S["xtile0"][0:1, :], in_=dram["bias_t"].ap()))
        c.wait("sync", "s_pr", c.cnt["s_pr"])
        c.dma("sync", "s_pr", lambda e: e.dma_start(
            out=S["xtile1"][0:1, :], in_=dram["bias_b"].ap()))
        bld = c.cnt["s_pr"]
        bv = c.dve(lambda e: e.scalar_tensor_tensor(
            S["xtile0"][0:1, :], S["xtile0"][0:1, :], S["alpha"][0:1, 0:1],
            S["xtile1"][0:1, :], ALU.mult, ALU.add),
            waits=[("s_sml", SML), ("s_pr", bld)])
        ev2 = 0
        for kc in range(NKD):
            c.wait("pe", "dve", bv if kc == 0 else ev2)
            pv = c.on("pe", lambda e, kc=kc: e.transpose(
                ps_tr[:, 304:305], S["xtile0"][0:1, kc * 128:(kc + 1) * 128],
                S["ident_f32"][0:1, 0:1]), inc="pe")
            ev2 = c.dve(lambda e, kc=kc: e.tensor_copy(S["bvfm"][:, kc:kc + 1],
                                                       ps_tr[:, 304:305]),
                        waits=[("pe", pv)])
        bvb = c.dve(lambda e: e.tensor_copy(S["bvfmb"][:], S["bvfm"][:]))
        def regF(rt, nh):
            return (ps_c if rt == 0 else ps_d)[:, nh * 512:(nh + 1) * 512]
        for fi, (un, outn) in enumerate([("ub", "Fb"), ("ut", "Ft")]):
            for kc in range(NKD):
                par = kc % 2
                lu = LD(par, (lambda par=par: fvb[par][:]),
                        (lambda un=un, kc=kc: dram[un].ap()[kc * 128:(kc + 1) * 128, :]),
                        ("pe", lastmm[par]))
                lw = LD(par, (lambda par=par: fwa[par][:]),
                        (lambda kc=kc: dram["woT"].ap()[kc * 128:(kc + 1) * 128, :]),
                        ("pe", lastmm[par]))
                c.wait("pe", f"g_fa{par}", max(lu, lw))
                if kc == 0:
                    c.wait("pe", "act", fold_ev)
                    c.wait("pe", "dve", bvb)
                for rt in range(2):
                    for nh in range(2):
                        last = (fi == 1) and (rt == 1) and (nh == 1)
                        c.on("pe", lambda e, par=par, rt=rt, nh=nh, kc=kc: e.matmul(
                            regF(rt, nh), fvb[par][:, rt * 128:(rt + 1) * 128],
                            fwa[par][:, nh * 512:(nh + 1) * 512],
                            start=(kc == 0), stop=(kc == NKD - 1)),
                            inc=("pe" if last else None))
                if fi == 0:
                    for nh in range(2):
                        c.on("pe", lambda e, par=par, nh=nh, kc=kc: e.matmul(
                            ps_a[0:1, nh * 512:(nh + 1) * 512], S["bvfmb"][:, kc:kc + 1],
                            fwa[par][:, nh * 512:(nh + 1) * 512],
                            start=(kc == 0), stop=(kc == NKD - 1)),
                            inc=("pe" if nh == 1 else None))
                lastmm[par] = c.cnt["pe"]
            mmv = c.cnt["pe"]
            for rt in range(2):
                for nh in range(2):
                    fold_ev = c.act(lambda e, outn=outn, rt=rt, nh=nh: e.activation(
                        out=S[outn][:, rt, nh * 512:(nh + 1) * 512], in_=regF(rt, nh),
                        func=AF.Copy), waits=[("pe", mmv)])
            if fi == 0:
                for nh in range(2):
                    fold_ev = c.act(lambda e, nh=nh: e.activation(
                        out=S["bprojT"][:, nh * 512:(nh + 1) * 512],
                        in_=ps_a[0:1, nh * 512:(nh + 1) * 512], func=AF.Copy),
                        waits=[("pe", mmv)])
        return fold_ev

    def emit_attn(st, fold_act, fold_fw):
        st["xn"] = {}; st["xf"] = {}; st["x2"] = {}
        c.wait("sync", "s_pr", c.cnt["s_pr"])
        c.dma("sync", "s_pr", lambda e: e.dma_start(
            out=S["invc_row"][:], in_=dram["invc_row"].ap()))
        invld = c.cnt["s_pr"]
        c.wait("pe", "s_pr", invld)
        c.wait("pe", "s_sml", SML)
        pv = c.on("pe", lambda e: e.matmul(
            ps_tr[:, 300:301], S["ones_1x128_f32"][:], S["alpha"][:],
            start=True, stop=True), inc="pe")
        alv = c.dve(lambda e: e.tensor_copy(S["alphab"][:], ps_tr[:, 300:301]),
                    waits=[("pe", pv)])
        bc_ev = 0
        for ch in range(NCH_A):
            if ch == 0:
                c.wait("pe", "dve", alv)
            c.wait("pe", "act", bc_ev)
            pv = c.on("pe", lambda e, ch=ch: e.matmul(
                ps_tr[:, 0:ACH], S["ones_1x128_f32"][:],
                S["invc_row"][:, ch * ACH:(ch + 1) * ACH],
                start=True, stop=True), inc="pe")
            bc_ev = c.act(lambda e, ch=ch: e.activation(
                out=at_rcf[:, ch * ACH:(ch + 1) * ACH], in_=ps_tr[:, 0:ACH],
                func=AF.Copy), waits=[("pe", pv)])
        c.dve(lambda e: e.tensor_mul(at_rcaf[:], at_rcf[:], at_rcf[:]),
              waits=[("act", bc_ev), ("dve", alv)])
        c.dve(lambda e: e.tensor_scalar_mul(at_rcaf[:], at_rcaf[:], S["alphab"][:]))
        NCH = NCH_A
        TPC = ACH // 128
        projd = {}
        evs = {}
        plist = [("Ab", "ab"), ("At", "at"), ("Cb", "cb"), ("Ct", "ct"),
                 ("Xtw", "zt")]

        def emit_ln_chunk(ch):
            cpar = ch % 2
            if ch >= 2:
                for _, on2 in plist:
                    for rt2 in range(2):
                        c.wait("sync", "act", projd[(on2, rt2, ch - 2)])
            for ti in range(TPC):
                i = ch * TPC + ti
                par = i % 2
                xt = S["xtile0"] if par == 0 else S["xtile1"]
                sx = f"s_x{par}"
                sc = 12 * par
                if i >= 2:
                    c.wait("sync", "dve", st["xn"][i - 2])
                c.wait("sync", "g_sa", st["f1_sa"])
                c.dma("sync", sx, lambda e, xt=xt, i=i: e.dma_start(
                    out=xt[:], in_=x_res.ap()[i * 128:(i + 1) * 128, :]))
                xv = c.cnt[sx]
                ew = [(f"s_tp{par}", st["xf"][i - 2])] if i >= 2 else []
                st["xn"][i] = ln_tile(st, i, xt, None, sc, sx, xv,
                                      (lambda par=par: at_xntm[par][:]), extra_waits=ew)
                c.wait("sync", "dve", st["xn"][i])
                st["xf"][i] = c.dma("sync", f"s_tp{par}",
                                    lambda e, par=par, ti=ti, cpar=cpar: e.dma_start(
                    out=at_xnfmc[cpar][:, :, ti * 128:(ti + 1) * 128],
                    in_=at_xntm[par][:], transpose=True))

        def emit_proj_chunk(ch):
            cpar = ch % 2
            xnfc = at_xnfmc[cpar]
            for k in range(10):
                wn, on = plist[k // 2]
                rt = k % 2
                kg = ch * 10 + k
                pp = psbufs[kg % 4]
                if k % 2 == 0:
                    if kg == 0:
                        c.wait("pe", "g_xt", fold_fw)
                        c.wait("pe", "act", fold_act)
                        c.wait("pe", "act", bc_ev)
                    if k == 0:
                        for ti in range(TPC):
                            i = ch * TPC + ti
                            c.wait("pe", f"s_tp{i % 2}", st["xf"][i])
                    if kg >= 4:
                        c.wait("pe", "act", evs[kg - 3])
                for kc in range(NKD):
                    c.on("pe", lambda e, wn=wn, rt=rt, kc=kc, pp=pp,
                         xnfc=xnfc: e.matmul(
                        pp[:, 0:ACH], S[wn][:, kc, rt * 128:(rt + 1) * 128],
                        xnfc[:, kc, :],
                        start=(kc == 0), stop=(kc == NKD - 1)),
                        inc=("pe" if kc == NKD - 1 else None))
                if k % 2 == 1:
                    mv = c.cnt["pe"]
                    ev0 = c.act(lambda e, on=on, ch=ch, q=psbufs[(kg - 1) % 4]:
                                e.activation(
                        out=rm[on][:, 0, ch * ACH:(ch + 1) * ACH], in_=q[:, 0:ACH],
                        func=AF.Copy), waits=[("pe", mv)])
                    ev1 = c.act(lambda e, on=on, ch=ch, pp=pp: e.activation(
                        out=rm[on][:, 1, ch * ACH:(ch + 1) * ACH], in_=pp[:, 0:ACH],
                        func=AF.Copy))
                    evs[kg - 1] = ev0
                    evs[kg] = ev1
                    projd[(on, 0, ch)] = ev0
                    projd[(on, 1, ch)] = ev1

        zscan = c.dve(lambda e: e.memset(at_carry[:], 0.0))

        def emit_scan_chunk(ch):
            for qi, on in enumerate(["cb", "ct", "zt"]):
                for rt in range(2):
                    init_ap = (
                        (lambda e, on=on, rt=rt, ch=ch:
                         rm[on][:, rt, ch * ACH - 1:ch * ACH]) if ch > 0
                        else (lambda e: at_carry[:, 0:1]))
                    c.dve(lambda e, on=on, rt=rt, ch=ch, init_ap=init_ap:
                          e.tensor_tensor_scan(
                        rm[on][:, rt, ch * ACH:(ch + 1) * ACH],
                        rm[on][:, rt, ch * ACH:(ch + 1) * ACH],
                        rm[on][:, rt, ch * ACH:(ch + 1) * ACH],
                        init_ap(e), ALU.add, ALU.bypass),
                        waits=[("act", projd[(on, rt, ch)]), ("dve", zscan)])

        emit_ln_chunk(0)
        for ch in range(NCH):
            if ch + 1 < NCH:
                emit_ln_chunk(ch + 1)
            emit_proj_chunk(ch)
            emit_scan_chunk(ch)

        def wait_proj(eng, names):
            for on in names:
                for rt in range(2):
                    for ch in range(NCH):
                        c.wait(eng, "act", projd[(on, rt, ch)])
        c.dve(lambda e: e.memset(at_car_loc[:], 0.0))
        for qi, on in enumerate(["cb", "ct", "zt"]):
            for rt in range(2):
                c.dve(lambda e, on=on, rt=rt, qi=qi: e.tensor_copy(
                    at_car_loc[:, qi * 2 + rt:qi * 2 + rt + 1],
                    rm[on][:, rt, TCORE - 1:TCORE]))
        tot = c.cnt["dve"]
        c.wait("sync", "dve", tot)
        c.wait("sync", "s_pr", c.cnt["s_pr"])
        c.dma("sync", "s_pr", lambda e: e.dma_start(out=cc_car_in.ap(), in_=at_car_loc[:]))
        c.wait("pool", "s_pr", c.cnt["s_pr"])
        c.on("pool", lambda e: e.collective_compute(
            "AllGather", ALU.bypass, replica_groups=[list(range(NCORE))],
            ins=[cc_car_in.ap()], outs=[cc_car_out.ap()]), inc="cc", n=1)
        c.wait("pool", "cc", c.cnt["cc"])
        c.dma("pool", "g_cc", lambda e: e.dma_start(
            out=at_car_all[:], in_=cc_car_out.ap().rearrange("(r p) s -> p r s", p=128)))
        carv = c.cnt["g_cc"]
        c.dve(lambda e: e.memset(at_carry[:], 0.0), waits=[("g_cc", carv)])
        for r_ in range(NCORE):
            c.dve(lambda e, r_=r_: e.scalar_tensor_tensor(
                at_carry[:], at_car_all[:, r_, :], S["wvec"][:, r_:r_ + 1], at_carry[:],
                ALU.mult, ALU.add))
        card = c.cnt["dve"]
        for qi, on in enumerate(["cb", "ct", "zt"]):
            for rt in range(2):
                c.dve(lambda e, on=on, rt=rt, qi=qi: e.tensor_scalar_add(
                    rm[on][:, rt, :], rm[on][:, rt, :],
                    at_carry[:, qi * 2 + rt:qi * 2 + rt + 1]),
                    waits=[("dve", card)])
        wait_proj("dve", ["ab", "at"])
        for rt in range(2):
            c.dve(lambda e, rt=rt: e.tensor_mul(rm["ab"][:, rt, :], rm["ab"][:, rt, :],
                                                at_rcf[:]), waits=[("act", bc_ev)])
            c.dve(lambda e, rt=rt: e.tensor_mul(rm["ab"][:, rt, :], rm["ab"][:, rt, :],
                                                rm["cb"][:, rt, :]))
            c.dve(lambda e, rt=rt: e.tensor_mul(rm["at"][:, rt, :], rm["at"][:, rt, :],
                                                at_rcaf[:]))
            c.dve(lambda e, rt=rt: e.tensor_mul(rm["at"][:, rt, :], rm["at"][:, rt, :],
                                                rm["zt"][:, rt, :]))
            c.dve(lambda e, rt=rt: e.tensor_mul(rm["at"][:, rt, :], rm["at"][:, rt, :],
                                                rm["ct"][:, rt, :]))
        intd = c.cnt["dve"]
        for i in range(NT):
            par = i % 2
            po = ps_c if par == 0 else ps_d
            xt = S["xtile0"] if par == 0 else S["xtile1"]
            c.wait("sync", "dve", st["x2"].get(i - 2, 0))
            c.dma("sync", f"s_x{par}", lambda e, xt=xt, i=i: e.dma_start(
                out=xt[:], in_=x_res.ap()[i * 128:(i + 1) * 128, :]))
            xv = c.cnt[f"s_x{par}"]
            c.wait("pe", "dve", intd)
            c.wait("pe", "dve", st["x2"].get(i - 2, 0))
            first = True
            for srcn, fold in [("ab", "Fb"), ("at", "Ft")]:
                for rt in range(2):
                    for nh in range(2):
                        c.on("pe", lambda e, srcn=srcn, fold=fold, rt=rt, nh=nh, i=i,
                             po=po, first=first: e.matmul(
                            po[:, nh * 512:(nh + 1) * 512],
                            rm[srcn][:, rt, i * 128:(i + 1) * 128],
                            S[fold][:, rt, nh * 512:(nh + 1) * 512],
                            start=first, stop=False))
                    first = False
            for nh in range(2):
                c.on("pe", lambda e, nh=nh, po=po: e.matmul(
                    po[:, nh * 512:(nh + 1) * 512], S["ones_1x128_bf16"][:],
                    S["bprojT"][:, nh * 512:(nh + 1) * 512], start=False, stop=True),
                    inc=("pe" if nh == 1 else None))
            mv = c.cnt["pe"]
            for nh in range(2):
                st["x2"][i] = c.dve(lambda e, xt=xt, nh=nh, po=po: e.tensor_add(
                    xt[:, nh * 512:(nh + 1) * 512], xt[:, nh * 512:(nh + 1) * 512],
                    po[:, nh * 512:(nh + 1) * 512]),
                    waits=([("pe", mv), (f"s_x{par}", xv)] if nh == 0 else ()))
            c.wait("sync", "dve", st["x2"][i])
            c.dma("sync", "s_st", lambda e, xt=xt, i=i: e.dma_start(
                out=y_out.ap()[i * 128:(i + 1) * 128, :], in_=xt[:]))
        return c.cnt["s_st"]

    st1 = {"wfree": {}, "sa0": 0}
    ffn_phase1(st1, x_res, [("g_pr", GPR), ("s_sml", SML)])
    ffn_phase2(st1)
    fold_act = emit_folds()
    fold_fw = c.cnt["g_xt"]
    if phases == "f1":
        c.wait("sync", "dve", c.cnt["dve"])
        c.dma("sync", "s_st", lambda e: e.dma_start(out=y_out.ap(), in_=x_res.ap()))
        c.wait("sync", "s_st", c.cnt["s_st"])
        c.emit(); es.close(); return nc
    ffn_phase3(st1, x_res, [("g_pr", GPR)], entry_act=fold_act, first_call=True)
    st1["f1_sa"] = c.cnt["g_sa"]
    if phases == "f1x":
        c.wait("sync", "g_sa", c.cnt["g_sa"])
        c.dma("sync", "s_st", lambda e: e.dma_start(out=y_out.ap(), in_=x_res.ap()))
        c.wait("sync", "s_st", c.cnt["s_st"])
        c.emit(); es.close(); return nc

    c.xbar(extra=[("g_sa", c.cnt["g_sa"]), ("s_cr0", c.cnt["s_cr0"]),
                  ("s_cr1", c.cnt["s_cr1"])])
    sta = {"f1_sa": st1["f1_sa"]}
    wb_done = emit_attn(sta, fold_act, fold_fw)
    if phases == "f1a":
        c.wait("sync", "s_st", c.cnt["s_st"])
        c.wait("sync", "g_sa", c.cnt["g_sa"])
        c.emit(); es.close(); return nc

    clear2 = [(f"g_sc{e_i}", c.cnt[f"g_sc{e_i}"]) for e_i in range(E)]
    c.xbar(extra=clear2 + [("g_sa", c.cnt["g_sa"]), ("s_st", wb_done),
                           ("s_cr0", c.cnt["s_cr0"]), ("s_cr1", c.cnt["s_cr1"])])
    st2 = {"wfree": dict(st1["wfree"]), "sa0": c.cnt["g_sa"]}
    ffn_phase1(st2, y_out, [("s_st", wb_done)])
    ffn_phase2(st2)
    ffn_phase3(st2, y_out, [("s_st", wb_done)], entry_act=0, first_call=False)

    c.wait("sync", "g_sa", c.cnt["g_sa"])
    c.wait("sync", "s_st", c.cnt["s_st"])
    c.emit()
    es.close()
    return nc


def _make_consts(TCORE, core):
    NT = TCORE // 128
    n, q = core // 4, core % 4
    cons = {}
    cons["ident_f32"] = np.eye(128, dtype=np.float32)
    bd = np.zeros((128, 128), np.float32)
    for g in range(16):
        bd[g * 8:(g + 1) * 8, g * 8:(g + 1) * 8] = 1.0
    cons["blockdiag8"] = bd
    cons["ones_1x128_f32"] = np.ones((1, 128), np.float32)
    cons["ones_1x128_bf16"] = np.ones((1, 128), np.float32).astype(ml_dtypes.bfloat16)
    cons["iota_em"] = np.broadcast_to(
        np.arange(TCORE, dtype=np.float32)[None, :], (E, TCORE)).copy()
    cons["tile_em"] = np.broadcast_to(
        (np.arange(TCORE) // 128).astype(np.float32)[None, :], (E, TCORE)).copy()
    t = np.arange(TCORE)
    thitlo = np.stack([t // 64, t % 64], -1).reshape(NT, 128, 2).transpose(1, 0, 2)
    cons["thitlo"] = np.ascontiguousarray(thitlo.astype(np.float32)).astype(ml_dtypes.bfloat16)
    gidx = q * TCORE + t
    cons["invc_row"] = (1.0 / (gidx + 1).astype(np.float64)).astype(np.float32)[None, :]
    sel = np.zeros((128, E), np.float32)
    for e_i in range(E):
        sel[64 * n + 8 * e_i, e_i] = 1.0
    cons["selmat"] = sel
    wv = np.zeros((128, E), np.float32)
    for r_ in range(4 * n, 4 * n + q):
        wv[:, r_] = 1.0
    cons["wvec"] = wv
    return cons


def _shared_weights(Wq, Wk, Wo, U_b, V_b, W_b, bias_b, U_t, V_t, W_t, X_t, bias_t,
                    alpha, gate_w, w1, w2):
    return {
        "wq": np.ascontiguousarray(Wq, np.float32),
        "wk": np.ascontiguousarray(Wk, np.float32),
        "vb": np.ascontiguousarray(V_b, np.float32),
        "vt": np.ascontiguousarray(V_t, np.float32),
        "wb": np.ascontiguousarray(W_b, np.float32),
        "wt": np.ascontiguousarray(W_t, np.float32),
        "xt_w": np.ascontiguousarray(X_t, np.float32),
        "ub": np.ascontiguousarray(U_b, np.float32),
        "ut": np.ascontiguousarray(U_t, np.float32),
        "woT": np.ascontiguousarray(np.asarray(Wo, np.float32).T),
        "bias_b": np.asarray(bias_b, np.float32).reshape(1, D),
        "bias_t": np.asarray(bias_t, np.float32).reshape(1, D),
        "alpha": np.asarray(alpha, np.float32).reshape(1, 1),
        "gate_wT": np.ascontiguousarray(np.asarray(gate_w, np.float32).T),
        "w1": np.ascontiguousarray(
            np.asarray(w1, np.float32).reshape(8, 8, 128, 8, 512)
            .transpose(0, 3, 2, 1, 4)),
        "w2": np.ascontiguousarray(
            np.asarray(w2, np.float32).reshape(8, 32, 128, 8, 128)
            .transpose(0, 3, 2, 1, 4)),
    }


def run_cores(x, shared, TCORE, phases="all", sim=False, use_mirror=False):
    nc = build_nc(TCORE=TCORE, phases=phases, use_mirror=use_mirror)
    xf = np.ascontiguousarray(x.reshape(NCORE, TCORE, D))
    in_maps = []
    for core in range(NCORE):
        m = dict(shared)
        m["xin"] = np.ascontiguousarray(xf[core])
        m.update(_make_consts(TCORE, core))
        in_maps.append(m)
    if sim:
        from concourse import bass_interp
        s = bass_interp.MultiCoreSim(nc, NCORE)
        for core in range(NCORE):
            for k, v in in_maps[core].items():
                s.cores[core].tensor(k)[:] = v
        s.simulate()
        outs = [np.array(s.cores[core].mem_tensor("y_out")) for core in range(NCORE)]
    else:
        res = run_bass_kernel_spmd(nc, in_maps, core_ids=list(range(NCORE)))
        outs = [res.results[core]["y_out"] for core in range(NCORE)]
    return np.stack(outs, 0)


def kernel(L0_attn_x, L0_attn_Wq, L0_attn_Wk, L0_attn_Wo, L0_attn_U_b, L0_attn_V_b,
           L0_attn_W_b, L0_attn_bias_b, L0_attn_U_t, L0_attn_V_t, L0_attn_W_t,
           L0_attn_X_t, L0_attn_bias_t, L0_attn_alpha, L0_ffn_gate_w, L0_ffn_w1,
           L0_ffn_w2):
    x = np.ascontiguousarray(np.asarray(L0_attn_x, dtype=np.float32))
    B, T, _ = x.shape
    TCORE = (B * T) // NCORE
    shared = _shared_weights(L0_attn_Wq, L0_attn_Wk, L0_attn_Wo, L0_attn_U_b,
                             L0_attn_V_b, L0_attn_W_b, L0_attn_bias_b, L0_attn_U_t,
                             L0_attn_V_t, L0_attn_W_t, L0_attn_X_t, L0_attn_bias_t,
                             L0_attn_alpha, L0_ffn_gate_w, L0_ffn_w1, L0_ffn_w2)
    out = run_cores(x, shared, TCORE, phases="all", sim=False)
    return out.reshape(B, T, D)


# version 23 (final)
# speedup vs baseline: 1.0396x; 1.0134x over previous
import numpy as np
import ml_dtypes
import concourse.bass as bass
import concourse.mybir as mybir
from concourse.bass import IndirectOffsetOnAxis
from concourse.bass_utils import run_bass_kernel_spmd

F32 = mybir.dt.float32
BF16 = mybir.dt.bfloat16
I32 = mybir.dt.int32
AF = mybir.ActivationFunctionType
ALU = mybir.AluOpType
AXX = mybir.AxisListType.X

NCORE = 8
D = 1024
R = 256
E = 8
F = 4096
EPS = 1e-5
NITER = 32
NKD = D // 128
NKF = F // 128
ENGS = ("pe", "dve", "act", "pool", "sync")


class Ctx:

    def __init__(self, nc, es):
        self.nc = nc
        self.es = es
        self.ops = {e: [] for e in ENGS}
        self.sem = {}
        self.cnt = {}

    def new_sem(self, name):
        self.sem[name] = self.es.enter_context(self.nc.semaphore(name))
        self.cnt[name] = 0

    def on(self, eng, fn, inc=None, n=1):
        if inc is not None:
            self.cnt[inc] += n
            sem = self.sem[inc]

            def run(e, fn=fn, sem=sem, n=n):
                fn(e).then_inc(sem, n)
            self.ops[eng].append(run)
            return self.cnt[inc]
        self.ops[eng].append(lambda e, fn=fn: fn(e))
        return None

    def dma(self, eng, sem_name, fn):
        return self.on(eng, fn, inc=sem_name, n=16)

    def wait(self, eng, sem_name, val):
        if val is None or val <= 0:
            return
        sem = self.sem[sem_name]
        self.ops[eng].append(lambda e, sem=sem, val=val: e.wait_ge(sem, val))

    def dve(self, fn, waits=()):
        for s, v in waits:
            self.wait("dve", s, v)
        self.wait("dve", "dve", self.cnt["dve"])
        return self.on("dve", fn, inc="dve")

    def act(self, fn, waits=()):
        for s, v in waits:
            self.wait("act", s, v)
        self.wait("act", "act", self.cnt["act"])
        return self.on("act", fn, inc="act")

    def xbar(self, extra=()):
        tots = [(s, v) for s, v in self.cnt.items() if v > 0]
        tots += list(extra)
        for eng in ENGS:
            for s, v in tots:
                if eng == {"pe": "pe", "dve": "dve", "act": "act"}.get(s, None):
                    continue
                self.wait(eng, s, v)

    def emit(self):
        blk = self.es.enter_context(self.nc.Block())
        blk.tensor(lambda e: [f(e) for f in self.ops["pe"]])
        blk.vector(lambda e: [f(e) for f in self.ops["dve"]])
        blk.scalar(lambda e: [f(e) for f in self.ops["act"]])
        blk.gpsimd(lambda e: [f(e) for f in self.ops["pool"]])
        blk.sync(lambda e: [f(e) for f in self.ops["sync"]])


def build_nc(TCORE=2048, phases="all", use_mirror=False):
    from contextlib import ExitStack
    NT = TCORE // 128
    TGLOB = 4 * TCORE
    KSEL = TGLOB // 4
    SEGW = TGLOB // 8
    CAP = ((KSEL // 4) * 5 // 4 + 127) // 128 * 128
    CB = CAP // 128
    PW = 1040
    FC1 = 512
    NFC1 = F // FC1

    ACH = min(512, TCORE)
    NCH_A = TCORE // ACH
    CGRP = []
    b0 = 0
    while b0 < CB:
        nb = min(4, CB - b0)
        CGRP.append((b0, nb))
        b0 += nb

    nc = bass.Bass("TRN2", target_bir_lowering=False, debug=False)
    es = ExitStack()
    c = Ctx(nc, es)

    dram = {}

    def din(name, shape, dt=F32):
        dram[name] = nc.dram_tensor(name, list(shape), dt, kind="ExternalInput")

    din("xin", [TCORE, D])
    for nm in ["wq", "wk"]:
        din(nm, [D, D])
    for nm in ["vb", "vt", "wb", "wt", "xt_w", "ub", "ut"]:
        din(nm, [D, R])
    din("woT", [D, D])
    din("bias_b", [1, D]); din("bias_t", [1, D]); din("alpha", [1, 1])
    din("gate_wT", [D, E])
    din("w1", [E, NFC1, 128, NKD, FC1]); din("w2", [E, NKD, 128, NKF, 128])
    din("ident_f32", [128, 128])
    din("blockdiag8", [128, 128])
    din("ones_1x128_f32", [1, 128])
    din("ones_1x128_bf16", [1, 128], BF16)
    din("iota_em", [E, TCORE])
    din("tile_em", [E, TCORE])
    din("thitlo", [128, NT, 2], BF16)
    din("invc_row", [1, TCORE])
    din("selmat", [128, E])
    din("wvec", [128, E])

    y_out = nc.dram_tensor("y_out", [TCORE, D], F32, kind="ExternalOutput")
    w1m = nc.dram_tensor("w1m", [E, NFC1, 128, NKD, FC1], BF16)
    w2m = nc.dram_tensor("w2m", [E, NKD, 128, NKF, 128], BF16)

    x_res = nc.dram_tensor("x_res", [TCORE, D], F32)
    compA = [nc.dram_tensor(f"compA{e_i}", [CAP + NT, PW], BF16) for e_i in range(E)]
    cc_prob_in = nc.dram_tensor("cc_prob_in", [E, TCORE], F32)
    cc_prob_out = nc.dram_tensor("cc_prob_out", [NCORE * E, TCORE], F32)
    cc_car_in = nc.dram_tensor("cc_car_in", [128, 8], F32)
    cc_car_out = nc.dram_tensor("cc_car_out", [NCORE * 128, 8], F32)

    for s in ["pe", "dve", "act", "cc",
              "s_sml", "s_x0", "s_x1", "s_tp0", "s_tp1", "s_cr0", "s_cr1",
              "s_st", "s_pr", "s_bis", "s_m10", "s_m11", "s_m20", "s_m21", "s_w0", "s_w1",
              "a_tp00", "a_tp01", "a_tp10", "a_tp11",
              "g_pr", "g_gt", "g_w0", "g_w1", "g_w2", "g_sa", "g_cc",
              "g_fa0", "g_fa1", "g_xt"]:
        c.new_sem(s)
    for e_i in range(E):
        c.new_sem(f"g_sc{e_i}")

    S = {}

    def sb(name, shape, dt=F32):
        S[name] = es.enter_context(nc.sbuf_tensor("sb_" + name, list(shape), dt))
        return S[name]

    sb("ident_f32", [128, 128])
    sb("blockdiag8", [128, 128])
    sb("ones_1x128_f32", [1, 128]); sb("ones_1x128_bf16", [1, 128], BF16)
    sb("gate_wT", [128, NKD, E], BF16)
    sb("selmat", [128, E]); sb("wvec", [128, E])
    sb("thitlo", [128, NT, 2], BF16)
    sb("alpha", [1, 1])
    sb("rstd_all", [128, NT])
    sb("probs_tm", [128, NT, E])
    sb("slots_i32", [128, NT, E], I32)
    sb("th_tm", [128, E]); sb("th_em", [E, 1]); sb("th_row", [1, E])
    sb("xtile0", [128, D]); sb("xtile1", [128, D])
    sb("stat", [128, 24])
    sb("et0", [128, E]); sb("et1", [128, E])
    sb("gm_s", [128, E])
    sb("lo", [128, 1]); sb("hi", [128, 1]); sb("th", [128, 1])
    sb("cnt_s", [128, 1]); sb("ge", [128, 1]); sb("d1", [128, 1]); sb("d2c", [128, 1])
    sb("Ab", [128, NKD, R], BF16); sb("At", [128, NKD, R], BF16)
    sb("Cb", [128, NKD, R], BF16); sb("Ct", [128, NKD, R], BF16)
    sb("Xtw", [128, NKD, R], BF16)
    sb("Fb", [128, 2, D], BF16); sb("Ft", [128, 2, D], BF16)
    sb("bvfm", [128, NKD])
    sb("bvfmb", [128, NKD], BF16)
    sb("bprojT", [1, D], BF16)
    sb("g_col0", [128, CB]); sb("g_col1", [128, CB])
    sb("idx0", [128, CB], I32); sb("idx1", [128, CB], I32)
    sb("trow0", [128, CB]); sb("trow1", [128, CB])
    payload = sb("payload", [128, NT, PW], BF16)
    xnfmt = [sb("xnfmt0", [128, NKD, 128], BF16), sb("xnfmt1", [128, NKD, 128], BF16)]
    fwa = [sb("fwa0", [128, 1024], BF16), sb("fwa1", [128, 1024], BF16)]
    fvb = [sb("fvb0", [128, 256], BF16), sb("fvb1", [128, 256], BF16)]
    sb("alphab", [128, 1])
    at_car_loc = sb("at_car_loc", [128, 8])
    at_car_all = sb("at_car_all", [128, NCORE, 8])
    at_carry = sb("at_carry", [128, 8])

    def ov_plan(items):
        offs, cur = {}, 0
        for name, shape, dt in items:
            nbytes = int(np.prod(shape[1:])) * (2 if dt in (BF16, mybir.dt.float16) else 4)
            nbytes = (nbytes + 31) // 32 * 32
            offs[name] = cur
            cur += nbytes
        return cur, offs

    o1a = [("probs_em", [E, TCORE], F32), ("em_incl", [E, TCORE], F32),
           ("em_ssel", [E, TCORE], F32), ("em_d2", [E, TCORE], F32),
           ("em_mask", [E, TCORE], BF16), ("em_neg", [E, TCORE], BF16),
           ("probbis", [128, SEGW], F32), ("iota_em", [E, TCORE], F32),
           ("tile_em", [E, TCORE], F32), ("maskjunk", [128, SEGW], BF16)]
    o1b = [("h", [128, NKF, CAP], BF16)]
    o2a = [("cbuf0", [128, CB, PW], BF16),
           ("xinfm0", [128, CB, NKD, 128], BF16),
           ("w1c0", [128, NKD, FC1], BF16), ("w1c1", [128, NKD, FC1], BF16),
           ("w2c0", [128, NKF, 128], BF16), ("w2c1", [128, NKF, 128], BF16),
           ("yfm0", [128, CB, NKD, 128], BF16), ("yfm1", [128, CB, NKD, 128], BF16),
           ("ytm0", [128, NKD, 128], BF16), ("ytm1", [128, NKD, 128], BF16),
           ("yf0", [128, D], F32), ("yf1", [128, D], F32)]
    ACHW = min(512, TCORE)
    o2b = [("at_xnfmc0", [128, NKD, ACHW], BF16), ("at_xnfmc1", [128, NKD, ACHW], BF16),
           ("rm_ab", [128, 2, TCORE], BF16), ("rm_at", [128, 2, TCORE], BF16),
           ("rm_cb", [128, 2, TCORE], BF16), ("rm_ct", [128, 2, TCORE], BF16),
           ("rm_zt", [128, 2, TCORE], BF16),
           ("at_rcf", [128, TCORE], BF16), ("at_rcaf", [128, TCORE], BF16),
           ("at_xntm0", [128, D], BF16), ("at_xntm1", [128, D], BF16),
           ("invc_row", [1, TCORE], F32)]
    p3 = o1b + o2a
    sz_p2, off_p2 = ov_plan(o1a)
    sz_p3, off_p3 = ov_plan(p3)
    sz_at, off_at = ov_plan(o2b)
    REG = max(sz_p2, sz_p3, sz_at)
    top = nc.sbuf_top
    reg_base = (top - REG) // 32 * 32
    assert reg_base >= nc.sbuf_base, (
        f"SBUF overlay overflow: reg_base={reg_base} auto_base={nc.sbuf_base} "
        f"sizes p2={sz_p2} p3={sz_p3} at={sz_at}")
    for items, offs in [(o1a, off_p2), (p3, off_p3), (o2b, off_at)]:
        for name, shape, dt in items:
            S[name] = nc.alloc_sbuf_tensor_at("ov_" + name, list(shape), dt,
                                              offset=reg_base + offs[name])
    probs_em = S["probs_em"]
    probbis = S["probbis"]
    em = {"mask": S["em_mask"], "incl": S["em_incl"], "ssel": S["em_ssel"],
          "d2": S["em_d2"], "neg": S["em_neg"], "slotf": S["em_ssel"]}
    cbuf = [S["cbuf0"], S["cbuf0"]]
    xinfm = [S["xinfm0"], S["xinfm0"]]
    h = S["h"]
    w1c = [S["w1c0"], S["w1c1"]]
    w2c = [S["w2c0"], S["w2c1"]]
    yfm = [S["yfm0"], S["yfm1"]]
    ytm = [S["ytm0"], S["ytm1"]]
    yf32 = [S["yf0"], S["yf1"]]
    at_xnfmc = [S["at_xnfmc0"], S["at_xnfmc1"]]
    rm = {n: S["rm_" + n] for n in ["ab", "at", "cb", "ct", "zt"]}
    at_rcf = S["at_rcf"]
    at_rcaf = S["at_rcaf"]
    at_xntm = [S["at_xntm0"], S["at_xntm1"]]

    ps_a = es.enter_context(nc.psum_tensor("ps_a", [128, 1024], F32))
    ps_b = es.enter_context(nc.psum_tensor("ps_b", [128, 1024], F32))
    ps_c = es.enter_context(nc.psum_tensor("ps_c", [128, 1024], F32))
    ps_d = es.enter_context(nc.psum_tensor("ps_d", [128, 1024], F32))
    ps_sc = ps_c
    ps_tr = ps_d
    psbufs = [ps_a, ps_b, ps_c, ps_d]

    for sbn in ["ident_f32", "blockdiag8", "ones_1x128_f32", "ones_1x128_bf16",
                "selmat", "wvec", "thitlo", "alpha"]:
        c.dma("sync", "s_sml", lambda e, sbn=sbn: e.dma_start(
            out=S[sbn][:], in_=dram[sbn].ap()))
    SML = c.cnt["s_sml"]
    c.dma("pool", "g_gt", lambda e: e.dma_start(
        out=S["gate_wT"][:], in_=dram["gate_wT"].ap().rearrange("(k p) e -> p k e", p=128)))
    c.dma("pool", "g_pr", lambda e: e.dma_start(out=x_res.ap(), in_=dram["xin"].ap()))
    GPR = c.cnt["g_pr"]

    c.dve(lambda e: e.memset(payload[:, :, D + E + 2:PW], 0.0))

    def ln_tile(st, i, xt, sq, sc, sx, xv, out_ap_fn, extra_waits=()):
        del sq
        """Emit LN for tile i: stats + normalized bf16 tile via out_ap_fn."""
        c.dve(lambda e, xt=xt, sc=sc: e.reduce_sum(S["stat"][:, sc:sc + 1], xt[:], axis=AXX),
              waits=[(sx, xv)])
        c.dve(lambda e, sc=sc: e.tensor_scalar_mul(S["stat"][:, sc + 1:sc + 2],
                                                   S["stat"][:, sc:sc + 1], 1.0 / D))
        c.dve(lambda e, xt=xt, sc=sc: e.scalar_tensor_tensor(
            out_ap_fn(), xt[:], 1.0, xt[:], ALU.mult, ALU.mult,
            accum_out=S["stat"][:, sc + 2:sc + 3]), waits=extra_waits)
        c.dve(lambda e, sc=sc: e.tensor_scalar_mul(S["stat"][:, sc + 3:sc + 4],
                                                   S["stat"][:, sc + 2:sc + 3], 1.0 / D))
        c.dve(lambda e, sc=sc: e.tensor_scalar(
            S["stat"][:, sc + 4:sc + 5], S["stat"][:, sc + 1:sc + 2],
            S["stat"][:, sc + 1:sc + 2], None, ALU.mult))
        c.dve(lambda e, sc=sc: e.tensor_sub(S["stat"][:, sc + 5:sc + 6],
                                            S["stat"][:, sc + 3:sc + 4],
                                            S["stat"][:, sc + 4:sc + 5]))
        c.dve(lambda e, sc=sc: e.tensor_scalar_add(S["stat"][:, sc + 6:sc + 7],
                                                   S["stat"][:, sc + 5:sc + 6], EPS))
        rv = c.dve(lambda e, sc=sc: e.reciprocal(S["stat"][:, sc + 7:sc + 8],
                                                 S["stat"][:, sc + 6:sc + 7]))
        rs = c.act(lambda e, i=i, sc=sc: e.activation(
            out=S["rstd_all"][:, i:i + 1], in_=S["stat"][:, sc + 7:sc + 8], func=AF.Sqrt),
            waits=[("dve", rv)])
        nm = c.dve(lambda e, i=i, sc=sc: e.tensor_scalar_mul(
            S["stat"][:, sc + 8:sc + 9], S["stat"][:, sc + 1:sc + 2], -1.0),
            waits=[("act", rs)])
        return c.dve(lambda e, xt=xt, i=i, sc=sc: e.scalar_tensor_tensor(
            out_ap_fn(), xt[:], S["stat"][:, sc + 8:sc + 9],
            S["rstd_all"][:, i:i + 1].to_broadcast([128, D]), ALU.add, ALU.mult),
            waits=extra_waits)

    def ffn_phase1(st, base, base_ready):
        st["xn"] = {}; st["sc"] = {}; st["sm"] = {}; st["pr"] = {}; st["em"] = {}
        for i in range(NT):
            par = i % 2
            xt = S["xtile0"] if par == 0 else S["xtile1"]
            sq = None
            et = S["et0"] if par == 0 else S["et1"]
            sx = f"s_x{par}"
            sc = 12 * par
            if i >= 2:
                c.wait("sync", "dve", st["xn"][i - 2])
            for s_, v_ in base_ready:
                c.wait("sync", s_, v_)
            c.dma("sync", sx, lambda e, xt=xt, i=i: e.dma_start(
                out=xt[:], in_=base.ap()[i * 128:(i + 1) * 128, :]))
            xv = c.cnt[sx]
            if i >= 2:
                c.wait("dve", "act", st["sm"][i - 2])
            st["xn"][i] = ln_tile(st, i, xt, sq, sc, sx, xv,
                                  (lambda i=i: payload[:, i, 0:D]))
            stp = f"s_tp{par}"
            if i >= 2:
                c.wait("sync", "pe", st["sc"][i - 2])
            c.wait("sync", "dve", st["xn"][i])
            c.dma("sync", stp, lambda e, par=par, i=i: e.dma_start(
                out=xnfmt[par][:], in_=payload[:, i, 0:D], transpose=True))
            tpv = c.cnt[stp]
            c.wait("pe", stp, tpv)
            c.wait("pe", "g_gt", 16)
            if i >= 1:
                c.wait("pe", "act", st["sm"][i - 1])
            for kc in range(NKD):
                st["sc"][i] = c.on("pe", lambda e, par=par, kc=kc: e.matmul(
                    ps_sc[:, 0:E], xnfmt[par][:, kc, :], S["gate_wT"][:, kc, :],
                    start=(kc == 0), stop=(kc == NKD - 1)),
                    inc=("pe" if kc == NKD - 1 else None))
            c.dve(lambda e, sc=sc: e.reduce_max(S["stat"][:, sc + 9:sc + 10],
                                                ps_sc[:, 0:E], axis=AXX),
                  waits=[("pe", st["sc"][i])])
            nx = c.dve(lambda e, sc=sc: e.tensor_scalar_mul(
                S["stat"][:, sc + 10:sc + 11], S["stat"][:, sc + 9:sc + 10], -1.0))
            st["sm"][i] = c.act(lambda e, et=et, sc=sc: e.activation(
                out=et[:], in_=ps_sc[:, 0:E], func=AF.Exp,
                bias=S["stat"][:, sc + 10:sc + 11], scale=1.0,
                accum_out=S["stat"][:, sc + 11:sc + 12]), waits=[("dve", nx)])
            c.dve(lambda e, sc=sc: e.reciprocal(S["stat"][:, sc + 9:sc + 10],
                                                S["stat"][:, sc + 11:sc + 12]),
                  waits=[("act", st["sm"][i])])
            st["pr"][i] = c.dve(lambda e, et=et, i=i, sc=sc: e.tensor_scalar_mul(
                S["probs_tm"][:, i, :], et[:], S["stat"][:, sc + 9:sc + 10]))
            c.wait("pe", "dve", st["pr"][i])
            if i >= 1:
                c.wait("pe", "dve", st["em"][i - 1])
            tv = c.on("pe", lambda e, i=i: e.transpose(
                ps_tr[0:E, 0:128], S["probs_tm"][:, i, :], S["ident_f32"][:]), inc="pe")
            st["em"][i] = c.dve(lambda e, i=i: e.tensor_copy(
                probs_em[:, i * 128:(i + 1) * 128], ps_tr[0:E, 0:128]),
                waits=[("pe", tv)])

    def ffn_phase2(st):
        st["sl"] = {}; st["gm"] = {}
        for nm in ["iota_em", "tile_em"]:
            c.wait("sync", "s_pr", c.cnt["s_pr"])
            c.dma("sync", "s_pr", lambda e, nm=nm: e.dma_start(
                out=S[nm][:], in_=dram[nm].ap()))
        iotld = c.cnt["s_pr"]
        c.wait("dve", "s_pr", iotld)
        c.wait("sync", "dve", st["em"][NT - 1])
        c.wait("sync", "s_pr", c.cnt["s_pr"])
        c.dma("sync", "s_pr", lambda e: e.dma_start(out=cc_prob_in.ap(), in_=probs_em[:]))
        c.wait("pool", "s_pr", c.cnt["s_pr"])
        c.on("pool", lambda e: e.collective_compute(
            "AllGather", ALU.bypass, replica_groups=[list(range(NCORE))],
            ins=[cc_prob_in.ap()], outs=[cc_prob_out.ap()]), inc="cc", n=1)
        c.wait("pool", "cc", c.cnt["cc"])
        c.wait("sync", "cc", c.cnt["cc"])
        for r_ in range(NCORE):
            n_, q_ = r_ // 4, r_ % 4
            for e_i in range(E):
                p0 = 64 * n_ + 8 * e_i + 2 * q_

                def mk(e, r_=r_, e_i=e_i, p0=p0):
                    src = cc_prob_out.ap()[r_ * E + e_i:r_ * E + e_i + 1, :].rearrange(
                        "one (s w) -> (one s) w", s=2)
                    return e.dma_start(out=probbis[p0:p0 + 2, :], in_=src)
                c.dma("sync", "s_bis", mk)
        gccv = c.cnt["s_bis"]
        c.dve(lambda e: e.memset(S["lo"][:], 0.0), waits=[("s_bis", gccv)])
        c.dve(lambda e: e.memset(S["hi"][:], 1.0))
        c.dve(lambda e: e.memset(S["th"][:], 0.5))
        for _ in range(NITER):
            cv = c.dve(lambda e: e.tensor_scalar(
                S["maskjunk"][:], probbis[:], S["th"][:], None, ALU.is_ge,
                op1=ALU.add, accum_out=S["cnt_s"][:]))
            c.wait("pe", "dve", cv)
            pv = c.on("pe", lambda e: e.matmul(
                ps_tr[:, 256:257], S["blockdiag8"][:], S["cnt_s"][:],
                start=True, stop=True), inc="pe")
            c.dve(lambda e: e.tensor_scalar(
                S["ge"][:], ps_tr[:, 256:257], float(KSEL), None, ALU.is_ge),
                waits=[("pe", pv)])
            c.dve(lambda e: e.tensor_sub(S["d1"][:], S["th"][:], S["lo"][:]))
            c.dve(lambda e: e.scalar_tensor_tensor(
                S["lo"][:], S["ge"][:], S["d1"][:], S["lo"][:], ALU.mult, ALU.add))
            c.dve(lambda e: e.tensor_sub(S["d1"][:], S["hi"][:], S["th"][:]))
            c.dve(lambda e: e.scalar_tensor_tensor(
                S["d2c"][:], S["ge"][:], S["d1"][:], S["th"][:], ALU.mult, ALU.add))
            c.dve(lambda e: e.tensor_copy(S["hi"][:], S["d2c"][:]))
            c.dve(lambda e: e.tensor_add(S["d1"][:], S["lo"][:], S["hi"][:]))
            c.dve(lambda e: e.tensor_scalar_mul(S["th"][:], S["d1"][:], 0.5))
        bis = c.cnt["dve"]
        c.wait("pe", "dve", bis)
        pv = c.on("pe", lambda e: e.matmul(ps_tr[0:E, 258:259], S["selmat"][:], S["lo"][:],
                                           start=True, stop=True), inc="pe")
        v = c.dve(lambda e: e.tensor_copy(S["th_em"][:], ps_tr[0:E, 258:259]),
                  waits=[("pe", pv)])
        c.wait("pe", "dve", v)
        pv = c.on("pe", lambda e: e.transpose(ps_tr[0:1, 260:260 + E], S["th_em"][:],
                                              S["ident_f32"][0:E, 0:E]), inc="pe")
        v = c.dve(lambda e: e.tensor_copy(S["th_row"][:], ps_tr[0:1, 260:260 + E]),
                  waits=[("pe", pv)])
        c.wait("pe", "dve", v)
        pv = c.on("pe", lambda e: e.matmul(ps_tr[:, 272:272 + E], S["ones_1x128_f32"][:],
                                           S["th_row"][:], start=True, stop=True), inc="pe")
        thtm = c.dve(lambda e: e.tensor_copy(S["th_tm"][:], ps_tr[:, 272:272 + E]),
                     waits=[("pe", pv)])
        c.dve(lambda e: e.tensor_scalar(
            em["mask"][:], probs_em[:], S["th_em"][:], None, ALU.is_ge))
        c.dve(lambda e: e.tensor_tensor_scan(
            em["incl"][:], em["mask"][:], em["mask"][:], 0.0, ALU.add, ALU.bypass))
        c.dve(lambda e: e.tensor_sub(em["ssel"][:], em["incl"][:], em["mask"][:]))
        c.dve(lambda e: e.scalar_tensor_tensor(
            em["d2"][:], em["ssel"][:], float(CAP - 1), S["iota_em"][:],
            ALU.add, ALU.subtract))
        c.dve(lambda e: e.tensor_scalar(em["neg"][:], em["d2"][:], 0.0, None, ALU.is_lt))
        c.dve(lambda e: e.scalar_tensor_tensor(
            em["d2"][:], em["neg"][:], 65536.0, em["d2"][:], ALU.mult, ALU.add))
        c.dve(lambda e: e.tensor_scalar(
            em["neg"][:], em["d2"][:], em["incl"][:, TCORE - 1:TCORE], None, ALU.is_lt))
        c.dve(lambda e: e.scalar_tensor_tensor(
            em["d2"][:], em["neg"][:], 65536.0, em["d2"][:], ALU.mult, ALU.add))
        c.dve(lambda e: e.tensor_sub(em["slotf"][:], em["ssel"][:], em["d2"][:]))
        c.dve(lambda e: e.tensor_mul(em["slotf"][:], em["slotf"][:], em["mask"][:]))
        c.dve(lambda e: e.tensor_add(em["slotf"][:], em["slotf"][:], em["d2"][:]))
        c.dve(lambda e: e.tensor_scalar_min(em["slotf"][:], em["slotf"][:],
                                            float(CAP)))
        c.dve(lambda e: e.tensor_scalar(em["neg"][:], em["slotf"][:], float(CAP),
                                        None, ALU.is_equal))
        c.dve(lambda e: e.tensor_mul(em["d2"][:], em["neg"][:], S["tile_em"][:]))
        slv = c.dve(lambda e: e.tensor_add(em["slotf"][:], em["slotf"][:], em["d2"][:]))
        for i in range(NT):
            c.wait("pe", "dve", slv if i == 0 else st["sl"][i - 1])
            pv = c.on("pe", lambda e, i=i: e.transpose(
                ps_tr[:, 288:288 + E], em["slotf"][:, i * 128:(i + 1) * 128],
                S["ident_f32"][0:E, 0:E]), inc="pe")
            st["sl"][i] = c.dve(lambda e, i=i: e.tensor_copy(
                S["slots_i32"][:, i, :], ps_tr[:, 288:288 + E]), waits=[("pe", pv)])
            c.dve(lambda e, i=i: e.tensor_tensor(
                S["gm_s"][:], S["probs_tm"][:, i, :], S["th_tm"][:], ALU.is_ge),
                waits=[("dve", thtm)])
            st["gm"][i] = c.dve(lambda e, i=i: e.tensor_tensor(
                payload[:, i, D:D + E], S["gm_s"][:], S["probs_tm"][:, i, :], ALU.mult))
        st["tcol"] = c.dve(lambda e: e.tensor_copy(
            payload[:, :, D + E:D + E + 2], S["thitlo"][:]))

    def ffn_phase3(st, target, target_ready, entry_act, first_call):
        first_call = first_call or not use_mirror
        st["mm1"] = {}; st["gel"] = {}; st["oev"] = {}; st["ysc"] = {}; st["sad"] = {}
        st["idxd"] = {}; st["tpv"] = {}; st["ytv"] = {}; st["gv"] = {}
        st["mir"] = {}; st["mir2"] = {}
        nwc = NFC1 + NKD
        def emit_scatter_batch_step(e_i, i):
            sem = f"g_sc{e_i}"
            if True:
                c.wait("pool", "dve", max(st["gm"][i], st["sl"][i], st["tcol"],
                                          st["xn"][i]))
                for s_, v_ in target_ready:
                    c.wait("pool", s_, v_)
                c.wait("pool", sem, c.cnt[sem])

                def mk(e, e_i=e_i, i=i):
                    return e.indirect_dma_start(
                        out=compA[e_i].ap(),
                        out_offset=IndirectOffsetOnAxis(S["slots_i32"][:, i, e_i:e_i + 1], 0),
                        in_=payload[:, i, :], in_offset=None)
                c.dma("pool", sem, mk)

        for i in range(NT):
            for e_i in range(E):
                emit_scatter_batch_step(e_i, i)
        first_pe_wait = entry_act

        def make_y_bundles(e_i):
            par = e_i % 2
            g_colv = S["g_col0"] if par == 0 else S["g_col1"]
            idxv = S["idx0"] if par == 0 else S["idx1"]
            oevall = st["oev"][(e_i, NKD - 1)]
            gv = st["gv"][e_i]
            yfme = yfm[par]
            g0 = c.cnt["g_sa"]
            for b in range(CB):
                st["sad"][(e_i, b)] = g0 + 16 * (b + 1)

            def chain(b):
                c.wait("act", "act", oevall)
                if b >= 2:
                    c.wait("act", "dve", st["ysc"][(e_i, b - 2)])
                elif e_i > 0:
                    c.wait("act", "dve", st["ysc"][(e_i - 1, CB - 1 - b)])
                tps = f"a_tp{par}{b % 2}"
                c.dma("act", tps, lambda e, b=b, yfme=yfme: e.dma_start(
                    out=ytm[b % 2][:], in_=yfme[:, b, :, :], transpose=True))
                ytv = c.cnt[tps]
                yb = yf32[b % 2]
                ysc_waits = [(tps, ytv), ("dve", gv)]
                if b >= 2:
                    ysc_waits.append(("g_sa", st["sad"][(e_i, b - 2)]))
                elif e_i > 0:
                    ysc_waits.append(("g_sa", st["sad"][(e_i - 1, CB - 1 - b)]))
                else:
                    ysc_waits.append(("g_sa", st["sa0"]))
                st["ysc"][(e_i, b)] = c.dve(
                    lambda e, b=b, g_colv=g_colv, yb=yb: e.tensor_scalar_mul(
                        yb[:], ytm[b % 2][:].rearrange("p c t -> p (c t)"),
                        g_colv[:, b:b + 1]), waits=ysc_waits)
                if b == CB - 1:
                    st["ytv"][e_i] = (c.cnt[f"a_tp{par}0"], c.cnt[f"a_tp{par}1"])

            def sadd(b):
                yb = yf32[b % 2]
                c.wait("pool", "dve", max(st["ysc"][(e_i, b)], st["idxd"][e_i]))
                c.wait("pool", "g_sa", c.cnt["g_sa"])

                def mksa(e, b=b, idxv=idxv, yb=yb):
                    return e.indirect_dma_start(
                        out=target.ap(),
                        out_offset=IndirectOffsetOnAxis(idxv[:, b:b + 1], 0),
                        in_=yb[:], in_offset=None, compute_op=ALU.add)
                got = c.dma("pool", "g_sa", mksa)
                assert got == st["sad"][(e_i, b)], (got, st["sad"][(e_i, b)])
            return ([lambda b=b: chain(b) for b in range(CB)],
                    [lambda b=b: sadd(b) for b in range(CB)])

        for e_i in range(E):
            par = e_i % 2
            scr = f"s_cr{par}"
            g_colv = S["g_col0"] if par == 0 else S["g_col1"]
            idxv = S["idx0"] if par == 0 else S["idx1"]
            trowv = S["trow0"] if par == 0 else S["trow1"]
            c.wait("sync", f"g_sc{e_i}", c.cnt[f"g_sc{e_i}"])
            if e_i >= 1:
                c.wait("sync", "dve", st["idxd"][e_i - 1])
                c.wait("sync", f"s_tp{(e_i - 1) % 2}", st["tpv"][e_i - 1])
            c.dma("sync", scr, lambda e, par=par, e_i=e_i: e.dma_start(
                out=cbuf[par][:],
                in_=compA[e_i].ap()[0:CAP, :].rearrange("(b p) w -> p b w", p=128)))
            crv = c.cnt[scr]
            st["rb"] = (scr, crv)
            if e_i >= 1:
                c.wait("sync", "pe", st["mm1"][e_i - 1])
            for b in range(CB):
                c.wait("sync", scr, crv)
                c.dma("sync", f"s_tp{par}", lambda e, par=par, b=b: e.dma_start(
                    out=xinfm[par][:, b, :, :], in_=cbuf[par][:, b, 0:D], transpose=True))
            tpv = c.cnt[f"s_tp{par}"]
            st["tpv"][e_i] = tpv
            gw = [(scr, crv)]
            if e_i >= 2:
                gw.append(("g_sa", st["sad"][(e_i - 2, CB - 1)]))
            gv = c.dve(lambda e, par=par, g_colv=g_colv, e_i=e_i: e.tensor_scalar_mul(
                g_colv[:], cbuf[par][:, :, D + e_i], 0.5), waits=gw)
            st["gv"][e_i] = gv
            c.dve(lambda e, par=par, trowv=trowv: e.scalar_tensor_tensor(
                trowv[:], cbuf[par][:, :, D + E], 64.0, cbuf[par][:, :, D + E + 1],
                ALU.mult, ALU.add))
            st["idxd"][e_i] = c.dve(
                lambda e, idxv=idxv, trowv=trowv: e.tensor_copy(idxv[:], trowv[:]))
            if e_i >= 1:
                ych, pending_y = make_y_bundles(e_i - 1)
                for f in ych:
                    f()
            else:
                pending_y = []
            for fc in range(NFC1):
                slot = (e_i * nwc + fc) % 2
                wsem = f"g_w{slot}" if first_call else f"s_w{slot}"
                if first_call:
                    c.wait("pool", "pe", st["wfree"].get(slot, 0))
                    c.wait("pool", f"s_m1{slot}", st["mir"].get(slot, 0))
                    c.dma("pool", wsem, lambda e, e_i=e_i, fc=fc, slot=slot: e.dma_start(
                        out=w1c[slot][:], in_=dram["w1"].ap()[e_i][fc]))
                    wv = c.cnt[wsem]
                    if use_mirror:
                        c.wait("sync", wsem, wv)
                        c.dma("sync", f"s_m1{slot}", lambda e, e_i=e_i, fc=fc, slot=slot:
                              e.dma_start(out=w1m.ap()[e_i][fc], in_=w1c[slot][:]))
                        st["mir"][slot] = c.cnt[f"s_m1{slot}"]
                else:
                    c.wait("sync", "pe", st["wfree"].get(slot, 0))
                    c.dma("sync", wsem, lambda e, e_i=e_i, fc=fc, slot=slot: e.dma_start(
                        out=w1c[slot][:], in_=w1m.ap()[e_i][fc]))
                    wv = c.cnt[wsem]
                c.wait("pe", wsem, wv)
                if fc == 0:
                    c.wait("pe", f"s_tp{par}", tpv)
                    if first_pe_wait:
                        c.wait("pe", "act", first_pe_wait)
                        first_pe_wait = 0
                for ft in range(4):
                    fidx = fc * 4 + ft
                    hp = psbufs[fidx % 4]
                    if ft % 2 == 0:
                        if fidx >= 4:
                            c.wait("pe", "act", st["gel"][(e_i, fidx - 3)])
                        elif e_i >= 1:
                            c.wait("pe", "act",
                                   st["oev"][(e_i - 1, 5 if fidx == 0 else 7)])
                    for kc in range(NKD):
                        for gi, (gb, gn) in enumerate(CGRP):
                            last = (kc == NKD - 1) and (gi == len(CGRP) - 1)
                            c.on("pe", lambda e, slot=slot, kc=kc, ft=ft, hp=hp,
                                 par=par, gb=gb, gn=gn: e.matmul(
                                hp[:, gb * 128:(gb + gn) * 128],
                                w1c[slot][:, kc, ft * 128:(ft + 1) * 128],
                                xinfm[par][:, gb:gb + gn, kc, :],
                                start=(kc == 0), stop=(kc == NKD - 1)),
                                inc=("pe" if last else None))
                    if ft % 2 == 1:
                        mmv = c.cnt["pe"]
                        st["gel"][(e_i, fidx - 1)] = c.act(
                            lambda e, fidx=fidx, hq=psbufs[(fidx - 1) % 4]:
                            e.activation(out=h[:, fidx - 1, :], in_=hq[:, 0:CAP],
                                         func=AF.Gelu), waits=[("pe", mmv)])
                        st["gel"][(e_i, fidx)] = c.act(
                            lambda e, fidx=fidx, hp=hp: e.activation(
                                out=h[:, fidx, :], in_=hp[:, 0:CAP], func=AF.Gelu))
                st["wfree"][slot] = c.cnt["pe"]
                if fc >= 1 and pending_y:
                    pending_y.pop(0)()
            st["mm1"][e_i] = c.cnt["pe"]
            gelall = st["gel"][(e_i, NKF - 1)]
            for dt_ in range(NKD):
                po = psbufs[dt_ % 4]
                slot = (e_i * nwc + NFC1 + dt_) % 2
                wsem = f"g_w{slot}" if first_call else f"s_w{slot}"
                if first_call:
                    c.wait("pool", "pe", st["wfree"].get(slot, 0))
                    c.wait("pool", f"s_m2{slot}", st["mir2"].get(slot, 0))
                    c.dma("pool", wsem, lambda e, e_i=e_i, dt_=dt_, slot=slot: e.dma_start(
                        out=w2c[slot][:], in_=dram["w2"].ap()[e_i][dt_]))
                    wv = c.cnt[wsem]
                    if use_mirror:
                        c.wait("sync", wsem, wv)
                        c.dma("sync", f"s_m2{slot}", lambda e, e_i=e_i, dt_=dt_, slot=slot:
                              e.dma_start(out=w2m.ap()[e_i][dt_], in_=w2c[slot][:]))
                        st["mir2"][slot] = c.cnt[f"s_m2{slot}"]
                else:
                    c.wait("sync", "pe", st["wfree"].get(slot, 0))
                    c.dma("sync", wsem, lambda e, e_i=e_i, dt_=dt_, slot=slot: e.dma_start(
                        out=w2c[slot][:], in_=w2m.ap()[e_i][dt_]))
                    wv = c.cnt[wsem]
                c.wait("pe", wsem, wv)
                if dt_ == 0:
                    c.wait("pe", "act", gelall)
                if dt_ % 2 == 0 and dt_ >= 4:
                    c.wait("pe", "act", st["oev"][(e_i, dt_ - 3)])
                if dt_ < 2 and e_i > 1:
                    c.wait("act", f"a_tp{e_i % 2}0", st["ytv"][e_i - 2][0])
                    c.wait("act", f"a_tp{e_i % 2}1", st["ytv"][e_i - 2][1])
                for fk in range(NKF):
                    for gi, (gb, gn) in enumerate(CGRP):
                        last = (fk == NKF - 1) and (gi == len(CGRP) - 1)
                        c.on("pe", lambda e, slot=slot, fk=fk, po=po, gb=gb,
                             gn=gn: e.matmul(
                            po[:, gb * 128:(gb + gn) * 128], w2c[slot][:, fk, :],
                            h[:, fk, gb * 128:(gb + gn) * 128],
                            start=(fk == 0), stop=(fk == NKF - 1)),
                            inc=("pe" if last else None))
                mm2v = c.cnt["pe"]
                st["wfree"][slot] = mm2v
                if dt_ % 2 == 1:
                    st["oev"][(e_i, dt_ - 1)] = c.act(
                        lambda e, dt_=dt_, pq=psbufs[(dt_ - 1) % 4], par=par:
                        e.activation(out=yfm[par][:, :, dt_ - 1, :],
                                     in_=pq[:, 0:CAP], func=AF.Copy),
                        waits=[("pe", mm2v)])
                    st["oev"][(e_i, dt_)] = c.act(
                        lambda e, dt_=dt_, po=po, par=par:
                        e.activation(out=yfm[par][:, :, dt_, :], in_=po[:, 0:CAP],
                                     func=AF.Copy))
            st["oevall"] = st["oev"][(e_i, NKD - 1)]
            for f in pending_y:
                f()
        ych, ysadds = make_y_bundles(E - 1)
        for f in ych:
            f()
        for f in ysadds:
            f()

    def emit_folds():
        fold_ev = 0
        c.wait("pe", "dve", c.cnt["dve"])
        c.wait("pe", "act", c.cnt["act"])

        def LD(par, dst_fn, src_fn, consumed):
            c.wait("pool", consumed[0], consumed[1])
            c.dma("pool", f"g_fa{par}", lambda e, dst_fn=dst_fn, src_fn=src_fn:
                  e.dma_start(out=dst_fn(), in_=src_fn()))
            return c.cnt[f"g_fa{par}"]

        c.dma("pool", "g_xt", lambda e: e.dma_start(
            out=S["Xtw"][:], in_=dram["xt_w"].ap().rearrange("(k p) r -> p k r", p=128)))
        def regA(mt):
            pst = [ps_a, ps_b, ps_c, ps_d][mt // 2]
            return pst[:, (mt % 2) * 512:(mt % 2) * 512 + R]
        lastmm = {0: 0, 1: 0}
        for wn, vn, outn in [("wq", "vb", "Ab"), ("wq", "vt", "At"),
                             ("wk", "wb", "Cb"), ("wk", "wt", "Ct")]:
            for kc in range(NKD):
                par = kc % 2
                lw = LD(par, (lambda par=par: fwa[par][:]),
                        (lambda wn=wn, kc=kc: dram[wn].ap()[kc * 128:(kc + 1) * 128, :]),
                        ("pe", lastmm[par]))
                lv = LD(par, (lambda par=par: fvb[par][:]),
                        (lambda vn=vn, kc=kc: dram[vn].ap()[kc * 128:(kc + 1) * 128, :]),
                        ("pe", lastmm[par]))
                c.wait("pe", f"g_fa{par}", max(lw, lv))
                if kc == 0:
                    c.wait("pe", "act", fold_ev)
                for mt in range(NKD):
                    c.on("pe", lambda e, par=par, mt=mt, kc=kc: e.matmul(
                        regA(mt), fwa[par][:, mt * 128:(mt + 1) * 128], fvb[par][:],
                        start=(kc == 0), stop=(kc == NKD - 1)),
                        inc=("pe" if mt == NKD - 1 else None))
                lastmm[par] = c.cnt["pe"]
            mmv = c.cnt["pe"]
            for mt in range(NKD):
                fold_ev = c.act(lambda e, outn=outn, mt=mt: e.activation(
                    out=S[outn][:, mt, :], in_=regA(mt), func=AF.Copy),
                    waits=[("pe", mmv)])
        c.wait("sync", "s_pr", c.cnt["s_pr"])
        c.dma("sync", "s_pr", lambda e: e.dma_start(
            out=S["xtile0"][0:1, :], in_=dram["bias_t"].ap()))
        c.wait("sync", "s_pr", c.cnt["s_pr"])
        c.dma("sync", "s_pr", lambda e: e.dma_start(
            out=S["xtile1"][0:1, :], in_=dram["bias_b"].ap()))
        bld = c.cnt["s_pr"]
        bv = c.dve(lambda e: e.scalar_tensor_tensor(
            S["xtile0"][0:1, :], S["xtile0"][0:1, :], S["alpha"][0:1, 0:1],
            S["xtile1"][0:1, :], ALU.mult, ALU.add),
            waits=[("s_sml", SML), ("s_pr", bld)])
        ev2 = 0
        for kc in range(NKD):
            c.wait("pe", "dve", bv if kc == 0 else ev2)
            pv = c.on("pe", lambda e, kc=kc: e.transpose(
                ps_tr[:, 304:305], S["xtile0"][0:1, kc * 128:(kc + 1) * 128],
                S["ident_f32"][0:1, 0:1]), inc="pe")
            ev2 = c.dve(lambda e, kc=kc: e.tensor_copy(S["bvfm"][:, kc:kc + 1],
                                                       ps_tr[:, 304:305]),
                        waits=[("pe", pv)])
        bvb = c.dve(lambda e: e.tensor_copy(S["bvfmb"][:], S["bvfm"][:]))
        def regF(rt, nh):
            return (ps_c if rt == 0 else ps_d)[:, nh * 512:(nh + 1) * 512]
        for fi, (un, outn) in enumerate([("ub", "Fb"), ("ut", "Ft")]):
            for kc in range(NKD):
                par = kc % 2
                lu = LD(par, (lambda par=par: fvb[par][:]),
                        (lambda un=un, kc=kc: dram[un].ap()[kc * 128:(kc + 1) * 128, :]),
                        ("pe", lastmm[par]))
                lw = LD(par, (lambda par=par: fwa[par][:]),
                        (lambda kc=kc: dram["woT"].ap()[kc * 128:(kc + 1) * 128, :]),
                        ("pe", lastmm[par]))
                c.wait("pe", f"g_fa{par}", max(lu, lw))
                if kc == 0:
                    c.wait("pe", "act", fold_ev)
                    c.wait("pe", "dve", bvb)
                for rt in range(2):
                    for nh in range(2):
                        last = (fi == 1) and (rt == 1) and (nh == 1)
                        c.on("pe", lambda e, par=par, rt=rt, nh=nh, kc=kc: e.matmul(
                            regF(rt, nh), fvb[par][:, rt * 128:(rt + 1) * 128],
                            fwa[par][:, nh * 512:(nh + 1) * 512],
                            start=(kc == 0), stop=(kc == NKD - 1)),
                            inc=("pe" if last else None))
                if fi == 0:
                    for nh in range(2):
                        c.on("pe", lambda e, par=par, nh=nh, kc=kc: e.matmul(
                            ps_a[0:1, nh * 512:(nh + 1) * 512], S["bvfmb"][:, kc:kc + 1],
                            fwa[par][:, nh * 512:(nh + 1) * 512],
                            start=(kc == 0), stop=(kc == NKD - 1)),
                            inc=("pe" if nh == 1 else None))
                lastmm[par] = c.cnt["pe"]
            mmv = c.cnt["pe"]
            for rt in range(2):
                for nh in range(2):
                    fold_ev = c.act(lambda e, outn=outn, rt=rt, nh=nh: e.activation(
                        out=S[outn][:, rt, nh * 512:(nh + 1) * 512], in_=regF(rt, nh),
                        func=AF.Copy), waits=[("pe", mmv)])
            if fi == 0:
                for nh in range(2):
                    fold_ev = c.act(lambda e, nh=nh: e.activation(
                        out=S["bprojT"][:, nh * 512:(nh + 1) * 512],
                        in_=ps_a[0:1, nh * 512:(nh + 1) * 512], func=AF.Copy),
                        waits=[("pe", mmv)])
        return fold_ev

    def emit_attn(st, fold_act, fold_fw):
        st["xn"] = {}; st["xf"] = {}; st["x2"] = {}
        c.wait("sync", "s_pr", c.cnt["s_pr"])
        c.dma("sync", "s_pr", lambda e: e.dma_start(
            out=S["invc_row"][:], in_=dram["invc_row"].ap()))
        invld = c.cnt["s_pr"]
        c.wait("pe", "s_pr", invld)
        c.wait("pe", "s_sml", SML)
        pv = c.on("pe", lambda e: e.matmul(
            ps_tr[:, 300:301], S["ones_1x128_f32"][:], S["alpha"][:],
            start=True, stop=True), inc="pe")
        alv = c.dve(lambda e: e.tensor_copy(S["alphab"][:], ps_tr[:, 300:301]),
                    waits=[("pe", pv)])
        bc_ev = 0
        for ch in range(NCH_A):
            if ch == 0:
                c.wait("pe", "dve", alv)
            c.wait("pe", "act", bc_ev)
            pv = c.on("pe", lambda e, ch=ch: e.matmul(
                ps_tr[:, 0:ACH], S["ones_1x128_f32"][:],
                S["invc_row"][:, ch * ACH:(ch + 1) * ACH],
                start=True, stop=True), inc="pe")
            bc_ev = c.act(lambda e, ch=ch: e.activation(
                out=at_rcf[:, ch * ACH:(ch + 1) * ACH], in_=ps_tr[:, 0:ACH],
                func=AF.Copy), waits=[("pe", pv)])
        c.dve(lambda e: e.tensor_mul(at_rcaf[:], at_rcf[:], at_rcf[:]),
              waits=[("act", bc_ev), ("dve", alv)])
        c.dve(lambda e: e.tensor_scalar_mul(at_rcaf[:], at_rcaf[:], S["alphab"][:]))
        NCH = NCH_A
        TPC = ACH // 128
        projd = {}
        evs = {}
        plist = [("Ab", "ab"), ("At", "at"), ("Cb", "cb"), ("Ct", "ct"),
                 ("Xtw", "zt")]

        def emit_ln_chunk(ch):
            cpar = ch % 2
            if ch >= 2:
                for _, on2 in plist:
                    for rt2 in range(2):
                        c.wait("sync", "act", projd[(on2, rt2, ch - 2)])
            for ti in range(TPC):
                i = ch * TPC + ti
                par = i % 2
                xt = S["xtile0"] if par == 0 else S["xtile1"]
                sx = f"s_x{par}"
                sc = 12 * par
                if i >= 2:
                    c.wait("sync", "dve", st["xn"][i - 2])
                c.wait("sync", "g_sa", st["f1_sa"])
                c.dma("sync", sx, lambda e, xt=xt, i=i: e.dma_start(
                    out=xt[:], in_=x_res.ap()[i * 128:(i + 1) * 128, :]))
                xv = c.cnt[sx]
                ew = [(f"s_tp{par}", st["xf"][i - 2])] if i >= 2 else []
                st["xn"][i] = ln_tile(st, i, xt, None, sc, sx, xv,
                                      (lambda par=par: at_xntm[par][:]), extra_waits=ew)
                c.wait("sync", "dve", st["xn"][i])
                st["xf"][i] = c.dma("sync", f"s_tp{par}",
                                    lambda e, par=par, ti=ti, cpar=cpar: e.dma_start(
                    out=at_xnfmc[cpar][:, :, ti * 128:(ti + 1) * 128],
                    in_=at_xntm[par][:], transpose=True))

        def emit_proj_chunk(ch):
            cpar = ch % 2
            xnfc = at_xnfmc[cpar]
            for k in range(10):
                wn, on = plist[k // 2]
                rt = k % 2
                kg = ch * 10 + k
                pp = psbufs[kg % 4]
                if k % 2 == 0:
                    if kg == 0:
                        c.wait("pe", "g_xt", fold_fw)
                        c.wait("pe", "act", fold_act)
                        c.wait("pe", "act", bc_ev)
                    if k == 0:
                        for ti in range(TPC):
                            i = ch * TPC + ti
                            c.wait("pe", f"s_tp{i % 2}", st["xf"][i])
                    if kg >= 4:
                        c.wait("pe", "act", evs[kg - 3])
                for kc in range(NKD):
                    c.on("pe", lambda e, wn=wn, rt=rt, kc=kc, pp=pp,
                         xnfc=xnfc: e.matmul(
                        pp[:, 0:ACH], S[wn][:, kc, rt * 128:(rt + 1) * 128],
                        xnfc[:, kc, :],
                        start=(kc == 0), stop=(kc == NKD - 1)),
                        inc=("pe" if kc == NKD - 1 else None))
                if k % 2 == 1:
                    mv = c.cnt["pe"]
                    ev0 = c.act(lambda e, on=on, ch=ch, q=psbufs[(kg - 1) % 4]:
                                e.activation(
                        out=rm[on][:, 0, ch * ACH:(ch + 1) * ACH], in_=q[:, 0:ACH],
                        func=AF.Copy), waits=[("pe", mv)])
                    ev1 = c.act(lambda e, on=on, ch=ch, pp=pp: e.activation(
                        out=rm[on][:, 1, ch * ACH:(ch + 1) * ACH], in_=pp[:, 0:ACH],
                        func=AF.Copy))
                    evs[kg - 1] = ev0
                    evs[kg] = ev1
                    projd[(on, 0, ch)] = ev0
                    projd[(on, 1, ch)] = ev1

        zscan = c.dve(lambda e: e.memset(at_carry[:], 0.0))

        def emit_scan_chunk(ch):
            for qi, on in enumerate(["cb", "ct", "zt"]):
                for rt in range(2):
                    init_ap = (
                        (lambda e, on=on, rt=rt, ch=ch:
                         rm[on][:, rt, ch * ACH - 1:ch * ACH]) if ch > 0
                        else (lambda e: at_carry[:, 0:1]))
                    c.dve(lambda e, on=on, rt=rt, ch=ch, init_ap=init_ap:
                          e.tensor_tensor_scan(
                        rm[on][:, rt, ch * ACH:(ch + 1) * ACH],
                        rm[on][:, rt, ch * ACH:(ch + 1) * ACH],
                        rm[on][:, rt, ch * ACH:(ch + 1) * ACH],
                        init_ap(e), ALU.add, ALU.bypass),
                        waits=[("act", projd[(on, rt, ch)]), ("dve", zscan)])

        emit_ln_chunk(0)
        for ch in range(NCH):
            if ch + 1 < NCH:
                emit_ln_chunk(ch + 1)
            emit_proj_chunk(ch)
            emit_scan_chunk(ch)

        def wait_proj(eng, names):
            for on in names:
                for rt in range(2):
                    for ch in range(NCH):
                        c.wait(eng, "act", projd[(on, rt, ch)])
        c.dve(lambda e: e.memset(at_car_loc[:], 0.0))
        for qi, on in enumerate(["cb", "ct", "zt"]):
            for rt in range(2):
                c.dve(lambda e, on=on, rt=rt, qi=qi: e.tensor_copy(
                    at_car_loc[:, qi * 2 + rt:qi * 2 + rt + 1],
                    rm[on][:, rt, TCORE - 1:TCORE]))
        tot = c.cnt["dve"]
        c.wait("sync", "dve", tot)
        c.wait("sync", "s_pr", c.cnt["s_pr"])
        c.dma("sync", "s_pr", lambda e: e.dma_start(out=cc_car_in.ap(), in_=at_car_loc[:]))
        c.wait("pool", "s_pr", c.cnt["s_pr"])
        c.on("pool", lambda e: e.collective_compute(
            "AllGather", ALU.bypass, replica_groups=[list(range(NCORE))],
            ins=[cc_car_in.ap()], outs=[cc_car_out.ap()]), inc="cc", n=1)
        c.wait("pool", "cc", c.cnt["cc"])
        c.dma("pool", "g_cc", lambda e: e.dma_start(
            out=at_car_all[:], in_=cc_car_out.ap().rearrange("(r p) s -> p r s", p=128)))
        carv = c.cnt["g_cc"]
        c.dve(lambda e: e.memset(at_carry[:], 0.0), waits=[("g_cc", carv)])
        for r_ in range(NCORE):
            c.dve(lambda e, r_=r_: e.scalar_tensor_tensor(
                at_carry[:], at_car_all[:, r_, :], S["wvec"][:, r_:r_ + 1], at_carry[:],
                ALU.mult, ALU.add))
        card = c.cnt["dve"]
        for qi, on in enumerate(["cb", "ct", "zt"]):
            for rt in range(2):
                c.dve(lambda e, on=on, rt=rt, qi=qi: e.tensor_scalar_add(
                    rm[on][:, rt, :], rm[on][:, rt, :],
                    at_carry[:, qi * 2 + rt:qi * 2 + rt + 1]),
                    waits=[("dve", card)])
        wait_proj("dve", ["ab", "at"])
        for rt in range(2):
            c.dve(lambda e, rt=rt: e.tensor_mul(rm["ab"][:, rt, :], rm["ab"][:, rt, :],
                                                at_rcf[:]), waits=[("act", bc_ev)])
            c.dve(lambda e, rt=rt: e.tensor_mul(rm["ab"][:, rt, :], rm["ab"][:, rt, :],
                                                rm["cb"][:, rt, :]))
            c.dve(lambda e, rt=rt: e.tensor_mul(rm["at"][:, rt, :], rm["at"][:, rt, :],
                                                at_rcaf[:]))
            c.dve(lambda e, rt=rt: e.tensor_mul(rm["at"][:, rt, :], rm["at"][:, rt, :],
                                                rm["zt"][:, rt, :]))
            c.dve(lambda e, rt=rt: e.tensor_mul(rm["at"][:, rt, :], rm["at"][:, rt, :],
                                                rm["ct"][:, rt, :]))
        intd = c.cnt["dve"]
        for i in range(NT):
            par = i % 2
            po = ps_c if par == 0 else ps_d
            xt = S["xtile0"] if par == 0 else S["xtile1"]
            c.wait("sync", "dve", st["x2"].get(i - 2, 0))
            c.dma("sync", f"s_x{par}", lambda e, xt=xt, i=i: e.dma_start(
                out=xt[:], in_=x_res.ap()[i * 128:(i + 1) * 128, :]))
            xv = c.cnt[f"s_x{par}"]
            c.wait("pe", "dve", intd)
            c.wait("pe", "dve", st["x2"].get(i - 2, 0))
            first = True
            for srcn, fold in [("ab", "Fb"), ("at", "Ft")]:
                for rt in range(2):
                    for nh in range(2):
                        c.on("pe", lambda e, srcn=srcn, fold=fold, rt=rt, nh=nh, i=i,
                             po=po, first=first: e.matmul(
                            po[:, nh * 512:(nh + 1) * 512],
                            rm[srcn][:, rt, i * 128:(i + 1) * 128],
                            S[fold][:, rt, nh * 512:(nh + 1) * 512],
                            start=first, stop=False))
                    first = False
            for nh in range(2):
                c.on("pe", lambda e, nh=nh, po=po: e.matmul(
                    po[:, nh * 512:(nh + 1) * 512], S["ones_1x128_bf16"][:],
                    S["bprojT"][:, nh * 512:(nh + 1) * 512], start=False, stop=True),
                    inc=("pe" if nh == 1 else None))
            mv = c.cnt["pe"]
            for nh in range(2):
                st["x2"][i] = c.dve(lambda e, xt=xt, nh=nh, po=po: e.tensor_add(
                    xt[:, nh * 512:(nh + 1) * 512], xt[:, nh * 512:(nh + 1) * 512],
                    po[:, nh * 512:(nh + 1) * 512]),
                    waits=([("pe", mv), (f"s_x{par}", xv)] if nh == 0 else ()))
            c.wait("sync", "dve", st["x2"][i])
            c.dma("sync", "s_st", lambda e, xt=xt, i=i: e.dma_start(
                out=y_out.ap()[i * 128:(i + 1) * 128, :], in_=xt[:]))
        return c.cnt["s_st"]

    st1 = {"wfree": {}, "sa0": 0}
    ffn_phase1(st1, x_res, [("g_pr", GPR), ("s_sml", SML)])
    ffn_phase2(st1)
    fold_act = emit_folds()
    fold_fw = c.cnt["g_xt"]
    if phases == "f1":
        c.wait("sync", "dve", c.cnt["dve"])
        c.dma("sync", "s_st", lambda e: e.dma_start(out=y_out.ap(), in_=x_res.ap()))
        c.wait("sync", "s_st", c.cnt["s_st"])
        c.emit(); es.close(); return nc
    ffn_phase3(st1, x_res, [("g_pr", GPR)], entry_act=fold_act, first_call=True)
    st1["f1_sa"] = c.cnt["g_sa"]
    if phases == "f1x":
        c.wait("sync", "g_sa", c.cnt["g_sa"])
        c.dma("sync", "s_st", lambda e: e.dma_start(out=y_out.ap(), in_=x_res.ap()))
        c.wait("sync", "s_st", c.cnt["s_st"])
        c.emit(); es.close(); return nc

    c.xbar(extra=[("g_sa", c.cnt["g_sa"]), ("s_cr0", c.cnt["s_cr0"]),
                  ("s_cr1", c.cnt["s_cr1"])])
    sta = {"f1_sa": st1["f1_sa"]}
    wb_done = emit_attn(sta, fold_act, fold_fw)
    if phases == "f1a":
        c.wait("sync", "s_st", c.cnt["s_st"])
        c.wait("sync", "g_sa", c.cnt["g_sa"])
        c.emit(); es.close(); return nc

    clear2 = [(f"g_sc{e_i}", c.cnt[f"g_sc{e_i}"]) for e_i in range(E)]
    c.xbar(extra=clear2 + [("g_sa", c.cnt["g_sa"]), ("s_st", wb_done),
                           ("s_cr0", c.cnt["s_cr0"]), ("s_cr1", c.cnt["s_cr1"])])
    st2 = {"wfree": dict(st1["wfree"]), "sa0": c.cnt["g_sa"]}
    ffn_phase1(st2, y_out, [("s_st", wb_done)])
    ffn_phase2(st2)
    ffn_phase3(st2, y_out, [("s_st", wb_done)], entry_act=0, first_call=False)

    c.wait("sync", "g_sa", c.cnt["g_sa"])
    c.wait("sync", "s_st", c.cnt["s_st"])
    c.emit()
    es.close()
    return nc


def _make_consts(TCORE, core):
    NT = TCORE // 128
    n, q = core // 4, core % 4
    cons = {}
    cons["ident_f32"] = np.eye(128, dtype=np.float32)
    bd = np.zeros((128, 128), np.float32)
    for g in range(16):
        bd[g * 8:(g + 1) * 8, g * 8:(g + 1) * 8] = 1.0
    cons["blockdiag8"] = bd
    cons["ones_1x128_f32"] = np.ones((1, 128), np.float32)
    cons["ones_1x128_bf16"] = np.ones((1, 128), np.float32).astype(ml_dtypes.bfloat16)
    cons["iota_em"] = np.broadcast_to(
        np.arange(TCORE, dtype=np.float32)[None, :], (E, TCORE)).copy()
    cons["tile_em"] = np.broadcast_to(
        (np.arange(TCORE) // 128).astype(np.float32)[None, :], (E, TCORE)).copy()
    t = np.arange(TCORE)
    thitlo = np.stack([t // 64, t % 64], -1).reshape(NT, 128, 2).transpose(1, 0, 2)
    cons["thitlo"] = np.ascontiguousarray(thitlo.astype(np.float32)).astype(ml_dtypes.bfloat16)
    gidx = q * TCORE + t
    cons["invc_row"] = (1.0 / (gidx + 1).astype(np.float64)).astype(np.float32)[None, :]
    sel = np.zeros((128, E), np.float32)
    for e_i in range(E):
        sel[64 * n + 8 * e_i, e_i] = 1.0
    cons["selmat"] = sel
    wv = np.zeros((128, E), np.float32)
    for r_ in range(4 * n, 4 * n + q):
        wv[:, r_] = 1.0
    cons["wvec"] = wv
    return cons


def _shared_weights(Wq, Wk, Wo, U_b, V_b, W_b, bias_b, U_t, V_t, W_t, X_t, bias_t,
                    alpha, gate_w, w1, w2):
    return {
        "wq": np.ascontiguousarray(Wq, np.float32),
        "wk": np.ascontiguousarray(Wk, np.float32),
        "vb": np.ascontiguousarray(V_b, np.float32),
        "vt": np.ascontiguousarray(V_t, np.float32),
        "wb": np.ascontiguousarray(W_b, np.float32),
        "wt": np.ascontiguousarray(W_t, np.float32),
        "xt_w": np.ascontiguousarray(X_t, np.float32),
        "ub": np.ascontiguousarray(U_b, np.float32),
        "ut": np.ascontiguousarray(U_t, np.float32),
        "woT": np.ascontiguousarray(np.asarray(Wo, np.float32).T),
        "bias_b": np.asarray(bias_b, np.float32).reshape(1, D),
        "bias_t": np.asarray(bias_t, np.float32).reshape(1, D),
        "alpha": np.asarray(alpha, np.float32).reshape(1, 1),
        "gate_wT": np.ascontiguousarray(np.asarray(gate_w, np.float32).T),
        "w1": np.ascontiguousarray(
            np.asarray(w1, np.float32).reshape(8, 8, 128, 8, 512)
            .transpose(0, 3, 2, 1, 4)),
        "w2": np.ascontiguousarray(
            np.asarray(w2, np.float32).reshape(8, 32, 128, 8, 128)
            .transpose(0, 3, 2, 1, 4)),
    }


def run_cores(x, shared, TCORE, phases="all", sim=False, use_mirror=False):
    nc = build_nc(TCORE=TCORE, phases=phases, use_mirror=use_mirror)
    xf = np.ascontiguousarray(x.reshape(NCORE, TCORE, D))
    in_maps = []
    for core in range(NCORE):
        m = dict(shared)
        m["xin"] = np.ascontiguousarray(xf[core])
        m.update(_make_consts(TCORE, core))
        in_maps.append(m)
    if sim:
        from concourse import bass_interp
        s = bass_interp.MultiCoreSim(nc, NCORE)
        for core in range(NCORE):
            for k, v in in_maps[core].items():
                s.cores[core].tensor(k)[:] = v
        s.simulate()
        outs = [np.array(s.cores[core].mem_tensor("y_out")) for core in range(NCORE)]
    else:
        res = run_bass_kernel_spmd(nc, in_maps, core_ids=list(range(NCORE)))
        outs = [res.results[core]["y_out"] for core in range(NCORE)]
    return np.stack(outs, 0)


def kernel(L0_attn_x, L0_attn_Wq, L0_attn_Wk, L0_attn_Wo, L0_attn_U_b, L0_attn_V_b,
           L0_attn_W_b, L0_attn_bias_b, L0_attn_U_t, L0_attn_V_t, L0_attn_W_t,
           L0_attn_X_t, L0_attn_bias_t, L0_attn_alpha, L0_ffn_gate_w, L0_ffn_w1,
           L0_ffn_w2):
    x = np.ascontiguousarray(np.asarray(L0_attn_x, dtype=np.float32))
    B, T, _ = x.shape
    TCORE = (B * T) // NCORE
    shared = _shared_weights(L0_attn_Wq, L0_attn_Wk, L0_attn_Wo, L0_attn_U_b,
                             L0_attn_V_b, L0_attn_W_b, L0_attn_bias_b, L0_attn_U_t,
                             L0_attn_V_t, L0_attn_W_t, L0_attn_X_t, L0_attn_bias_t,
                             L0_attn_alpha, L0_ffn_gate_w, L0_ffn_w1, L0_ffn_w2)
    out = run_cores(x, shared, TCORE, phases="all", sim=False)
    return out.reshape(B, T, D)
